# version 05
# baseline (speedup 1.0000x reference)
import numpy as np
import ml_dtypes
from contextlib import ExitStack

import concourse.bass as bass
import concourse.mybir as mybir
from concourse.bass_utils import run_bass_kernel_spmd

F32 = mybir.dt.float32
BF16 = mybir.dt.bfloat16
I32 = mybir.dt.int32
AF = mybir.ActivationFunctionType
ALU = mybir.AluOpType

P = 128
D = 2048
DI = 4096
H = 64
HD = 64
NS = 128
Q = 64
TPC = 1024
NT = 8
NCH = 16
KC = D // P
HID = 8192
EPS = 1e-5

TWO_PI = float(2 * np.pi)
INV_2PI = float(1.0 / (2 * np.pi))
HALF_PI = float(np.pi / 2)

C_ONE = 0
C_HPI = 1
C_EPSLN = 2
C_EPSRMS = 3
C_SIGN = 4
C_ZERO = 5
NCONST = 6

DTSIZE = {F32: 4, BF16: 2, I32: 4}


class Arena:

    def __init__(self, base_ap, nbytes):
        self.base = base_ap
        self.nbytes = nbytes
        self.blocks = {}
        self.free_list = [(0, nbytes)]

    def alloc(self, name, shape, dtype):
        assert 1 <= shape[0] <= P, shape
        n = int(np.prod(shape[1:])) * DTSIZE[dtype]
        n = (n + 63) & ~63
        for i, (off, size) in enumerate(self.free_list):
            if size >= n:
                self.free_list[i] = (off + n, size - n)
                if self.free_list[i][1] == 0:
                    del self.free_list[i]
                self.blocks[name] = (off, n)
                return self._view(off, shape, dtype)
        raise RuntimeError(f"arena OOM for {name}: need {n}, free={self.free_list}")

    def _view(self, off, shape, dtype):
        assert off % 4 == 0
        ap = self.base[:, off // 4:(off + int(np.prod(shape[1:])) * DTSIZE[dtype] + 3) // 4]
        if dtype != F32:
            ap = ap.bitcast(dtype)
        want = [P] + list(shape[1:])
        ap = ap.rearrange("p (x) -> p x") if False else ap
        if len(shape) == 2:
            ap = ap[:, 0:shape[1]] if shape[0] == P else ap[0:shape[0], 0:shape[1]]
        else:
            tot = int(np.prod(shape[1:]))
            ap = ap[:, 0:tot]
            kw = {}
            dims = shape[1:]
            pat_in = "p (" + " ".join(f"d{i}" for i in range(len(dims))) + ")"
            pat_out = "p " + " ".join(f"d{i}" for i in range(len(dims)))
            for i, d_ in enumerate(dims):
                kw[f"d{i}"] = d_
            ap = ap.rearrange(f"{pat_in} -> {pat_out}", **kw)
            if shape[0] != P:
                ap = ap[0:shape[0]]
        return ap

    def free(self, name):
        off, n = self.blocks.pop(name)
        self.free_list.append((off, n))
        self.free_list.sort()
        merged = []
        for off, n in self.free_list:
            if merged and merged[-1][0] + merged[-1][1] == off:
                merged[-1] = (merged[-1][0], merged[-1][1] + n)
            else:
                merged.append((off, n))
        self.free_list = [tuple(x) for x in merged]

    def high_water(self):
        return self.nbytes - sum(s for _, s in self.free_list)


class Prog:

    ENGS = ("te", "ve", "ac", "po", "sy")

    def __init__(self, nc, stack):
        self.nc = nc
        self.stack = stack
        self.q = {e: [] for e in self.ENGS}
        self.sems = {}
        self.cnt = {}
        self.waited = {e: {} for e in self.ENGS}
        for name in ("s_te", "s_ve", "s_ac", "s_po", "s_cc"):
            self.sems[name] = stack.enter_context(nc.semaphore(name))
            self.cnt[name] = 0

    def _dsem(self, key):
        name = "d_" + key
        if name not in self.sems:
            self.sems[name] = self.stack.enter_context(self.nc.semaphore(name))
            self.cnt[name] = 0
        return name

    SELF_SERIAL = {"ve": "s_ve", "ac": "s_ac", "po": "s_po"}

    def _emit(self, eng, fn, sem=None, inc=0):
        ssem = self.SELF_SERIAL.get(eng)
        if ssem is not None:
            self.wait(eng, (ssem, self.cnt[ssem]))
            if not inc:
                sem, inc = ssem, 1
        if inc:
            self.cnt[sem] += inc
            val = self.cnt[sem]
            self.q[eng].append([fn, self.sems[sem], inc])
            return (sem, val)
        self.q[eng].append([fn, None, 0])
        return None

    def te(self, fn, inc=0):
        return self._emit("te", fn, "s_te", inc)

    def ve(self, fn, inc=0):
        return self._emit("ve", fn, "s_ve", inc)

    def ac(self, fn, inc=0):
        return self._emit("ac", fn, "s_ac", inc)

    def po(self, fn, inc=0):
        return self._emit("po", fn, "s_po", inc)

    def dma_sy(self, key, fn):
        return self._emit("sy", fn, self._dsem(key), 16)

    def dma_po(self, key, fn):
        return self._emit("po", fn, self._dsem(key), 16)

    def cc(self, fn):
        return self._emit("po", fn, "s_cc", 1)

    def wait(self, eng, ev):
        if ev is None:
            return
        sem, val = ev
        if self.waited[eng].get(sem, 0) >= val:
            return
        self.waited[eng][sem] = val
        self.q[eng].append(["waitS", sem, val, "wait"])

    def waits(self, eng, evs):
        best = {}
        for ev in evs:
            if ev is None:
                continue
            sem, val = ev
            if best.get(sem, -1) < val:
                best[sem] = val
        for sem, val in best.items():
            self.wait(eng, (sem, val))

    def mark(self, eng):
        sem = {"te": "s_te", "ve": "s_ve", "ac": "s_ac", "po": "s_po"}[eng]
        if eng != "te":
            return (sem, self.cnt[sem])
        s = self.sems[sem]
        q = self.q[eng]
        for item in reversed(q):
            if len(item) == 3:
                if item[1] is None:
                    self.cnt[sem] += 1
                    item[1] = s
                    item[2] = 1
                return (sem, self.cnt[sem])
        return (sem, self.cnt[sem])

    def dma_quiesce(self):
        for sn, sv in sorted(self.cnt.items()):
            if sn.startswith("d_") and sv > 0:
                for eng in ("sy", "po"):
                    self.wait(eng, (sn, sv))

    def barrier(self, engs=None):
        engs = engs or ("te", "ve", "ac", "po")
        evs = {e: self.mark(e) for e in engs}
        for eng in engs:
            for peer, ev in evs.items():
                if peer != eng:
                    self.wait(eng, ev)
        for peer, ev in evs.items():
            self.wait("sy", ev)

    def finalize(self, block):
        def runq(e, items):
            for item in items:
                if item[0] == "waitS":
                    _, sem, val, _ = item
                    e.wait_ge(self.sems[sem], val)
                    continue
                fn, sem, inc = item[0], item[1], item[2]
                inst = fn(e)
                if sem is not None:
                    inst.then_inc(sem, inc)

        block.tensor(lambda t: runq(t, self.q["te"]))
        block.vector(lambda v: runq(v, self.q["ve"]))
        block.scalar(lambda a: runq(a, self.q["ac"]))
        block.gpsimd(lambda g: runq(g, self.q["po"]))
        block.sync(lambda s: runq(s, self.q["sy"]))


def build_program(debug=(), stop_after=None):
    nc = bass.Bass("TRN2", target_bir_lowering=False, debug=False)
    dbg = set(debug)

    def dri(name, shape, dt=F32):
        return nc.dram_tensor(name, shape, dt, kind="ExternalInput").ap()

    xT = dri("xT", [D, TPC])
    wtA = dri("wtA", [D, DI * 2])
    wtB = dri("wtB", [D, 448])
    woutT = dri("woutT", [DI, D])
    wgT = dri("wgT", [D, HID])
    wuT = dri("wuT", [D, HID])
    wdT = dri("wdT", [HID, D])
    vsm = dri("vsm", [P, 8])
    consts = dri("consts", [P, NCONST])
    onwbc = dri("onwbc", [P, DI])
    mth = dri("mth", [P, 8])
    mab = dri("mab", [P, 8])
    trimask = dri("trimask", [P, Q])
    resetm_in = dri("resetm", [P, TPC])
    ones_in = dri("ones_in", [P, 1])
    onesb_in = dri("onesb_in", [P, 1], BF16)
    onesrow_in = dri("onesrow_in", [1, P], BF16)
    identb_in = dri("identb_in", [P, P], BF16)
    identf_in = dri("identf_in", [P, P], F32)
    onesrowf_in = dri("onesrowf_in", [1, P], F32)
    permb_in = dri("permb_in", [P, P], F32)

    o_out = nc.dram_tensor("o_out", [D, TPC], F32, kind="ExternalOutput").ap()

    dout = {}

    def dbgout(name, shape, dt=F32):
        if name in dbg:
            dout[name] = nc.dram_tensor("dbg_" + name, shape, dt, kind="ExternalOutput").ap()
            return dout[name]
        return None

    d_w2 = nc.dram_tensor("d_w2", [NT, P, DI], BF16).ap()
    d_zs = nc.dram_tensor("d_zs", [NT, P, DI], BF16).ap()
    d_yznT = nc.dram_tensor("d_yznT", [KC * 2, NT, P, P], BF16).ap()
    d_s = nc.dram_tensor("d_s", [KC, P, TPC], BF16).ap()
    d_xn = nc.dram_tensor("d_xn", [P, KC * TPC], BF16).ap()
    d_acs_hi = nc.dram_tensor("d_acs_hi", [H, TPC], BF16).ap()
    d_acs_lo = nc.dram_tensor("d_acs_lo", [H, TPC], BF16).ap()
    d_gamT = nc.dram_tensor("d_gamT", [NCH, H], F32).ap()
    d_cc1i = nc.dram_tensor("d_cc1i", [1, 256], F32).ap()
    d_cc1o = nc.dram_tensor("d_cc1o", [4, 256], F32).ap()
    d_ccFi = nc.dram_tensor("d_ccFi", [P, DI], F32).ap()
    d_ccFo = nc.dram_tensor("d_ccFo", [4 * P, DI], F32).ap()
    d_ccMi = nc.dram_tensor("d_ccMi", [1, H], F32).ap()
    d_ccMo = nc.dram_tensor("d_ccMo", [4, H], F32).ap()
    d_coef = nc.dram_tensor("d_coef", [4, H], F32).ap()

    with ExitStack() as stack:
        sb = lambda n, s, d: stack.enter_context(nc.sbuf_tensor(n, s, d))
        pst = lambda n, s, d=F32: stack.enter_context(nc.psum_tensor(n, s, d))
        PRG = Prog(nc, stack)

        cst = sb("cst", [P, NCONST], F32)
        vsm_sb = sb("vsm_sb", [P, 8], F32)
        onesc = sb("onesc", [P, 1], F32)
        onesbc = sb("onesbc", [P, 1], BF16)
        onesrow = sb("onesrow", [1, P], BF16)
        identb = sb("identb", [P, P], BF16)
        identf = sb("identf", [P, P], F32)
        onesrowf = sb("onesrowf", [1, P], F32)
        permb = sb("permb", [P, P], F32)
        trim = sb("trim", [P, Q], F32)
        resetm = sb("resetm_sb", [P, TPC], F32)
        mth_sb = sb("mth_sb", [P, 8], F32)
        mab_sb = sb("mab_sb", [P, 8], F32)
        gam_sb = sb("gam_sb", [P, NCH], F32)
        acstot = sb("acstot", [P, 1], F32)

        ARENA_BYTES = 200 * 1024
        arena_t = sb("arena", [P, ARENA_BYTES // 4], F32)
        AR = Arena(arena_t, ARENA_BYTES)
        xn_sb = AR.alloc("xn_sb", [P, KC, TPC], BF16)
        sig1T = AR.alloc("sig1T", [P, NT, H], F32)
        eC2T = AR.alloc("eC2T", [P, NT, H], F32)
        eF2T = AR.alloc("eF2T", [P, NT, H], F32)
        ewT = AR.alloc("ewT", [P, NT, H], F32)
        eRelT = AR.alloc("eRelT", [P, NT, H], F32)
        acs2T = AR.alloc("acs2T", [P, NT, H], F32)

        ps = [pst(f"ps{i}", [P, 512]) for i in range(7)]
        ps_tp = pst("ps_tp", [P, 1024], BF16)

        block = stack.enter_context(nc.Block())

        ev_cst = PRG.dma_sy("const", lambda s: s.dma_start(out=cst[:, :], in_=consts))
        ev_vsm = PRG.dma_sy("const", lambda s: s.dma_start(out=vsm_sb[:, :], in_=vsm))
        ev_ones = PRG.dma_sy("const", lambda s: s.dma_start(out=onesc[:, :], in_=ones_in))
        ev_onesb = PRG.dma_sy("const", lambda s: s.dma_start(out=onesbc[:, :], in_=onesb_in))
        ev_onesr = PRG.dma_sy("const", lambda s: s.dma_start(out=onesrow[:, :], in_=onesrow_in))
        ev_id = PRG.dma_po("bconst", lambda g: g.dma_start(out=identb[:, :], in_=identb_in))
        ev_idf = PRG.dma_sy("const", lambda s: s.dma_start(out=identf[:, :], in_=identf_in))
        ev_orf = PRG.dma_sy("const", lambda s: s.dma_start(out=onesrowf[:, :], in_=onesrowf_in))
        ev_perm = PRG.dma_sy("const", lambda s: s.dma_start(out=permb[:, :], in_=permb_in))
        ev_tri = PRG.dma_sy("const", lambda s: s.dma_start(out=trim[:, :], in_=trimask))
        ev_rsm = PRG.dma_sy("const", lambda s: s.dma_start(out=resetm[:, :], in_=resetm_in))
        ev_mth = PRG.dma_sy("const", lambda s: s.dma_start(out=mth_sb[:, :], in_=mth))
        ev_mab = PRG.dma_sy("const", lambda s: s.dma_start(out=mab_sb[:, :], in_=mab))

        ev_cst = ev_vsm = ev_ones = ev_onesb = ev_onesr = ev_idf = ev_orf = \
            ev_perm = ev_tri = ev_mth = ev_mab = ("d_const", PRG.cnt["d_const"])
        for _e in ("te", "ve", "ac", "po"):
            PRG.wait(_e, ev_cst)
            PRG.wait(_e, ev_id)
        x_sb = AR.alloc("x_sb", [P, KC, TPC], F32)
        ev_x = PRG.dma_sy("xload", lambda s: s.dma_start(
            out=x_sb, in_=xT.rearrange("(kc p) t -> p kc t", p=P)))

        wB_sb = AR.alloc("wB_sb", [P, KC, 448], BF16)
        ev_wB = PRG.dma_po("wB", lambda g: g.dma_start(
            out=wB_sb, in_=wtB.rearrange("(kc p) n -> p kc n", p=P)))

        def _finish():
            PRG.wait("sy", ev_x)
            PRG.barrier()
            for kc in range(KC):
                PRG.dma_sy("fin", lambda s, kc=kc: s.dma_start(
                    out=o_out.rearrange("(kc p) t -> p kc t", p=P)[:, kc, :],
                    in_=x_sb[:, kc, :]))
            for _sn, _sv in sorted(PRG.cnt.items()):
                if _sn.startswith("d_") and _sv > 0:
                    PRG.q["sy"].append(["waitS", _sn, _sv, "wait"])
            PRG.finalize(block)

        if stop_after == "p0":
            _finish()
            return nc

        xsq = AR.alloc("xsq", [P, KC, TPC], BF16)
        PRG.wait("ac", ev_x)
        ev_sq = None
        for kc in range(KC):
            ev_sq = PRG.ac(lambda a, kc=kc: a.activation(
                xsq[:, kc, :], x_sb[:, kc, :], AF.Square, bias=cst[:, C_ZERO:C_ZERO + 1],
                scale=1.0), inc=1 if kc == KC - 1 else 0)
        if stop_after == "p1a":
            _finish()
            return nc
        PRG.waits("te", [ev_x, ev_ones, ev_onesb, ev_sq])
        for th in range(2):
            for kc in range(KC):
                PRG.te(lambda t, th=th, kc=kc: t.matmul(
                    ps[th][0:1, :], onesc[:, :], x_sb[:, kc, 512 * th:512 * (th + 1)],
                    start=(kc == 0), stop=(kc == KC - 1)))
        for th in range(2):
            for kc in range(KC):
                PRG.te(lambda t, th=th, kc=kc: t.matmul(
                    ps[2 + th][0:1, :], onesbc[:, :], xsq[:, kc, 512 * th:512 * (th + 1)],
                    start=(kc == 0), stop=(kc == KC - 1)))
        ev_stat = PRG.mark("te")
        if stop_after == "p1b":
            _finish()
            return nc

        rowm = AR.alloc("rowm", [1, TPC], F32)
        rowv = AR.alloc("rowv", [1, TPC], F32)
        rowr = AR.alloc("rowr", [1, TPC], F32)
        rtmp = AR.alloc("rtmp", [1, TPC], F32)
        rmr_hi = AR.alloc("rmr_hi", [1, TPC], BF16)
        rmr_lo = AR.alloc("rmr_lo", [1, TPC], BF16)
        rmm_hi = AR.alloc("rmm_hi", [1, TPC], BF16)
        rmm_lo = AR.alloc("rmm_lo", [1, TPC], BF16)
        PRG.wait("ve", ev_stat)
        for th in range(2):
            sl = slice(512 * th, 512 * (th + 1))
            PRG.ve(lambda v, sl=sl, th=th: v.tensor_scalar(
                out=rowm[:, sl], in0=ps[th][0:1, :], scalar1=1.0 / D, scalar2=None,
                op0=ALU.mult))
            PRG.ve(lambda v, sl=sl, th=th: v.tensor_scalar(
                out=rowv[:, sl], in0=ps[2 + th][0:1, :], scalar1=1.0 / D, scalar2=None,
                op0=ALU.mult))
        ev_statrd = PRG.mark("ve")
        PRG.ve(lambda v: v.scalar_tensor_tensor(
            out=rtmp[:, :], in0=rowm[:, :], scalar=1.0, in1=rowm[:, :],
            op0=ALU.mult, op1=ALU.mult))
        ev_v = PRG.ve(lambda v: v.tensor_tensor(
            rowv[:, :], rowv[:, :], rtmp[:, :], ALU.subtract), inc=1)
        PRG.waits("ac", [ev_v, ev_cst])
        ev_sqr = PRG.ac(lambda a: a.activation(
            rtmp[:, :], rowv[:, :], AF.Sqrt, bias=cst[0:1, C_EPSLN:C_EPSLN + 1],
            scale=1.0), inc=1)
        PRG.wait("ve", ev_sqr)
        PRG.ve(lambda v: v.reciprocal(rowr[:, :], rtmp[:, :]))
        PRG.ve(lambda v: v.tensor_copy(rmr_hi[:, :], rowr[:, :]))
        PRG.ve(lambda v: v.tensor_tensor(rtmp[:, :], rowr[:, :], rmr_hi[:, :], ALU.subtract))
        PRG.ve(lambda v: v.tensor_copy(rmr_lo[:, :], rtmp[:, :]))
        PRG.ve(lambda v: v.tensor_tensor(rtmp[:, :], rowm[:, :], rowr[:, :], ALU.mult))
        PRG.ve(lambda v: v.tensor_copy(rmm_hi[:, :], rtmp[:, :]))
        PRG.ve(lambda v: v.tensor_tensor(rowm[:, :], rtmp[:, :], rmm_hi[:, :], ALU.subtract))
        ev_rows = PRG.ve(lambda v: v.tensor_copy(rmm_lo[:, :], rowm[:, :]), inc=1)

        if stop_after == "p1c":
            _finish()
            return nc
        PRG.waits("te", [ev_rows, ev_onesr, ev_statrd])
        for th in range(2):
            sl = slice(512 * th, 512 * (th + 1))
            PRG.te(lambda t, sl=sl, th=th: t.matmul(
                ps[4 + th][:, :], onesrow[:, :], rmr_hi[:, sl], start=True, stop=False))
            PRG.te(lambda t, sl=sl, th=th: t.matmul(
                ps[4 + th][:, :], onesrow[:, :], rmr_lo[:, sl], start=False, stop=True))
            PRG.te(lambda t, sl=sl, th=th: t.matmul(
                ps[0 + th][:, :], onesrow[:, :], rmm_hi[:, sl], start=True, stop=False))
            PRG.te(lambda t, sl=sl, th=th: t.matmul(
                ps[0 + th][:, :], onesrow[:, :], rmm_lo[:, sl], start=False, stop=True))
        ev_bc = PRG.mark("te")
        if stop_after == "p1d":
            _finish()
            return nc

        xtmp = AR.alloc("xtmp", [P, 512], F32)
        PRG.wait("ve", ev_bc)
        ev_xn = None
        for kc in range(KC):
            for th in range(2):
                sl = slice(512 * th, 512 * (th + 1))
                PRG.ve(lambda v, kc=kc, sl=sl, th=th: v.tensor_tensor(
                    xtmp[:, :], x_sb[:, kc, sl], ps[4 + th][:, :], ALU.mult))
                ev_xn = PRG.ve(lambda v, kc=kc, sl=sl, th=th: v.tensor_tensor(
                    xn_sb[:, kc, sl], xtmp[:, :], ps[0 + th][:, :], ALU.subtract),
                    inc=1 if (kc == KC - 1 and th == 1) else 0)

        if dbgout("xn", [P, KC * TPC]):
            xd = AR.alloc("xd_dbg", [P, TPC], F32)
            for kc in range(KC):
                PRG.wait("ve", ev_xn)
                e = PRG.ve(lambda v, kc=kc: v.tensor_copy(xd, xn_sb[:, kc, :]), inc=1)
                PRG.wait("sy", e)
                e2 = PRG.dma_sy("dbgxn", lambda s, kc=kc: s.dma_start(
                    out=dout["xn"][:, TPC * kc:TPC * (kc + 1)], in_=xd))
                PRG.wait("ve", e2)
            AR.free("xd_dbg")

        PRG.barrier()
        PRG.dma_quiesce()
        for n in ("xsq", "rowm", "rowv", "rowr", "rtmp", "rmr_hi", "rmr_lo",
                  "rmm_hi", "rmm_lo", "xtmp", "x_sb"):
            AR.free(n)

        if stop_after == "p1":
            _finish()
            return nc

        Braw = AR.alloc("Braw", [P, TPC], F32)
        Bsq = AR.alloc("Bsq", [P, TPC], BF16)
        Craw = AR.alloc("Craw", [P, TPC], F32)
        Csq = AR.alloc("Csq", [P, TPC], BF16)
        spF = AR.alloc("spF", [P, TPC], F32)
        spG = AR.alloc("spG", [P, TPC], F32)
        thf = AR.alloc("thf", [P, TPC], F32)

        PRG.waits("te", [ev_wB, ev_xn])

        def passB_mm(cols, mwid, b0, b1, outbase=0):
            for th in range(2):
                sl = slice(512 * th, 512 * (th + 1))
                pb = ps[b0] if th == 0 else ps[b1]
                for kc in range(KC):
                    PRG.te(lambda t, cols=cols, mwid=mwid, kc=kc, sl=sl, pb=pb, ob=outbase:
                           t.matmul(pb[ob:ob + mwid, :], wB_sb[:, kc, cols:cols + mwid],
                                    xn_sb[:, kc, sl], start=(kc == 0), stop=(kc == KC - 1)))

        def evac(dst, b0, b1, func, rows=P, marks=True):
            evs = []
            for th in range(2):
                sl = slice(512 * th, 512 * (th + 1))
                pb = ps[b0] if th == 0 else ps[b1]
                evs.append(PRG.ac(lambda a, sl=sl, pb=pb, dst=dst, func=func, rows=rows:
                                  a.activation(dst[0:rows, sl], pb[0:rows, :], func,
                                               bias=cst[0:rows, C_ZERO:C_ZERO + 1], scale=1.0),
                                  inc=1 if marks else 0))
            return evs[-1]

        passB_mm(0, 128, 0, 1)
        evB = PRG.mark("te")
        PRG.wait("ac", evB)
        evBr = evac(Braw, 0, 1, AF.Identity)
        evBs = evac(Bsq, 0, 1, AF.Square)
        passB_mm(128, 128, 2, 3)
        evC = PRG.mark("te")
        PRG.wait("ac", evC)
        evCr = evac(Craw, 2, 3, AF.Identity)
        evCs = evac(Csq, 2, 3, AF.Square)
        PRG.waits("te", [evBr, evBs])
        passB_mm(256, 64, 0, 1)
        ev1 = PRG.mark("te")
        PRG.wait("ac", ev1)
        ev_dtr = evac(spF, 0, 1, AF.Identity, rows=64)
        PRG.waits("te", [evCr, evCs])
        passB_mm(320, 64, 2, 3)
        ev2 = PRG.mark("te")
        PRG.wait("ac", ev2)
        ev_lam = evac(spG, 2, 3, AF.Identity, rows=64)
        PRG.waits("te", [ev_dtr])
        passB_mm(384, 64, 4, 5)
        passB_mm(384, 64, 4, 5, outbase=64)
        ev3 = PRG.mark("te")
        PRG.wait("ac", ev3)
        ev_th = evac(thf, 4, 5, AF.Identity)

        if dbgout("passB", [P, 4 * TPC]):
            t_ = dout["passB"]
            PRG.waits("sy", [evBr, evCr, ev_dtr, ev_lam, ev_th])
            PRG.dma_sy("dbg", lambda s: s.dma_start(out=t_[:, 0:TPC], in_=Braw))
            PRG.dma_sy("dbg", lambda s: s.dma_start(out=t_[:, TPC:2 * TPC], in_=Craw))
            PRG.dma_sy("dbg", lambda s: s.dma_start(out=t_[0:64, 2 * TPC:3 * TPC], in_=spF[0:64, :]))
            PRG.dma_sy("dbg", lambda s: s.dma_start(out=t_[0:64, 3 * TPC:4 * TPC], in_=spG[0:64, :]))

        PRG.barrier()

        if stop_after == "p2":
            _finish()
            return nc

        AR.free("wB_sb")
        dt_t = AR.alloc("dt_t", [64, TPC], F32)
        lam_t = AR.alloc("lam_t", [64, TPC], F32)
        A_t = AR.alloc("A_t", [64, TPC], F32)
        An_t = AR.alloc("An_t", [64, TPC], F32)
        sc1 = AR.alloc("sc1", [64, TPC], F32)
        sc2 = AR.alloc("sc2", [64, TPC], F32)
        negeA = AR.alloc("negeA", [64, 1], F32)
        tsum = AR.alloc("tsum", [P, 1], F32)

        PRG.waits("ve", [ev_dtr, ev_vsm])
        PRG.ve(lambda v: v.tensor_scalar(out=spF[0:64, :], in0=spF[0:64, :],
                                         scalar1=vsm_sb[0:64, 2:3], scalar2=None, op0=ALU.add))
        ev = PRG.mark("ve")
        PRG.wait("ac", ev)
        PRG.ac(lambda a: a.activation(sc1[0:64, :], spF[0:64, :], AF.Abs,
                                      bias=cst[0:64, C_ZERO:C_ZERO + 1], scale=1.0))
        PRG.ac(lambda a: a.activation(sc2[0:64, :], sc1[0:64, :], AF.Exp,
                                      bias=cst[0:64, C_ZERO:C_ZERO + 1], scale=-1.0))
        ev = PRG.ac(lambda a: a.activation(sc1[0:64, :], sc2[0:64, :], AF.Ln,
                                           bias=cst[0:64, C_ONE:C_ONE + 1], scale=1.0), inc=1)
        PRG.wait("ve", ev)
        PRG.ve(lambda v: v.tensor_scalar(out=sc2[0:64, :], in0=spF[0:64, :],
                                         scalar1=0.0, scalar2=None, op0=ALU.max))
        ev_dt = PRG.ve(lambda v: v.tensor_tensor(dt_t[0:64, :], sc2[0:64, :], sc1[0:64, :],
                                                 ALU.add), inc=1)
        PRG.wait("ac", ev_lam)
        ev_lm = PRG.ac(lambda a: a.activation(lam_t[0:64, :], spG[0:64, :], AF.Sigmoid,
                                              bias=cst[0:64, C_ZERO:C_ZERO + 1], scale=1.0),
                       inc=1)
        PRG.ac(lambda a: a.activation(negeA[0:64, :], vsm_sb[0:64, 0:1], AF.Exp,
                                      bias=cst[0:64, C_ZERO:C_ZERO + 1], scale=1.0))
        ev = PRG.mark("ac")
        PRG.waits("ve", [ev, ev_dt])
        PRG.ve(lambda v: v.tensor_scalar(out=negeA[0:64, :], in0=negeA[0:64, :],
                                         scalar1=-1.0, scalar2=None, op0=ALU.mult))
        ev_A = PRG.ve(lambda v: v.tensor_scalar(out=A_t[0:64, :], in0=dt_t[0:64, :],
                                                scalar1=negeA[0:64, :], scalar2=None,
                                                op0=ALU.mult), inc=1)
        PRG.wait("ve", ev_th)
        ev_ts = PRG.ve(lambda v: v.tensor_reduce(tsum[:, :], thf[:, :],
                                                 mybir.AxisListType.X, ALU.add), inc=1)
        cc1v = d_cc1i.rearrange("a (b c) -> (a b) c", c=1)
        PRG.waits("po", [ev_A, ev_ts, ev_cst])
        e1 = PRG.dma_po("cc1s", lambda g: g.dma_start(out=cc1v[0:64], in_=A_t[0:64, 0:1],
                                              allow_slow_non_contiguous=True))
        e2 = PRG.dma_po("cc1s", lambda g: g.dma_start(out=cc1v[64:128], in_=tsum[0:64, :],
                                              allow_slow_non_contiguous=True))
        e3 = PRG.dma_po("cc1s", lambda g: g.dma_start(out=cc1v[128:192], in_=tsum[0:64, :],
                                              allow_slow_non_contiguous=True))
        e4 = PRG.dma_po("cc1s", lambda g: g.dma_start(out=cc1v[192:256],
                                              in_=cst[0:64, C_ZERO:C_ZERO + 1],
                                              allow_slow_non_contiguous=True))
        PRG.waits("po", [e1, e2, e3, e4])
        ev_cc1 = PRG.cc(lambda g: g.collective_compute(
            "AllGather", ALU.bypass, replica_groups=[[0, 1, 2, 3], [4, 5, 6, 7]],
            ins=[d_cc1i.opt()], outs=[d_cc1o.opt()]))

        wbufs = [AR.alloc(f"wbuf{i}", [P, KC, 512], BF16) for i in range(2)]
        zsts = [AR.alloc(f"zst{i}", [P, 512], BF16) for i in range(2)]
        w2sls = [AR.alloc(f"w2sl{i}", [P, 512], BF16) for i in range(2)]
        xdFg = AR.alloc("xdFg", [P, NT, 512], BF16)
        Ftr = AR.alloc("Ftr", [P, 512], F32)
        st_pA = {"wbuf": [None, None], "wuse": [None, None], "zst": [None, None],
                 "w2d": [None, None], "psA": [None, None], "Fev": None, "Fdma": None}

        def passA_group(g, is_z, deps):
            par = g % 2
            PRG.wait("po", st_pA["wuse"][par])
            st_pA["wbuf"][par] = PRG.dma_po(
                f"wb{par}", lambda gg, g=g: gg.dma_start(
                    out=wbufs[par],
                    in_=wtA[:, 512 * g:512 * (g + 1)].rearrange("(kc p) n -> p kc n", p=P)))
            for t in range(NT):
                bpar = t % 2
                PRG.waits("te", [st_pA["wbuf"][par], st_pA["psA"][bpar], ev_xn])
                for kc in range(KC):
                    PRG.te(lambda tt, kc=kc, t=t, par=par, bpar=bpar: tt.matmul(
                        ps[bpar][:, :], xn_sb[:, kc, 128 * t:128 * (t + 1)],
                        wbufs[par][:, kc, :], start=(kc == 0), stop=(kc == KC - 1)))
                ev_mm = PRG.mark("te")
                if t == NT - 1:
                    st_pA["wuse"][par] = ev_mm
                if is_z:
                    PRG.waits("ac", [ev_mm, st_pA["zst"][par]])
                    eva = PRG.ac(lambda a, bpar=bpar, par=par: a.activation(
                        zsts[par], ps[bpar][:, :], AF.Silu,
                        bias=cst[:, C_ZERO:C_ZERO + 1], scale=1.0), inc=1)
                    st_pA["psA"][bpar] = eva
                    PRG.wait("sy", eva)
                    st_pA["zst"][par] = PRG.dma_sy(f"zs{par}", lambda sy, t=t, g=g, par=par: sy.dma_start(
                        out=d_zs[t, :, 512 * (g - 8):512 * (g - 7)], in_=zsts[par]))
                else:
                    PRG.waits("ve", [ev_mm, st_pA["w2d"][par],
                                     st_pA.get("w2u", [None, None])[par],
                                     st_pA.get("xdu")] + deps)
                    evv = PRG.ve(lambda v, bpar=bpar, par=par, t=t, gp=g: v.tensor_tensor(
                        w2sls[par].rearrange("p (a b) -> p a b", a=8),
                        ps[bpar][:, :].rearrange("p (a b) -> p a b", a=8),
                        sig1T[:, t, 8 * gp:8 * gp + 8][:, :, None].broadcast_to([P, 8, 64]),
                        ALU.mult), inc=1)
                    PRG.wait("sy", evv)
                    st_pA["w2d"][par] = PRG.dma_sy(f"wd{par}", lambda sy, t=t, gp=g, par=par: sy.dma_start(
                        out=d_w2[t, :, 512 * gp:512 * (gp + 1)], in_=w2sls[par]))
                    PRG.wait("po", evv)
                    PRG.waits("po", [st_pA.get("xdu")])
                    evp = PRG.po(lambda g_, par=par, t=t, gp=g: g_.tensor_tensor(
                        xdFg[:, t, :].rearrange("p (a b) -> p a b", a=8),
                        w2sls[par].rearrange("p (a b) -> p a b", a=8),
                        eF2T[:, t, 8 * gp:8 * gp + 8][:, :, None].broadcast_to([P, 8, 64]),
                        ALU.mult), inc=1)
                    st_pA.setdefault("w2u", [None, None])[par] = evp
                    st_pA["psA"][bpar] = evv
                    if t == NT - 1:
                        st_pA["wuse"][par] = evp
            if not is_z:
                ev_xdF = PRG.mark("po")
                PRG.waits("te", [ev_xdF, ev_BnT, st_pA["Fev"], st_pA.get("Fmrg")])
                st_pA["xdu"] = None
                for c in range(NCH):
                    t, half = c // 2, c % 2
                    PRG.te(lambda tt, t=t, half=half, c=c: tt.matmul(
                        ps[2 + half][:, :], BnT_b[64 * half:64 * half + 64, t, :],
                        xdFg[64 * half:64 * half + 64, t, :],
                        start=(c < 2), stop=(c >= NCH - 2),
                        tile_position=(64 * half, 0)))
                evf = PRG.mark("te")
                st_pA["xdu"] = evf
                PRG.waits("ac", [evf, st_pA["Fdma"], st_pA.get("Fmrg")])
                eva = PRG.ac(lambda a: a.activation(
                    Ftr, ps[2][:, :], AF.Identity,
                    bias=cst[:, C_ZERO:C_ZERO + 1], scale=1.0), inc=1)
                st_pA["Fev"] = eva
                PRG.waits("ve", [eva])
                evm = PRG.ve(lambda v: v.tensor_tensor(Ftr, Ftr, ps[3][:, :], ALU.add),
                             inc=1)
                st_pA["Fmrg"] = evm
                PRG.wait("sy", evm)
                st_pA["Fdma"] = PRG.dma_sy("ftr", lambda sy, gp=g: sy.dma_start(
                    out=d_ccFi[:, 512 * gp:512 * (gp + 1)], in_=Ftr))

        passA_group(8, True, [])
        passA_group(9, True, [])

        g1sb = AR.alloc("g1sb", [64, 256], F32)
        g1aT = AR.alloc("g1aT", [P, 4], F32)
        g1bT = AR.alloc("g1bT", [P, 4], F32)
        mrA = AR.alloc("mrA", [P, 1], F32)
        mrB = AR.alloc("mrB", [P, 1], F32)
        prefF = AR.alloc("prefF", [P, 1], F32)
        PRG.wait("sy", ev_cc1)
        ev_g1 = PRG.dma_sy("g1", lambda sy: sy.dma_start(out=g1sb[0:4, :], in_=d_cc1o))
        PRG.waits("te", [ev_g1, ev_idf, ev_lam, ev_th])
        PRG.te(lambda t: t.transpose(ps[3][:, 0:4], g1sb[0:4, 0:128], identf[0:4, 0:4]))
        ev_t1 = PRG.mark("te")
        PRG.te(lambda t: t.transpose(ps[4][:, 0:4], g1sb[0:4, 128:256], identf[0:4, 0:4]))
        ev_t2 = PRG.mark("te")
        PRG.waits("ve", [ev_t1, ev_t2, ev_mth])
        PRG.ve(lambda v: v.tensor_copy(g1aT[:, :], ps[3][:, 0:4]))
        PRG.ve(lambda v: v.tensor_copy(g1bT[:, :], ps[4][:, 0:4]))
        ev_g1rd = PRG.mark("ve")
        PRG.ve(lambda v: v.tensor_tensor(g1aT[:, :], g1aT[:, :], mth_sb[:, 0:4], ALU.mult))
        PRG.ve(lambda v: v.tensor_tensor(g1bT[:, :], g1bT[:, :], mth_sb[:, 4:8], ALU.mult))
        PRG.ve(lambda v: v.tensor_reduce(mrA[:, :], g1aT[:, :], mybir.AxisListType.X, ALU.add))
        PRG.ve(lambda v: v.tensor_reduce(mrB[:, :], g1bT[:, :], mybir.AxisListType.X, ALU.add))
        PRG.ve(lambda v: v.tensor_copy(prefF[0:64, :], mrB[0:64, :]))
        ev_pref = PRG.ve(lambda v: v.tensor_copy(prefF[64:128, :], mrA[64:128, :]), inc=1)

        onesT = AR.alloc("onesT", [P, TPC], F32)
        angf = AR.alloc("angf", [P, TPC], F32)
        kif = AR.alloc("kif", [P, TPC], I32)
        csf = AR.alloc("csf", [P, TPC], F32)
        snf = AR.alloc("snf", [P, TPC], F32)
        scrA = AR.alloc("scrA", [P, TPC], F32)
        scrB = AR.alloc("scrB", [P, TPC], F32)
        PRG.ve(lambda v: v.tensor_scalar(out=onesT[:, :], in0=thf[:, :], scalar1=0.0,
                                         scalar2=1.0, op0=ALU.mult, op1=ALU.add))
        PRG.ve(lambda v: v.tensor_tensor_scan(angf[:, :], onesT[:, :], thf[:, :],
                                              prefF[:, :], ALU.mult, ALU.add))
        PRG.ve(lambda v: v.tensor_scalar(out=kif[:, :], in0=angf[:, :], scalar1=INV_2PI,
                                         scalar2=None, op0=ALU.mult))
        PRG.ve(lambda v: v.scalar_tensor_tensor(out=thf[:, :], in0=kif[:, :],
                                                scalar=-TWO_PI, in1=angf[:, :],
                                                op0=ALU.mult, op1=ALU.add))
        PRG.ve(lambda v: v.tensor_scalar(out=angf[:, :], in0=thf[:, :],
                                         scalar1=float(np.pi), scalar2=None, op0=ALU.is_gt))
        PRG.ve(lambda v: v.scalar_tensor_tensor(out=thf[:, :], in0=angf[:, :],
                                                scalar=-TWO_PI, in1=thf[:, :],
                                                op0=ALU.mult, op1=ALU.add))
        PRG.ve(lambda v: v.tensor_scalar(out=angf[:, :], in0=thf[:, :],
                                         scalar1=float(-np.pi), scalar2=None, op0=ALU.is_lt))
        PRG.ve(lambda v: v.scalar_tensor_tensor(out=thf[:, :], in0=angf[:, :],
                                                scalar=TWO_PI, in1=thf[:, :],
                                                op0=ALU.mult, op1=ALU.add))
        ev_angr = PRG.mark("ve")
        PRG.wait("ac", ev_angr)
        ev_sin = PRG.ac(lambda a: a.activation(snf[:, :], thf[:, :], AF.Sin,
                                               bias=cst[:, C_ZERO:C_ZERO + 1], scale=1.0),
                        inc=1)
        PRG.ac(lambda a: a.activation(angf[:, :], thf[:, :], AF.Abs,
                                       bias=cst[:, C_ZERO:C_ZERO + 1], scale=1.0))
        ev_cos = PRG.ac(lambda a: a.activation(csf[:, :], angf[:, :], AF.Sin,
                                               bias=cst[:, C_HPI:C_HPI + 1], scale=-1.0),
                        inc=1)

        passA_group(10, True, [])

        Bn_f = AR.alloc("Bn_f", [P, TPC], F32)
        Cn_f = AR.alloc("Cn_f", [P, TPC], F32)
        Bn_b = AR.alloc("Bn_b", [P, TPC], BF16)
        Cn_b = AR.alloc("Cn_b", [P, TPC], BF16)
        BnT_b = AR.alloc("BnT_b", [P, NT, NS], BF16)
        rrow = AR.alloc("rrow", [1, TPC], F32)
        rhj = AR.alloc("rhj", [1, TPC], BF16)
        rlj = AR.alloc("rlj", [1, TPC], BF16)

        def rms_rope(raw, sq, wci, dst_f, dst_b, evsq):
            PRG.waits("te", [evsq])
            for th in range(2):
                PRG.te(lambda t, th=th: t.matmul(
                    ps[3 + th][0:1, :], onesbc[:, :], sq[:, 512 * th:512 * (th + 1)],
                    start=True, stop=True))
            evq = PRG.mark("te")
            PRG.wait("ac", evq)
            for th in range(2):
                PRG.ac(lambda a, th=th: a.activation(
                    rrow[:, 512 * th:512 * (th + 1)], ps[3 + th][0:1, :], AF.Sqrt,
                    bias=cst[0:1, C_EPSRMS:C_EPSRMS + 1], scale=1.0 / NS))
            evq2 = PRG.mark("ac")
            PRG.wait("ve", evq2)
            PRG.ve(lambda v: v.reciprocal(rrow[:, :], rrow[:, :]))
            PRG.ve(lambda v: v.tensor_copy(rhj[:, :], rrow[:, :]))
            PRG.ve(lambda v: v.tensor_tensor(rrow[:, :], rrow[:, :], rhj[:, :], ALU.subtract))
            evq3 = PRG.ve(lambda v: v.tensor_copy(rlj[:, :], rrow[:, :]), inc=1)
            PRG.wait("te", evq3)
            for th in range(2):
                sl = slice(512 * th, 512 * (th + 1))
                PRG.te(lambda t, sl=sl, th=th: t.matmul(
                    ps[3 + th][:, :], onesrow[:, :], rhj[:, sl], start=True, stop=False))
                PRG.te(lambda t, sl=sl, th=th: t.matmul(
                    ps[3 + th][:, :], onesrow[:, :], rlj[:, sl], start=False, stop=True))
            evq4 = PRG.mark("te")
            PRG.wait("ve", evq4)
            for th in range(2):
                sl = slice(512 * th, 512 * (th + 1))
                PRG.ve(lambda v, sl=sl, th=th: v.scalar_tensor_tensor(
                    out=dst_f[:, sl], in0=raw[:, sl], scalar=vsm_sb[:, wci:wci + 1],
                    in1=ps[3 + th][:, :], op0=ALU.mult, op1=ALU.mult))
            ev_n = PRG.mark("ve")
            PRG.waits("te", [ev_n, ev_perm])
            for th in range(2):
                sl = slice(512 * th, 512 * (th + 1))
                PRG.te(lambda t, sl=sl, th=th: t.matmul(
                    ps[5 + th][:, :], permb[:, :], dst_f[:, sl], start=True, stop=True))
            ev_sw = PRG.mark("te")
            PRG.waits("ve", [ev_sw, ev_cos, ev_sin])
            for th in range(2):
                sl = slice(512 * th, 512 * (th + 1))
                PRG.ve(lambda v, sl=sl: v.tensor_tensor(scrA[:, sl], csf[:, sl],
                                                        dst_f[:, sl], ALU.mult))
                PRG.ve(lambda v, sl=sl, th=th: v.scalar_tensor_tensor(
                    out=scrB[:, sl], in0=snf[:, sl], scalar=cst[:, C_SIGN:C_SIGN + 1],
                    in1=ps[5 + th][:, :], op0=ALU.mult, op1=ALU.mult))
                PRG.ve(lambda v, sl=sl: v.tensor_tensor(dst_f[:, sl], scrA[:, sl],
                                                        scrB[:, sl], ALU.add))
            ev_r = PRG.ve(lambda v: v.tensor_copy(dst_b, dst_f), inc=1)
            return ev_r

        PRG.wait("te", ev_g1rd)
        ev_Bn = rms_rope(Braw, Bsq, 3, Bn_f, Bn_b, evBs)
        ev_Cn = rms_rope(Craw, Csq, 4, Cn_f, Cn_b, evCs)

        passA_group(11, True, [])

        PRG.waits("te", [ev_Bn, ev_id])
        for t8 in range(NT):
            PRG.te(lambda t, t8=t8: t.transpose(
                ps_tp[:, 0:P], Bn_b[:, 128 * t8:128 * (t8 + 1)], identb[:, :]))
            evq = PRG.mark("te")
            PRG.wait("ac", evq)
            eva = PRG.ac(lambda a, t8=t8: a.activation(
                BnT_b[:, t8, :], ps_tp[:, 0:P], AF.Identity,
                bias=cst[:, C_ZERO:C_ZERO + 1], scale=1.0), inc=1)
            PRG.wait("te", eva)
        ev_BnT = PRG.mark("ac")

        ddg = AR.alloc("ddg", [P, TPC], BF16)
        PRG.waits("ve", [ev_Bn, ev_Cn])
        PRG.ve(lambda v: v.tensor_tensor(scrA[:, :], Bn_f[:, :], Cn_f[:, :], ALU.mult))
        ev_dd = PRG.ve(lambda v: v.tensor_copy(ddg[:, :], scrA[:, :]), inc=1)
        PRG.wait("te", ev_dd)
        for th in range(2):
            PRG.te(lambda t, th=th: t.matmul(
                ps[3 + th][0:1, :], onesbc[:, :], ddg[:, 512 * th:512 * (th + 1)],
                start=True, stop=True))
        ev_dr = PRG.mark("te")
        PRG.wait("ve", ev_dr)
        for th in range(2):
            sl = slice(512 * th, 512 * (th + 1))
            PRG.ve(lambda v, sl=sl, th=th: v.tensor_copy(rrow[:, sl], ps[3 + th][0:1, :]))
        PRG.ve(lambda v: v.tensor_copy(rhj[:, :], rrow[:, :]))
        PRG.ve(lambda v: v.tensor_tensor(rrow[:, :], rrow[:, :], rhj[:, :], ALU.subtract))
        ev_dl = PRG.ve(lambda v: v.tensor_copy(rlj[:, :], rrow[:, :]), inc=1)
        PRG.wait("te", ev_dl)
        for th in range(2):
            sl = slice(512 * th, 512 * (th + 1))
            PRG.te(lambda t, sl=sl, th=th: t.matmul(
                ps[3 + th][0:64, :], onesrow[:, 0:64], rhj[:, sl], start=True, stop=False))
            PRG.te(lambda t, sl=sl, th=th: t.matmul(
                ps[3 + th][0:64, :], onesrow[:, 0:64], rlj[:, sl], start=False, stop=True))
        ev_db = PRG.mark("te")

        passA_group(12, True, [])

        early = ["kif", "angf", "Braw", "Bsq", "Craw", "Csq", "thf"]
        if "rope" not in dbg:
            early += ["csf", "snf"]
        early += ["scrA", "scrB"]
        for n in early:
            AR.free(n)
        Acs = AR.alloc("Acs", [64, TPC], F32)
        Acs2 = AR.alloc("Acs2", [64, TPC], F32)
        AcsSt = AR.alloc("AcsSt", [64, TPC], F32)
        eC2 = AR.alloc("eC2", [64, TPC], F32)
        eRel = AR.alloc("eRel", [64, TPC], F32)
        eF2 = AR.alloc("eF2", [64, TPC], F32)
        sig1 = AR.alloc("sig1", [64, TPC], F32)
        ew_t = AR.alloc("ew_t", [64, TPC], F32)
        e2_t = AR.alloc("e2_t", [64, TPC], F32)

        PRG.waits("ve", [ev_A, ev_pref, ev_rsm])
        PRG.ve(lambda v: v.tensor_tensor_scan(Acs[0:64, :], resetm[0:64, :], A_t[0:64, :],
                                              0.0, ALU.mult, ALU.add))
        PRG.ve(lambda v: v.tensor_copy(An_t[0:64, 0:TPC - 1], A_t[0:64, 1:TPC]))
        PRG.ve(lambda v: v.tensor_copy(An_t[0:64, TPC - 1:TPC], mrA[0:64, :]))
        PRG.ve(lambda v: v.tensor_tensor(Acs2[0:64, :], Acs[0:64, :], An_t[0:64, :], ALU.add))
        PRG.ve(lambda v: v.tensor_tensor_scan(AcsSt[0:64, :], onesT[0:64, :], A_t[0:64, :],
                                              0.0, ALU.mult, ALU.add))
        PRG.ve(lambda v: v.tensor_tensor(
            sc1[0:64, :].rearrange("p (a b) -> p a b", b=Q),
            Acs[0:64, Q - 1:TPC:Q][:, :, None].broadcast_to([64, NCH, Q]),
            Acs2[0:64, :].rearrange("p (a b) -> p a b", b=Q), ALU.subtract))
        ev5 = PRG.mark("ve")
        PRG.wait("ac", ev5)
        PRG.ac(lambda a: a.activation(eC2[0:64, :], sc1[0:64, :], AF.Exp,
                                      bias=cst[0:64, C_ZERO:C_ZERO + 1], scale=1.0))
        PRG.ac(lambda a: a.activation(eRel[0:64, :], Acs[0:64, :], AF.Exp,
                                      bias=cst[0:64, C_ZERO:C_ZERO + 1], scale=1.0))
        ev_acst = PRG.ac(lambda a: a.activation(acstot[0:64, :], AcsSt[0:64, TPC - 1:TPC],
                                                AF.Identity, bias=cst[0:64, C_ZERO:C_ZERO + 1],
                                                scale=1.0), inc=1)
        PRG.ve(lambda v: v.tensor_tensor(sc2[0:64, :], AcsSt[0:64, :], An_t[0:64, :],
                                         ALU.add))
        ev5b = PRG.mark("ve")
        PRG.wait("ac", ev5b)
        ev_eF = PRG.ac(lambda a: a.activation(eF2[0:64, :], sc2[0:64, :], AF.Exp,
                                              bias=acstot[0:64, 0:1], scale=-1.0), inc=1)
        PRG.ac(lambda a: a.activation(sc1[0:64, :], An_t[0:64, :], AF.Exp,
                                      bias=cst[0:64, C_ZERO:C_ZERO + 1], scale=1.0))
        ev6 = PRG.mark("ac")
        PRG.waits("ve", [ev6, ev_lm])
        PRG.ve(lambda v: v.tensor_tensor(sc2[0:64, :], lam_t[0:64, :], sc1[0:64, :], ALU.mult))
        PRG.ve(lambda v: v.scalar_tensor_tensor(out=sc2[0:64, :], in0=lam_t[0:64, :],
                                                scalar=-1.0, in1=sc2[0:64, :],
                                                op0=ALU.mult, op1=ALU.add))
        PRG.ve(lambda v: v.tensor_scalar(out=sc2[0:64, :], in0=sc2[0:64, :], scalar1=1.0,
                                         scalar2=None, op0=ALU.add))
        PRG.ve(lambda v: v.tensor_tensor(sig1[0:64, :], dt_t[0:64, :], sc2[0:64, :], ALU.mult))
        PRG.ve(lambda v: v.tensor_tensor(sc1[0:64, :], dt_t[0:64, :], lam_t[0:64, :], ALU.mult))
        PRG.wait("ve", ev_db)
        for th in range(2):
            sl = slice(512 * th, 512 * (th + 1))
            PRG.ve(lambda v, sl=sl, th=th: v.tensor_tensor(
                e2_t[0:64, sl], sc1[0:64, sl], ps[3 + th][0:64, :], ALU.mult))
        PRG.ve(lambda v: v.tensor_scalar(out=e2_t[0:64, :], in0=e2_t[0:64, :],
                                         scalar1=vsm_sb[0:64, 1:2], scalar2=None, op0=ALU.add))
        PRG.ve(lambda v: v.reciprocal(sc2[0:64, :], sig1[0:64, :]))
        PRG.ve(lambda v: v.tensor_tensor(ew_t[0:64, :], e2_t[0:64, :], sc2[0:64, :], ALU.mult))
        PRG.ve(lambda v: v.tensor_copy(gam_sb[0:64, :], Acs[0:64, Q - 1:TPC:Q]))
        ev7 = PRG.mark("ve")
        PRG.wait("ac", ev7)
        ev_gam = PRG.ac(lambda a: a.activation(gam_sb[0:64, :], gam_sb[0:64, :], AF.Exp,
                                               bias=cst[0:64, C_ZERO:C_ZERO + 1], scale=1.0),
                        inc=1)

        passA_group(13, True, [])

        def sigT(src, dst, evdep):
            PRG.waits("te", [evdep, ev_idf])
            for t8 in range(NT):
                PRG.te(lambda t, t8=t8, src=src: t.transpose(
                    ps[3 + (t8 % 2)][:, 0:64], src[0:64, 128 * t8:128 * (t8 + 1)],
                    identf[0:64, 0:64]))
                evq = PRG.mark("te")
                PRG.wait("ac", evq)
                eva = PRG.ac(lambda a, t8=t8, dst=dst: a.activation(
                    dst[:, t8, :], ps[3 + (t8 % 2)][:, 0:64], AF.Identity,
                    bias=cst[:, C_ZERO:C_ZERO + 1], scale=1.0), inc=1)
                PRG.wait("te", eva)
            return PRG.mark("ac")

        ev_sv = PRG.mark("ve")
        ev_sigT = sigT(sig1, sig1T, ev_sv)
        ev_eC2T = sigT(eC2, eC2T, ev_sv)
        ev_eF2T = sigT(eF2, eF2T, ev_eF)
        ev_ewT = sigT(ew_t, ewT, ev_sv)
        ev_eRelT = sigT(eRel, eRelT, ev_sv)
        ev_acs2T = sigT(Acs2, acs2T, ev_sv)

        achib = AR.alloc("achib", [64, TPC], BF16)
        aclob = AR.alloc("aclob", [64, TPC], BF16)
        PRG.ve(lambda v: v.tensor_copy(achib[0:64, :], Acs[0:64, :]))
        PRG.ve(lambda v: v.tensor_tensor(sc1[0:64, :], Acs[0:64, :], achib[0:64, :],
                                         ALU.subtract))
        ev8 = PRG.ve(lambda v: v.tensor_copy(aclob[0:64, :], sc1[0:64, :]), inc=1)
        PRG.wait("sy", ev8)
        ev_ahi = PRG.dma_sy("acsh", lambda sy: sy.dma_start(out=d_acs_hi, in_=achib[0:64, :]))
        ev_alo = PRG.dma_sy("acsl", lambda sy: sy.dma_start(out=d_acs_lo, in_=aclob[0:64, :]))
        gamT_sb = AR.alloc("gamT_sb", [64, H], F32)
        PRG.waits("te", [ev_gam])
        PRG.te(lambda t: t.transpose(ps[3][0:NCH, 0:64], gam_sb[0:64, 0:NCH], identf[0:64, 0:64]))
        ev9 = PRG.mark("te")
        PRG.wait("ac", ev9)
        ev10 = PRG.ac(lambda a: a.activation(gamT_sb[0:NCH, :], ps[3][0:NCH, 0:64],
                                             AF.Identity, bias=cst[0:NCH, C_ZERO:C_ZERO + 1],
                                             scale=1.0), inc=1)
        PRG.wait("sy", ev10)
        ev_gamd = PRG.dma_sy("gamd", lambda sy: sy.dma_start(out=d_gamT, in_=gamT_sb[0:NCH, :]))
        PRG.waits("te", [ev_acst])
        PRG.te(lambda t: t.transpose(ps[4][0:1, 0:64], acstot[0:64, 0:1], identf[0:64, 0:64]))
        ev11 = PRG.mark("te")
        PRG.wait("ac", ev11)
        PRG.wait("ac", ev_gamd)
        ev12 = PRG.ac(lambda a: a.activation(gamT_sb[0:1, :], ps[4][0:1, 0:64], AF.Identity,
                                             bias=cst[0:1, C_ZERO:C_ZERO + 1], scale=1.0),
                      inc=1)
        PRG.wait("sy", ev12)
        ev_Mi = PRG.dma_sy("mi", lambda sy: sy.dma_start(out=d_ccMi, in_=gamT_sb[0:1, :]))

        passA_group(14, True, [])
        passA_group(15, True, [])

        xs_deps = [ev_sigT, ev_eF2T]
        for g in range(8):
            passA_group(g, False, xs_deps)

        if "F" in dbg and dbgout("F", [P, DI]):
            PRG.wait("sy", st_pA["Fdma"])
            PRG.dma_sy("dbg", lambda sy: sy.dma_start(out=dout["F"], in_=d_ccFi))
        if "w2" in dbg and dbgout("w2", [NT * P, DI]):
            PRG.wait("sy", st_pA["w2d"][0])
            PRG.wait("sy", st_pA["w2d"][1])
            PRG.dma_sy("dbg", lambda sy: sy.dma_start(
                out=dout["w2"], in_=d_w2.rearrange("a p n -> (a p) n")))
        if "scal" in dbg and dbgout("scal", [64, 8 * TPC]):
            t_ = dout["scal"]
            PRG.waits("sy", [ev_eF, ev7, ev_gam])
            for i, src in enumerate((dt_t, lam_t, A_t, Acs, Acs2, sig1, ew_t, e2_t)):
                PRG.dma_sy("dbg", lambda sy, i=i, src=src: sy.dma_start(
                    out=t_[:, TPC * i:TPC * (i + 1)], in_=src[0:64, :]))
        if "rope" in dbg and dbgout("rope", [P, 4 * TPC]):
            t_ = dout["rope"]
            PRG.waits("sy", [ev_Bn, ev_Cn, ev_cos, ev_sin])
            PRG.dma_sy("dbg", lambda sy: sy.dma_start(out=t_[:, 0:TPC], in_=Bn_f))
            PRG.dma_sy("dbg", lambda sy: sy.dma_start(out=t_[:, TPC:2 * TPC], in_=Cn_f))
            PRG.dma_sy("dbg", lambda sy: sy.dma_start(out=t_[:, 2 * TPC:3 * TPC], in_=csf))
            PRG.dma_sy("dbg", lambda sy: sy.dma_start(out=t_[:, 3 * TPC:4 * TPC], in_=snf))

        nc._dbg_blocks = dict(AR.blocks)
        ev_xnst = PRG.dma_sy("xnst", lambda sy: sy.dma_start(
            out=d_xn, in_=xn_sb.rearrange("p a b -> p (a b)")))
        PRG.barrier()
        PRG.dma_quiesce()
        plist = ["spF", "spG", "xn_sb", "sig1T", "eF2T",
                 "dt_t", "lam_t", "A_t", "An_t", "sc1", "sc2", "negeA", "tsum",
                 "g1sb", "g1aT", "g1bT", "mrB", "prefF", "onesT",
                 "Bn_f", "Cn_f", "rrow", "rhj", "rlj", "ddg",
                 "Acs", "AcsSt", "eC2", "eRel", "eF2", "sig1", "ew_t", "e2_t",
                 "achib", "aclob", "gamT_sb", "Acs2",
                 "wbuf0", "wbuf1", "zst0", "zst1", "w2sl0", "w2sl1", "xdFg", "Ftr"]
        if "rope" in dbg:
            plist += ["csf", "snf"]
        for n in plist:
            AR.free(n)

        if stop_after == "p5":
            _finish()
            return nc

        PRG.waits("po", [st_pA["Fdma"], ev_Mi])
        ev_ccF = PRG.cc(lambda g: g.collective_compute(
            "AllGather", ALU.bypass, replica_groups=[[0, 1, 2, 3], [4, 5, 6, 7]],
            ins=[d_ccFi.opt()], outs=[d_ccFo.opt()]))
        ev_ccM = PRG.cc(lambda g: g.collective_compute(
            "AllGather", ALU.bypass, replica_groups=[[0, 1, 2, 3], [4, 5, 6, 7]],
            ins=[d_ccMi.opt()], outs=[d_ccMo.opt()]))

        R_sb = AR.alloc("R_sb", [P, DI], F32)
        Rb = [AR.alloc("Rb0", [P, DI], BF16), AR.alloc("Rb1", [P, DI], BF16)]
        Fg = AR.alloc("Fg", [P, DI], F32)
        g2 = AR.alloc("g2", [64, H], F32)
        mT = AR.alloc("mT", [64, 4], F32)
        Ssuf = AR.alloc("Ssuf", [64, 4], F32)
        coefT = AR.alloc("coefT", [64, 4], F32)
        coefR = AR.alloc("coefR", [4, H], F32)
        crow = AR.alloc("crow", [1, 4 * H], F32)

        PRG.wait("sy", ev_ccM)
        ev13 = PRG.dma_sy("g2l", lambda sy: sy.dma_start(out=g2[0:4, :], in_=d_ccMo))
        PRG.waits("ve", [ev13, ev_mab])
        ev14 = PRG.ve(lambda v: v.tensor_scalar(out=g2[0:4, :], in0=g2[0:4, :],
                                                scalar1=mab_sb[0:4, 4:5], scalar2=None,
                                                op0=ALU.mult), inc=1)
        PRG.wait("te", ev14)
        PRG.te(lambda t: t.transpose(ps[3][0:64, 0:4], g2[0:4, 0:64], identf[0:4, 0:4]))
        ev15 = PRG.mark("te")
        PRG.wait("ve", ev15)
        PRG.ve(lambda v: v.tensor_copy(mT[0:64, :], ps[3][0:64, 0:4]))
        PRG.ve(lambda v: v.tensor_scalar(out=Ssuf[0:64, 3:4], in0=mT[0:64, 3:4],
                                         scalar1=0.0, scalar2=None, op0=ALU.mult))
        PRG.ve(lambda v: v.tensor_copy(Ssuf[0:64, 2:3], mT[0:64, 3:4]))
        PRG.ve(lambda v: v.tensor_tensor(Ssuf[0:64, 1:2], mT[0:64, 2:3], mT[0:64, 3:4],
                                         ALU.add))
        PRG.ve(lambda v: v.tensor_tensor(Ssuf[0:64, 0:1], Ssuf[0:64, 1:2], mT[0:64, 1:2],
                                         ALU.add))
        ev16 = PRG.mark("ve")
        PRG.wait("ac", ev16)
        ev17 = PRG.ac(lambda a: a.activation(coefT[0:64, :], Ssuf[0:64, :], AF.Exp,
                                             bias=cst[0:64, C_ZERO:C_ZERO + 1], scale=1.0),
                      inc=1)
        PRG.wait("ve", ev17)
        ev18 = PRG.ve(lambda v: v.tensor_tensor(coefT[0:64, :], coefT[0:64, :],
                                                mab_sb[0:64, 0:4], ALU.mult), inc=1)
        PRG.wait("te", ev18)
        PRG.te(lambda t: t.transpose(ps[3][0:4, 0:64], coefT[0:64, 0:4], identf[0:64, 0:64]))
        ev19 = PRG.mark("te")
        PRG.wait("ac", ev19)
        ev20 = PRG.ac(lambda a: a.activation(coefR[0:4, :], ps[3][0:4, 0:64], AF.Identity,
                                             bias=cst[0:4, C_ZERO:C_ZERO + 1], scale=1.0),
                      inc=1)
        PRG.wait("po", ev20)
        ev22 = PRG.dma_po("coefw", lambda g: g.dma_start(out=d_coef, in_=coefR[0:4, :]))
        PRG.wait("sy", ev22)
        ev23 = PRG.dma_sy("crowl", lambda sy: sy.dma_start(out=crow[0:1, :],
                                                  in_=d_coef.rearrange("a b -> (a b)")))
        PRG.wait("sy", ev_ccF)
        ev_R = None
        for j in range(4):
            PRG.wait("sy", ev_R)
            evL = PRG.dma_sy("fgl", lambda sy, j=j: sy.dma_start(
                out=Fg, in_=d_ccFo[P * j:P * (j + 1), :]))
            PRG.waits("te", [ev23, evL, ev_R])
            PRG.te(lambda t, j=j: t.matmul(ps[3][:, 0:64], onesrowf[:, :],
                                           crow[0:1, 64 * j:64 * (j + 1)],
                                           start=True, stop=True))
            ev24 = PRG.mark("te")
            PRG.waits("ve", [ev24, evL])
            if j == 0:
                ev_R = PRG.ve(lambda v: v.tensor_tensor(
                    R_sb.rearrange("p (a b) -> p a b", a=H),
                    Fg.rearrange("p (a b) -> p a b", a=H),
                    ps[3][:, 0:64][:, :, None].broadcast_to([P, H, HD]), ALU.mult), inc=1)
            else:
                PRG.ve(lambda v: v.tensor_tensor(
                    Fg.rearrange("p (a b) -> p a b", a=H),
                    Fg.rearrange("p (a b) -> p a b", a=H),
                    ps[3][:, 0:64][:, :, None].broadcast_to([P, H, HD]), ALU.mult))
                ev_R = PRG.ve(lambda v: v.tensor_tensor(R_sb, R_sb, Fg, ALU.add), inc=1)
        ev_Rb = PRG.ve(lambda v: v.tensor_copy(Rb[0], R_sb), inc=1)

        if "R" in dbg and dbgout("R", [P, DI]):
            PRG.wait("sy", ev_Rb)
            xdump = AR.alloc("xdump", [P, DI], F32)
            e = PRG.ve(lambda v: v.tensor_copy(xdump, R_sb), inc=1)
            PRG.wait("sy", e)
            e2_ = PRG.dma_sy("dbgR", lambda sy: sy.dma_start(out=dout["R"], in_=xdump))
            PRG.wait("ve", e2_)
            AR.free("xdump")

        PRG.barrier()
        PRG.dma_quiesce()
        AR.free("Fg")
        AR.free("g2"); AR.free("mT"); AR.free("Ssuf"); AR.free("coefT")
        AR.free("coefR"); AR.free("crow"); AR.free("mrA")

        if stop_after == "p7":
            _finish()
            return nc

        onw_sb = AR.alloc("onw_sb", [P, DI], BF16)
        ev_onw = PRG.dma_po("onw", lambda g: g.dma_start(out=onw_sb, in_=onwbc))
        w2p = [AR.alloc(f"w2p{i}", [P, DI], BF16) for i in range(2)]
        zsp0 = AR.alloc("zsp0", [P, DI], BF16)
        zsp = [zsp0, zsp0]
        xdp0 = AR.alloc("xdp0", [P, DI], BF16)
        xdp = [xdp0, xdp0]
        achi_c = [AR.alloc(f"achi{i}", [1, DI], BF16) for i in range(2)]
        aclo_c = [AR.alloc(f"aclo{i}", [1, DI], BF16) for i in range(2)]
        gmr = [AR.alloc(f"gmr{i}", [1, 2 * H], F32) for i in range(2)]
        dmb = [AR.alloc(f"dmb{i}", [P, 512], F32) for i in range(2)]
        emb = [AR.alloc(f"emb{i}", [P, 512], BF16) for i in range(2)]
        mtall = [AR.alloc(f"mtall{i}", [P, 8, 512], BF16) for i in range(2)]
        exh2all = AR.alloc("exh2all", [P, 8, 512], BF16)
        scm = AR.alloc("scm", [P, Q], BF16)
        gP_sb = AR.alloc("gP_sb", [P, 2, H], F32)
        t1b = AR.alloc("t1b", [P, 512], F32)
        u0b = AR.alloc("u0b", [P, 512], BF16)
        u1b = AR.alloc("u1b", [P, 512], BF16)
        ybf = AR.alloc("ybf", [P, 512], F32)
        e2b = AR.alloc("e2b", [P, 512], F32)
        yzp = AR.alloc("yzp", [P, DI], BF16)
        sqs = AR.alloc("sqs", [P, 512], F32)
        sscol = AR.alloc("sscol", [P, 8], F32)
        rstdc = AR.alloc("rstdc", [P, 1], F32)
        yznb = AR.alloc("yznb", [P, DI], BF16)
        yevb = AR.alloc("yevb", [P, 512], BF16)

        EV = {
            "ld_w2p": [None, None], "ld_zsp": [None, None],
            "ld_ah": [None, None], "ld_al": [None, None], "ld_gmr": [None, None],
            "use_w2p": [None, None], "use_zsp": [None, None],
            "use_ah": [None, None], "use_gmr": [None, None],
            "Rb0cast": ev_Rb,
            "yzn_rd": None, "tp": None, "yzdma": None,
            "bcA_rd": [None, None],
            "s0_rd": [None, None], "s1_rd": None,
            "epi_rd": [None, None],
            "mtall": [None, None], "exh2": None, "xdp_ev": [None, None],
            "scm_ev": None, "gP_ev": None, "pair_ve": None,
        }

        PRG.waits("sy", [st_pA["w2d"][0], st_pA["w2d"][1],
                         st_pA["zst"][0], st_pA["zst"][1], ev_ahi, ev_alo, ev_gamd])

        def p8_load(p8):
            if p8 >= NT:
                return
            par = p8 % 2
            PRG.wait("sy", EV["use_w2p"][par])
            EV["ld_w2p"][par] = PRG.dma_sy(f"w2p{par}", lambda sy, p8=p8, par=par:
                                           sy.dma_start(out=w2p[par], in_=d_w2[p8]))
            PRG.wait("sy", EV["use_ah"][par])
            for ci, c in enumerate((2 * p8, 2 * p8 + 1)):
                e_a = PRG.dma_sy(f"ah{ci}", lambda sy, c=c, ci=ci: sy.dma_start(
                    out=achi_c[ci].rearrange("a (h t) -> a h t", t=Q),
                    in_=d_acs_hi[:, Q * c:Q * (c + 1)].rearrange("h t -> () h t"),
                    allow_slow_non_contiguous=True))
                e_b = PRG.dma_sy(f"al{ci}", lambda sy, c=c, ci=ci: sy.dma_start(
                    out=aclo_c[ci].rearrange("a (h t) -> a h t", t=Q),
                    in_=d_acs_lo[:, Q * c:Q * (c + 1)].rearrange("h t -> () h t"),
                    allow_slow_non_contiguous=True))
                EV["ld_ah"][ci] = e_a
                EV["ld_al"][ci] = e_b
            PRG.wait("sy", EV["use_gmr"][par])
            EV["ld_gmr"][par] = PRG.dma_sy(f"gmr{par}", lambda sy, p8=p8, par=par:
                                           sy.dma_start(
                out=gmr[par].rearrange("a (b c) -> a b c", b=2),
                in_=d_gamT[2 * p8:2 * p8 + 2, :].rearrange("b c -> () b c")))

        p8_load(0)
        for p8 in range(NT):
            par = p8 % 2
            c0, c1 = 2 * p8, 2 * p8 + 1
            ahevs = [EV["ld_ah"][0], EV["ld_ah"][1], EV["ld_al"][0], EV["ld_al"][1]]
            PRG.wait("sy", EV["use_zsp"][par])
            EV["ld_zsp"][par] = PRG.dma_sy("zsp0", lambda sy, p8=p8, par=par:
                                           sy.dma_start(out=zsp[par], in_=d_zs[p8]))

            PRG.waits("ve", [EV["ld_w2p"][par], EV["s1_rd"], ev_eC2T])
            EV["xdp_ev"][par] = PRG.ve(lambda v, p8=p8, par=par: v.tensor_tensor(
                xdp[par].rearrange("p (a b) -> p a b", a=H),
                w2p[par].rearrange("p (a b) -> p a b", a=H),
                eC2T[:, p8, :][:, :, None].broadcast_to([P, H, HD]), ALU.mult), inc=1)
            PRG.waits("te", [ev_Bn, ev_Cn, EV["scm_ev"], EV["s1_rd"]])
            for ci, c in enumerate((c0, c1)):
                PRG.te(lambda t, c=c, ci=ci: t.matmul(
                    ps[6][64 * ci:64 * ci + 64, 0:64], Bn_b[:, Q * c:Q * (c + 1)],
                    Cn_b[:, Q * c:Q * (c + 1)], start=True, stop=True))
            ev_sc = PRG.mark("te")
            PRG.waits("ve", [ev_sc, ev_tri])
            ev_scm = PRG.ve(lambda v: v.tensor_tensor(scm, ps[6][:, 0:64], trim[:, :],
                                                      ALU.mult), inc=1)
            EV["scm_ev"] = ev_scm
            PRG.waits("te", [EV["ld_gmr"][par], EV["gP_ev"], ev_scm])
            for ci in range(2):
                PRG.te(lambda t, ci=ci, par=par: t.matmul(
                    ps[6][:, 64 + 64 * ci:128 + 64 * ci][:, 0:64] if False else
                    ps[6][:, 128 + 64 * ci:192 + 64 * ci],
                    onesrowf[:, :], gmr[par][0:1, 64 * ci:64 * (ci + 1)],
                    start=True, stop=True))
            ev_gp = PRG.mark("te")
            PRG.wait("ac", ev_gp)
            ev_gpe = PRG.ac(lambda a: a.activation(
                gP_sb.rearrange("p a b -> p (a b)"), ps[6][:, 128:256], AF.Identity,
                bias=cst[:, C_ZERO:C_ZERO + 1], scale=1.0), inc=1)
            EV["use_gmr"][par] = ev_gpe
            EV["gP_ev"] = ev_gpe

            pend_mtil = []

            def flush_mtil():
                while pend_mtil:
                    hg0, hp0, ev_e0 = pend_mtil.pop(0)
                    PRG.waits("po", [ev_e0, ev_scm, EV["mtall"][par]])
                    PRG.po(lambda g, hg=hg0, par=par, hp2=hp0: g.tensor_tensor(
                        mtall[par][:, hg, :].rearrange("p (a b) -> p a b", a=8),
                        emb[hp2].rearrange("p (a b) -> p a b", a=8),
                        scm[:, None, :].broadcast_to([P, 8, Q]), ALU.mult))

            for hg in range(8):
                hp2 = hg % 2
                bcb = ps[4 + hp2]
                PRG.waits("te", ahevs + [EV["bcA_rd"][hp2], EV["s0_rd"][hp2]])
                for ci in range(2):
                    PRG.te(lambda t, ci=ci, hg=hg, bcb=bcb: t.matmul(
                        bcb[64 * ci:64 * ci + 64, :], onesrow[0:1, 0:64],
                        achi_c[ci][0:1, 512 * hg:512 * (hg + 1)], start=True, stop=False))
                    PRG.te(lambda t, ci=ci, hg=hg, bcb=bcb: t.matmul(
                        bcb[64 * ci:64 * ci + 64, :], onesrow[0:1, 0:64],
                        aclo_c[ci][0:1, 512 * hg:512 * (hg + 1)], start=False, stop=True))
                ev_bca = PRG.mark("te")
                PRG.waits("ve", [ev_bca, ev_acs2T,
                                 EV.setdefault("dmb_rd", [None, None])[hp2]])
                for ci in range(2):
                    PRG.ve(lambda v, ci=ci, p8=p8, hg=hg, hp2=hp2, bcb=bcb:
                           v.scalar_tensor_tensor(
                        out=dmb[hp2][64 * ci:64 * ci + 64, :]
                            .rearrange("p (a b) -> p a b", a=8),
                        in0=acs2T[64 * ci:64 * ci + 64, p8, 8 * hg:8 * hg + 8]
                            [:, :, None].broadcast_to([64, 8, Q]),
                        scalar=-1.0,
                        in1=bcb[64 * ci:64 * ci + 64, :].rearrange("p (a b) -> p a b", a=8),
                        op0=ALU.mult, op1=ALU.add))
                ev_dmw = PRG.mark("ve")
                EV["bcA_rd"][hp2] = ev_dmw
                PRG.waits("po", [ev_dmw])
                ev_dm0 = PRG.po(lambda g, hp2=hp2: g.tensor_scalar(
                    out=dmb[hp2], in0=dmb[hp2], scalar1=0.0, scalar2=None,
                    op0=ALU.min), inc=1)
                PRG.wait("ac", ev_dm0)
                ev_e = PRG.ac(lambda a, hp2=hp2: a.activation(
                    emb[hp2], dmb[hp2], AF.Exp,
                    bias=cst[:, C_ZERO:C_ZERO + 1], scale=1.0), inc=1)
                EV["dmb_rd"][hp2] = ev_e
                if len(pend_mtil) >= 1:
                    flush_mtil()
                pend_mtil.append((hg, hp2, ev_e))
            flush_mtil()
            ev_mtall = PRG.mark("po")
            PRG.waits("po", [EV["ld_w2p"][par], ev_ewT, EV["exh2"]])
            for hg in range(8):
                PRG.po(lambda g, hg=hg, p8=p8, par=par: g.tensor_tensor(
                    exh2all[:, hg, :].rearrange("p (a b) -> p a b", a=8),
                    w2p[par][:, 512 * hg:512 * (hg + 1)].rearrange("p (a b) -> p a b", a=8),
                    ewT[:, p8, 8 * hg:8 * hg + 8][:, :, None].broadcast_to([P, 8, HD]),
                    ALU.mult))
            ev_exh2 = PRG.mark("po")
            EV["use_ah"] = [ev_mtall, ev_mtall]

            p8_load(p8 + 1)

            rb_ready_prev = EV["Rb0cast"]

            def ydyo0s0(hg, p8=p8, par=par, c0=c0):
                hp2 = hg % 2
                ydb = ps[0 + hp2]
                yob = ps[2 + hp2]
                PRG.waits("te", [ev_mtall, EV["ld_w2p"][par], EV["epi_rd"][hp2],
                                 rb_ready_prev, EV["xdp_ev"][par], ev_BnT,
                                 EV["s0_rd"][hp2], ev_gpe])
                for hp in range(8):
                    for ci in range(2):
                        PRG.te(lambda t, hp=hp, ci=ci, hg=hg, ydb=ydb, par=par: t.matmul(
                            ydb[64 * ci:64 * ci + 64, 64 * hp:64 * hp + 64],
                            mtall[par][64 * ci:64 * ci + 64, hg, 64 * hp:64 * hp + 64],
                            w2p[par][64 * ci:64 * ci + 64,
                                     512 * hg + 64 * hp:512 * hg + 64 * hp + 64],
                            start=True, stop=True, tile_position=(64 * ci, 64 * ci)))
                for hp in range(8):
                    PRG.te(lambda t, hp=hp, hg=hg, c=c0, yob=yob: t.matmul(
                        yob[0:64, 64 * hp:64 * hp + 64], Cn_b[:, Q * c:Q * (c + 1)],
                        Rb[0][:, 512 * hg + 64 * hp:512 * hg + 64 * hp + 64],
                        start=True, stop=True, tile_position=(0, 0)))
                PRG.te(lambda t, p8=p8, hg=hg, par=par, hp2=hp2: t.matmul(
                    ps[4 + hp2][:, :], BnT_b[0:64, p8, :],
                    xdp[par][0:64, 512 * hg:512 * (hg + 1)],
                    start=True, stop=True, tile_position=(0, 0)))
                return PRG.mark("te")

            def upd0(hg, ev_te):
                hp2 = hg % 2
                hcol = slice(512 * hg, 512 * (hg + 1))
                PRG.waits("po", [ev_gpe, EV.get("u0_rd")])
                evm = PRG.po(lambda g, hg=hg, hcol=hcol: g.tensor_tensor(
                    u0b.rearrange("p (a b) -> p a b", a=8),
                    R_sb[:, hcol].rearrange("p (a b) -> p a b", a=8),
                    gP_sb[:, 0, 8 * hg:8 * hg + 8][:, :, None].broadcast_to([P, 8, HD]),
                    ALU.mult), inc=1)
                PRG.waits("ve", [ev_te, evm])
                ev = PRG.ve(lambda v, hcol=hcol, hp2=hp2: v.tensor_tensor(
                    R_sb[:, hcol], u0b, ps[4 + hp2][:, :], ALU.add), inc=1)
                EV["s0_rd"][hp2] = ev
                EV["u0_rd"] = ev
                evc = PRG.ve(lambda v, hcol=hcol: v.tensor_copy(
                    Rb[1][:, hcol], R_sb[:, hcol]), inc=1)
                return evc

            def yo1s1(hg, ev_cast, p8=p8, par=par, c1=c1):
                hp2 = hg % 2
                yob = ps[2 + hp2]
                PRG.waits("te", [ev_cast, EV["s1_rd"]])
                for hp in range(8):
                    PRG.te(lambda t, hp=hp, hg=hg, c=c1, yob=yob: t.matmul(
                        yob[64:128, 64 * hp:64 * hp + 64], Cn_b[:, Q * c:Q * (c + 1)],
                        Rb[1][:, 512 * hg + 64 * hp:512 * hg + 64 * hp + 64],
                        start=True, stop=True, tile_position=(0, 64)))
                PRG.te(lambda t, p8=p8, hg=hg, par=par: t.matmul(
                    ps[6][:, :], BnT_b[64:128, p8, :],
                    xdp[par][64:128, 512 * hg:512 * (hg + 1)],
                    start=True, stop=True, tile_position=(64, 0)))
                return PRG.mark("te")

            def upd1(hg, ev_te):
                hcol = slice(512 * hg, 512 * (hg + 1))
                PRG.waits("po", [EV["s0_rd"][hg % 2], EV.get("u1_rd")])
                evm = PRG.po(lambda g, hg=hg, hcol=hcol: g.tensor_tensor(
                    u1b.rearrange("p (a b) -> p a b", a=8),
                    R_sb[:, hcol].rearrange("p (a b) -> p a b", a=8),
                    gP_sb[:, 1, 8 * hg:8 * hg + 8][:, :, None].broadcast_to([P, 8, HD]),
                    ALU.mult), inc=1)
                PRG.waits("ve", [ev_te, evm])
                ev = PRG.ve(lambda v, hcol=hcol: v.tensor_tensor(
                    R_sb[:, hcol], u1b, ps[6][:, :], ALU.add), inc=1)
                EV["s1_rd"] = ev
                EV["u1_rd"] = ev
                evc = PRG.ve(lambda v, hcol=hcol: v.tensor_copy(
                    Rb[0][:, hcol], R_sb[:, hcol]), inc=1)
                return evc

            def epi(hg, ev_yo1, p8=p8, par=par):
                hp2 = hg % 2
                ydb = ps[0 + hp2]
                yob = ps[2 + hp2]
                hcol = slice(512 * hg, 512 * (hg + 1))
                PRG.waits("ve", [ev_yo1, ev_eRelT])
                PRG.ve(lambda v, p8=p8, hg=hg, yob=yob: v.tensor_tensor(
                    t1b.rearrange("p (a b) -> p a b", a=8),
                    yob[:, :].rearrange("p (a b) -> p a b", a=8),
                    eRelT[:, p8, 8 * hg:8 * hg + 8][:, :, None].broadcast_to([P, 8, HD]),
                    ALU.mult))
                ev1 = PRG.ve(lambda v, ydb=ydb: v.scalar_tensor_tensor(
                    out=ybf, in0=ydb[:, :], scalar=1.0, in1=t1b,
                    op0=ALU.mult, op1=ALU.add), inc=1)
                EV["epi_rd"][hp2] = ev1
                PRG.waits("po", [ev1, ev_exh2, EV["ld_zsp"][par], EV["yzp_rd"]
                                 if "yzp_rd" in EV else None])
                PRG.po(lambda g, hg=hg: g.tensor_tensor(e2b, ybf, exh2all[:, hg, :],
                                                        ALU.add))
                evz = PRG.po(lambda g, hcol=hcol, par=par: g.tensor_tensor(
                    yzp[:, hcol], e2b, zsp[par][:, hcol], ALU.mult), inc=1)
                PRG.waits("ve", [evz])
                PRG.ve(lambda v, hcol=hcol, hg=hg: v.scalar_tensor_tensor(
                    out=sqs, in0=yzp[:, hcol], scalar=1.0, in1=yzp[:, hcol],
                    op0=ALU.mult, op1=ALU.mult, accum_out=sscol[:, hg:hg + 1]))
                return evz

            ev_te = [None] * 9
            ev_c0l = [None] * 8
            ev_te[0] = ydyo0s0(0)
            ev_c0l[0] = upd0(0, ev_te[0])
            ev_te[1] = ydyo0s0(1)
            ev_c0l[1] = upd0(1, ev_te[1])
            for hg in range(8):
                ev_y1 = yo1s1(hg, ev_c0l[hg])
                ev_c1 = upd1(hg, ev_y1)
                epi(hg, ev_y1)
                if hg < 6:
                    ev_te[hg + 2] = ydyo0s0(hg + 2)
                    ev_c0l[hg + 2] = upd0(hg + 2, ev_te[hg + 2])
            EV["Rb0cast"] = ev_c1
            EV["use_w2p"][par] = ev_exh2
            _pom = PRG.mark("po")
            EV["use_zsp"] = [_pom, _pom]
            EV["mtall"][par] = PRG.mark("te")
            EV["exh2"] = _pom

            ev_ss = PRG.ve(lambda v: v.tensor_reduce(rstdc, sscol, mybir.AxisListType.X,
                                                     ALU.add), inc=1)
            PRG.wait("ac", ev_ss)
            ev_sq3 = PRG.ac(lambda a: a.activation(rstdc, rstdc, AF.Sqrt,
                                                   bias=cst[:, C_EPSRMS:C_EPSRMS + 1],
                                                   scale=1.0 / DI), inc=1)
            PRG.waits("ve", [ev_sq3, ev_onw, EV["yzn_rd"]])
            PRG.ve(lambda v: v.reciprocal(rstdc, rstdc))
            ev_yzn = PRG.ve(lambda v: v.scalar_tensor_tensor(
                out=yznb, in0=yzp, scalar=rstdc[:, 0:1], in1=onw_sb,
                op0=ALU.mult, op1=ALU.mult), inc=1)
            EV["yzp_rd"] = ev_yzn
            for bq in range(8):
                PRG.waits("te", [ev_yzn, EV["tp"], ev_id])
                for k4 in range(4):
                    dc = 4 * bq + k4
                    PRG.te(lambda t, dc=dc, k4=k4: t.transpose(
                        ps_tp[:, 128 * k4:128 * (k4 + 1)],
                        yznb[:, 128 * dc:128 * (dc + 1)], identb[:, :]))
                ev_tp = PRG.mark("te")
                PRG.waits("ac", [ev_tp, EV["yzdma"]])
                ev_te2 = PRG.ac(lambda a: a.activation(
                    yevb, ps_tp[:, 0:512],
                    AF.Identity, bias=cst[:, C_ZERO:C_ZERO + 1], scale=1.0), inc=1)
                EV["tp"] = ev_te2
                PRG.wait("sy", ev_te2)
                EV["yzdma"] = PRG.dma_sy("yzw", lambda sy, bq=bq, p8=p8: sy.dma_start(
                    out=d_yznT[4 * bq:4 * bq + 4, p8].rearrange("a di t -> di a t"),
                    in_=yevb.rearrange("p (a t) -> p a t", a=4)))
            EV["yzn_rd"] = ev_tp

        ev_ld = {"yzdma": EV["yzdma"]}

        if "y" in dbg and dbgout("y", [NT * P, DI]):
            PRG.wait("sy", ev_ld["yzdma"])
            PRG.dma_sy("dbg", lambda sy: sy.dma_start(
                out=dout["y"].rearrange("(pr t) (dc di) -> dc pr di t", pr=NT, dc=KC * 2),
                in_=d_yznT[:, :, :, :]))

        PRG.barrier()
        PRG.dma_quiesce()
        for n in ("onw_sb", "w2p0", "w2p1", "zsp0", "xdp0", "achi0", "achi1",
                  "aclo0", "aclo1", "gmr0", "gmr1", "dmb0", "dmb1", "emb0", "emb1",
                  "mtall0", "mtall1", "exh2all", "scm", "gP_sb", "t1b", "ybf", "e2b",
                  "yzp", "sqs", "sscol", "rstdc", "yznb", "yevb", "R_sb", "Rb0", "Rb1",
                  "u0b", "u1b",
                  "Bn_b", "Cn_b", "BnT_b",
                  "eC2T", "ewT", "eRelT", "acs2T"):
            AR.free(n)

        if stop_after == "p8":
            _finish()
            return nc

        yzn_all = AR.alloc("yzn_all", [P, KC * 2, TPC], BF16)
        wo_sl = [AR.alloc(f"wo_sl{i}", [P, KC * 2, P], BF16) for i in range(2)]
        s_ev = AR.alloc("s_ev", [P, TPC], BF16)
        ev_yl = None
        PRG.wait("sy", ev_ld["yzdma"])
        for dc in range(KC * 2):
            ev_yl = PRG.dma_sy("yl", lambda sy, dc=dc: sy.dma_start(
                out=yzn_all[:, dc, :].rearrange("p (a t) -> p a t", a=NT),
                in_=d_yznT[dc].rearrange("pr di t -> di pr t")))
        ev_wo = [None, None]
        ev_wou = [None, None]
        ev_sdma = None
        ev_sev = None
        for dm in range(KC):
            par = dm % 2
            PRG.wait("po", ev_wou[par])
            ev_wo[par] = PRG.dma_po(f"wo{par}", lambda g, dm=dm, par=par: g.dma_start(
                out=wo_sl[par],
                in_=woutT[:, P * dm:P * (dm + 1)].rearrange("(kc p) n -> p kc n", p=P)))
            PRG.waits("te", [ev_wo[par], ev_yl, ev_sev])
            for kc in range(KC * 2):
                for th in range(2):
                    PRG.te(lambda t, kc=kc, th=th, par=par: t.matmul(
                        ps[th][:, :], wo_sl[par][:, kc, :],
                        yzn_all[:, kc, 512 * th:512 * (th + 1)],
                        start=(kc == 0), stop=(kc == KC * 2 - 1)))
            ev_mm = PRG.mark("te")
            if dm >= KC - 2:
                ev_wou[par] = ev_mm
            else:
                ev_wou[par] = ev_mm
            PRG.waits("ac", [ev_mm, ev_sdma])
            ev_sev = PRG.ac(lambda a: a.activation(
                s_ev[:, 0:512], ps[0][:, :], AF.Identity,
                bias=cst[:, C_ZERO:C_ZERO + 1], scale=1.0))
            ev_sev = PRG.ac(lambda a: a.activation(
                s_ev[:, 512:1024], ps[1][:, :], AF.Identity,
                bias=cst[:, C_ZERO:C_ZERO + 1], scale=1.0), inc=1)
            PRG.wait("sy", ev_sev)
            ev_sdma = PRG.dma_sy("sd", lambda sy, dm=dm: sy.dma_start(out=d_s[dm], in_=s_ev))

        PRG.barrier()
        PRG.dma_quiesce()
        for n in ("yzn_all", "wo_sl0", "wo_sl1", "s_ev"):
            AR.free(n)

        if stop_after == "p9":
            _finish()
            return nc

        hidact = AR.alloc("hidact", [P, 64, TPC], BF16)
        xn2_sb = AR.alloc("xn2_sb", [P, KC, TPC], BF16)
        PRG.wait("sy", ev_xnst)
        ev_xnld = PRG.dma_sy("xnld", lambda sy: sy.dma_start(
            out=xn2_sb.rearrange("p a b -> p (a b)"), in_=d_xn))
        wg_sl = [AR.alloc(f"wg_sl{i}", [P, KC, P], BF16) for i in range(2)]
        wu_sl = [AR.alloc(f"wu_sl{i}", [P, KC, P], BF16) for i in range(2)]
        sgb = AR.alloc("sgb", [P, TPC], F32)
        ev_g = {"wg": [None, None], "wu": [None, None], "wgu": [None, None],
                "wuu": [None, None], "hv": None}
        for hc in range(64):
            par = hc % 2
            PRG.wait("po", ev_g["wgu"][par])
            ev_g["wg"][par] = PRG.dma_po(f"wg{par}", lambda g, hc=hc, par=par: g.dma_start(
                out=wg_sl[par],
                in_=wgT[:, P * hc:P * (hc + 1)].rearrange("(kc p) n -> p kc n", p=P)))
            PRG.wait("po", ev_g["wuu"][par])
            ev_g["wu"][par] = PRG.dma_po(f"wu{par}", lambda g, hc=hc, par=par: g.dma_start(
                out=wu_sl[par],
                in_=wuT[:, P * hc:P * (hc + 1)].rearrange("(kc p) n -> p kc n", p=P)))
            PRG.waits("te", [ev_g["wg"][par], ev_g["hv"], ev_g.get("sg"), ev_xnld])
            for kc in range(KC):
                for th in range(2):
                    PRG.te(lambda t, kc=kc, th=th, par=par: t.matmul(
                        ps[0 + th][:, :], wg_sl[par][:, kc, :],
                        xn2_sb[:, kc, 512 * th:512 * (th + 1)],
                        start=(kc == 0), stop=(kc == KC - 1)))
            ev_gmm = PRG.mark("te")
            ev_g["wgu"][par] = ev_gmm
            PRG.wait("te", ev_g["wu"][par])
            for kc in range(KC):
                for th in range(2):
                    PRG.te(lambda t, kc=kc, th=th, par=par: t.matmul(
                        ps[2 + th][:, :], wu_sl[par][:, kc, :],
                        xn2_sb[:, kc, 512 * th:512 * (th + 1)],
                        start=(kc == 0), stop=(kc == KC - 1)))
            ev_umm = PRG.mark("te")
            ev_g["wuu"][par] = ev_umm
            PRG.wait("ac", ev_gmm)
            ev_sg = PRG.ac(lambda a: a.activation(
                sgb[:, 0:512], ps[0][:, :], AF.Silu, bias=cst[:, C_ZERO:C_ZERO + 1],
                scale=1.0))
            ev_sg = PRG.ac(lambda a: a.activation(
                sgb[:, 512:1024], ps[1][:, :], AF.Silu, bias=cst[:, C_ZERO:C_ZERO + 1],
                scale=1.0), inc=1)
            ev_g["sg"] = ev_sg
            PRG.waits("ve", [ev_sg, ev_umm])
            for th in range(2):
                ev_hv = PRG.ve(lambda v, hc=hc, th=th: v.tensor_tensor(
                    hidact[:, hc, 512 * th:512 * (th + 1)], sgb[:, 512 * th:512 * (th + 1)],
                    ps[2 + th][:, :], ALU.mult), inc=1)
            ev_g["hv"] = ev_hv

        PRG.barrier()
        PRG.dma_quiesce()
        AR.free("wg_sl0"); AR.free("wg_sl1"); AR.free("wu_sl0"); AR.free("wu_sl1")
        AR.free("sgb")
        AR.free("xn2_sb")

        wd_sl = [AR.alloc(f"wd_sl{i}", [P, KC, 512], BF16) for i in range(2)]
        gchunk = AR.alloc("gchunk", [P, TPC], BF16)
        xchunk = AR.alloc("xchunk", [P, TPC], F32)
        schunk = AR.alloc("schunk", [P, TPC], BF16)
        obuf = AR.alloc("obuf", [P, TPC], F32)
        groups = [(0, 3), (3, 3), (6, 3), (9, 3), (12, 2), (14, 2)]
        ev_d = {"wd": [None, None], "wdu": [None, None], "ev_ev": None, "odma": None,
                "xl": None, "sl": ev_sdma}
        for gi, (dm0, ndm) in enumerate(groups):
            for kq in range(4):
                par = kq % 2
                PRG.wait("po", ev_d["wdu"][par])
                ev_d["wd"][par] = PRG.dma_po(f"wdn{par}", lambda g, dm0=dm0, ndm=ndm, kq=kq, par=par: g.dma_start(
                    out=wd_sl[par][:, :, 0:128 * ndm],
                    in_=wdT[2048 * kq:2048 * (kq + 1), P * dm0:P * dm0 + P * ndm]
                        .rearrange("(kc p) n -> p kc n", p=P)))
                PRG.waits("te", [ev_d["wd"][par], ev_d["ev_ev"] if kq == 0 else None,
                                 ev_g["hv"]])
                for kc in range(KC):
                    for dmc in range(ndm):
                        for th in range(2):
                            PRG.te(lambda t, kc=kc, kq=kq, dmc=dmc, th=th, par=par: t.matmul(
                                ps[2 * dmc + th][:, :], wd_sl[par][:, kc, 128 * dmc:128 * dmc + 128],
                                hidact[:, 16 * kq + kc, 512 * th:512 * (th + 1)],
                                start=(kq == 0 and kc == 0), stop=(kq == 3 and kc == KC - 1)))
                ev_dmm = PRG.mark("te")
                ev_d["wdu"][par] = ev_dmm
            for dmc in range(ndm):
                dm = dm0 + dmc
                PRG.waits("ac", [ev_dmm, ev_d["ev_ev"], ev_d.get("gcr")])
                ev_gc = PRG.ac(lambda a, dmc=dmc: a.activation(
                    gchunk[:, 0:512], ps[2 * dmc][:, :], AF.Silu,
                    bias=cst[:, C_ZERO:C_ZERO + 1], scale=1.0))
                ev_gc = PRG.ac(lambda a, dmc=dmc: a.activation(
                    gchunk[:, 512:1024], ps[2 * dmc + 1][:, :], AF.Silu,
                    bias=cst[:, C_ZERO:C_ZERO + 1], scale=1.0), inc=1)
                ev_d["ev_ev"] = ev_gc
                PRG.waits("sy", [ev_d["xl"]])
                ev_x2 = PRG.dma_sy("xc", lambda sy, dm=dm: sy.dma_start(
                    out=xchunk, in_=xT.rearrange("(kc p) t -> p kc t", p=P)[:, dm, :]))
                PRG.waits("sy", [ev_d["sl"]])
                ev_s2 = PRG.dma_sy("sc", lambda sy, dm=dm: sy.dma_start(
                    out=schunk, in_=d_s[dm]))
                PRG.waits("ve", [ev_gc, ev_x2, ev_s2, ev_d["odma"]])
                PRG.ve(lambda v: v.tensor_tensor(obuf, gchunk, schunk, ALU.mult))
                ev_ob = PRG.ve(lambda v: v.tensor_tensor(obuf, obuf, xchunk, ALU.add), inc=1)
                ev_d["xl"] = ev_ob
                ev_d["sl"] = ev_ob
                ev_d["gcr"] = ev_ob
                PRG.wait("sy", ev_ob)
                ev_d["odma"] = PRG.dma_sy("outd", lambda sy, dm=dm: sy.dma_start(
                    out=o_out.rearrange("(kc p) t -> p kc t", p=P)[:, dm, :], in_=obuf))

        for _sn, _sv in sorted(PRG.cnt.items()):
            if _sn.startswith("d_") and _sv > 0:
                PRG.q["sy"].append(["waitS", _sn, _sv, "wait"])
        PRG.finalize(block)
        return nc



def _host_prep(inputs):
    W_in = np.asarray(inputs["L0_attn_W_in"], np.float32)
    u = np.asarray(inputs["L0_attn_u"], np.float32)
    A_log = np.asarray(inputs["L0_attn_A_log"], np.float32)
    Dh = np.asarray(inputs["L0_attn_D"], np.float32)
    dtb = np.asarray(inputs["L0_attn_dt_bias"], np.float32)
    Bnw = np.asarray(inputs["L0_attn_B_norm_w"], np.float32)
    Cnw = np.asarray(inputs["L0_attn_C_norm_w"], np.float32)
    onw = np.asarray(inputs["L0_attn_out_norm_w"], np.float32)
    W_out = np.asarray(inputs["L0_attn_W_out"], np.float32)
    Wg = np.asarray(inputs["L0_ffn_w_gate"], np.float32)
    Wu = np.asarray(inputs["L0_ffn_w_up"], np.float32)
    Wd = np.asarray(inputs["L0_ffn_w_down"], np.float32)

    perm = np.concatenate([np.arange(0, NS, 2), np.arange(1, NS, 2)])
    wtA = np.ascontiguousarray(np.concatenate([W_in[DI:2 * DI], W_in[0:DI]], 0).T)
    rows = np.concatenate([
        W_in[2 * DI + perm], W_in[2 * DI + NS + perm],
        W_in[2 * DI + 2 * NS:2 * DI + 2 * NS + H],
        W_in[2 * DI + 2 * NS + H:2 * DI + 2 * NS + 2 * H],
        W_in[2 * DI + 2 * NS + 2 * H:],
    ], 0)
    wtB = np.ascontiguousarray(rows.T)
    woutT = np.ascontiguousarray(W_out.T)
    wgT = np.ascontiguousarray(Wg.T)
    wuT = np.ascontiguousarray(Wu.T)
    wdT = np.ascontiguousarray(Wd.T)

    vsm = np.zeros((P, 8), np.float32)
    vsm[0:H, 0] = A_log
    vsm[0:H, 1] = Dh
    vsm[0:H, 2] = dtb
    vsm[:, 3] = Bnw[perm]
    vsm[:, 4] = Cnw[perm]

    consts = np.zeros((P, NCONST), np.float32)
    consts[:, C_ONE] = 1.0
    consts[:, C_HPI] = HALF_PI
    consts[:, C_EPSLN] = EPS
    consts[:, C_EPSRMS] = EPS
    consts[0:H, C_SIGN] = -1.0
    consts[H:, C_SIGN] = 1.0

    onwbc = np.broadcast_to(onw[None, :], (P, DI)).copy()

    trimask = np.zeros((P, Q), np.float32)
    for si in range(Q):
        trimask[si, si + 1:] = 1.0
        trimask[64 + si, si + 1:] = 1.0

    resetm = np.ones((P, TPC), np.float32)
    resetm[:, 0::Q] = 0.0
    ones_col = np.ones((P, 1), np.float32)
    onesb_col = np.ones((P, 1), ml_dtypes.bfloat16)
    onesb_row = np.ones((1, P), ml_dtypes.bfloat16)
    identb = np.eye(P, dtype=ml_dtypes.bfloat16)
    identf = np.eye(P, dtype=np.float32)
    onesf_row = np.ones((1, P), np.float32)
    permm = np.zeros((P, P), np.float32)
    for i in range(64):
        permm[64 + i, i] = 1.0
        permm[i, 64 + i] = 1.0

    shared = dict(wtA=wtA, wtB=wtB, woutT=woutT, wgT=wgT, wuT=wuT, wdT=wdT,
                  vsm=vsm, consts=consts, onwbc=onwbc, trimask=trimask, resetm=resetm,
                  ones_in=ones_col, onesb_in=onesb_col, onesrow_in=onesb_row,
                  identb_in=identb, identf_in=identf, onesrowf_in=onesf_row,
                  permb_in=permm)

    in_maps = []
    for q in range(8):
        b, m = divmod(q, 4)
        im = dict(shared)
        im["xT"] = np.ascontiguousarray(u[b, TPC * m:TPC * (m + 1), :].T)
        mthq = np.zeros((P, 8), np.float32)
        if m < 3:
            mthq[0:H, m + 1] = 1.0
        mthq[H:, 0:m] = 1.0
        mthq[0:H, 4:4 + m] = 1.0
        im["mth"] = mthq
        mabq = np.zeros((P, 8), np.float32)
        for j in range(4):
            if j < m:
                mabq[:, j] = 1.0
        for kk in range(4):
            if kk <= m - 1:
                mabq[kk, 4] = 1.0
        im["mab"] = mabq
        in_maps.append(im)
    return in_maps


def kernel(**inputs):
    in_maps = _host_prep(inputs)
    nc = build_program()
    res = run_bass_kernel_spmd(nc, in_maps, core_ids=list(range(8)))
    out = np.zeros((2, 4096, D), np.float32)
    for q in range(8):
        b, m = divmod(q, 4)
        out[b, TPC * m:TPC * (m + 1), :] = np.asarray(res.results[q]["o_out"]).T
    return out


# version 06
# speedup vs baseline: 1.0955x; 1.0955x over previous
import numpy as np
import ml_dtypes
from contextlib import ExitStack

import concourse.bass as bass
import concourse.mybir as mybir
from concourse.bass_utils import run_bass_kernel_spmd

F32 = mybir.dt.float32
BF16 = mybir.dt.bfloat16
I32 = mybir.dt.int32
AF = mybir.ActivationFunctionType
ALU = mybir.AluOpType

P = 128
D = 2048
DI = 4096
H = 64
HD = 64
NS = 128
Q = 64
TPC = 1024
NT = 8
NCH = 16
KC = D // P
HID = 8192
EPS = 1e-5

TWO_PI = float(2 * np.pi)
INV_2PI = float(1.0 / (2 * np.pi))
HALF_PI = float(np.pi / 2)

C_ONE = 0
C_HPI = 1
C_EPSLN = 2
C_EPSRMS = 3
C_SIGN = 4
C_ZERO = 5
NCONST = 6

DTSIZE = {F32: 4, BF16: 2, I32: 4}


class Arena:

    def __init__(self, base_ap, nbytes):
        self.base = base_ap
        self.nbytes = nbytes
        self.blocks = {}
        self.free_list = [(0, nbytes)]

    def alloc(self, name, shape, dtype):
        assert 1 <= shape[0] <= P, shape
        n = int(np.prod(shape[1:])) * DTSIZE[dtype]
        n = (n + 63) & ~63
        for i, (off, size) in enumerate(self.free_list):
            if size >= n:
                self.free_list[i] = (off + n, size - n)
                if self.free_list[i][1] == 0:
                    del self.free_list[i]
                self.blocks[name] = (off, n)
                return self._view(off, shape, dtype)
        raise RuntimeError(f"arena OOM for {name}: need {n}, free={self.free_list}")

    def _view(self, off, shape, dtype):
        assert off % 4 == 0
        ap = self.base[:, off // 4:(off + int(np.prod(shape[1:])) * DTSIZE[dtype] + 3) // 4]
        if dtype != F32:
            ap = ap.bitcast(dtype)
        want = [P] + list(shape[1:])
        ap = ap.rearrange("p (x) -> p x") if False else ap
        if len(shape) == 2:
            ap = ap[:, 0:shape[1]] if shape[0] == P else ap[0:shape[0], 0:shape[1]]
        else:
            tot = int(np.prod(shape[1:]))
            ap = ap[:, 0:tot]
            kw = {}
            dims = shape[1:]
            pat_in = "p (" + " ".join(f"d{i}" for i in range(len(dims))) + ")"
            pat_out = "p " + " ".join(f"d{i}" for i in range(len(dims)))
            for i, d_ in enumerate(dims):
                kw[f"d{i}"] = d_
            ap = ap.rearrange(f"{pat_in} -> {pat_out}", **kw)
            if shape[0] != P:
                ap = ap[0:shape[0]]
        return ap

    def free(self, name):
        off, n = self.blocks.pop(name)
        self.free_list.append((off, n))
        self.free_list.sort()
        merged = []
        for off, n in self.free_list:
            if merged and merged[-1][0] + merged[-1][1] == off:
                merged[-1] = (merged[-1][0], merged[-1][1] + n)
            else:
                merged.append((off, n))
        self.free_list = [tuple(x) for x in merged]

    def high_water(self):
        return self.nbytes - sum(s for _, s in self.free_list)


class Prog:

    ENGS = ("te", "ve", "ac", "po", "sy")

    def __init__(self, nc, stack):
        self.nc = nc
        self.stack = stack
        self.q = {e: [] for e in self.ENGS}
        self.sems = {}
        self.cnt = {}
        self.waited = {e: {} for e in self.ENGS}
        for name in ("s_te", "s_ve", "s_ac", "s_po", "s_cc"):
            self.sems[name] = stack.enter_context(nc.semaphore(name))
            self.cnt[name] = 0

    def _dsem(self, key):
        name = "d_" + key
        if name not in self.sems:
            self.sems[name] = self.stack.enter_context(self.nc.semaphore(name))
            self.cnt[name] = 0
        return name

    SELF_SERIAL = {"ve": "s_ve", "ac": "s_ac", "po": "s_po"}

    def _emit(self, eng, fn, sem=None, inc=0):
        ssem = self.SELF_SERIAL.get(eng)
        if ssem is not None:
            self.wait(eng, (ssem, self.cnt[ssem]))
            if not inc:
                sem, inc = ssem, 1
        if inc:
            self.cnt[sem] += inc
            val = self.cnt[sem]
            self.q[eng].append([fn, self.sems[sem], inc])
            return (sem, val)
        self.q[eng].append([fn, None, 0])
        return None

    def te(self, fn, inc=0):
        return self._emit("te", fn, "s_te", inc)

    def ve(self, fn, inc=0):
        return self._emit("ve", fn, "s_ve", inc)

    def ac(self, fn, inc=0):
        return self._emit("ac", fn, "s_ac", inc)

    def po(self, fn, inc=0):
        return self._emit("po", fn, "s_po", inc)

    def dma_sy(self, key, fn):
        return self._emit("sy", fn, self._dsem(key), 16)

    def dma_po(self, key, fn):
        return self._emit("po", fn, self._dsem(key), 16)

    def cc(self, fn):
        return self._emit("po", fn, "s_cc", 1)

    def wait(self, eng, ev):
        if ev is None:
            return
        sem, val = ev
        if self.waited[eng].get(sem, 0) >= val:
            return
        self.waited[eng][sem] = val
        self.q[eng].append(["waitS", sem, val, "wait"])

    def waits(self, eng, evs):
        best = {}
        for ev in evs:
            if ev is None:
                continue
            sem, val = ev
            if best.get(sem, -1) < val:
                best[sem] = val
        for sem, val in best.items():
            self.wait(eng, (sem, val))

    def mark(self, eng):
        sem = {"te": "s_te", "ve": "s_ve", "ac": "s_ac", "po": "s_po"}[eng]
        if eng != "te":
            return (sem, self.cnt[sem])
        s = self.sems[sem]
        q = self.q[eng]
        for item in reversed(q):
            if len(item) == 3:
                if item[1] is None:
                    self.cnt[sem] += 1
                    item[1] = s
                    item[2] = 1
                return (sem, self.cnt[sem])
        return (sem, self.cnt[sem])

    def dma_quiesce(self):
        for sn, sv in sorted(self.cnt.items()):
            if sn.startswith("d_") and sv > 0:
                for eng in ("sy", "po"):
                    self.wait(eng, (sn, sv))

    def barrier(self, engs=None):
        engs = engs or ("te", "ve", "ac", "po")
        evs = {e: self.mark(e) for e in engs}
        for eng in engs:
            for peer, ev in evs.items():
                if peer != eng:
                    self.wait(eng, ev)
        for peer, ev in evs.items():
            self.wait("sy", ev)

    def finalize(self, block):
        def runq(e, items):
            for item in items:
                if item[0] == "waitS":
                    _, sem, val, _ = item
                    e.wait_ge(self.sems[sem], val)
                    continue
                fn, sem, inc = item[0], item[1], item[2]
                inst = fn(e)
                if sem is not None:
                    inst.then_inc(sem, inc)

        block.tensor(lambda t: runq(t, self.q["te"]))
        block.vector(lambda v: runq(v, self.q["ve"]))
        block.scalar(lambda a: runq(a, self.q["ac"]))
        block.gpsimd(lambda g: runq(g, self.q["po"]))
        block.sync(lambda s: runq(s, self.q["sy"]))


def build_program(debug=(), stop_after=None):
    nc = bass.Bass("TRN2", target_bir_lowering=False, debug=False)
    dbg = set(debug)

    def dri(name, shape, dt=F32):
        return nc.dram_tensor(name, shape, dt, kind="ExternalInput").ap()

    xT = dri("xT", [D, TPC])
    wtA = dri("wtA", [D, DI * 2])
    wtB = dri("wtB", [D, 448])
    woutT = dri("woutT", [DI, D])
    wgT = dri("wgT", [D, HID])
    wuT = dri("wuT", [D, HID])
    wdT = dri("wdT", [HID, D])
    vsm = dri("vsm", [P, 8])
    consts = dri("consts", [P, NCONST])
    onwbc = dri("onwbc", [P, DI])
    mth = dri("mth", [P, 8])
    mab = dri("mab", [P, 8])
    trimask = dri("trimask", [P, Q])
    resetm_in = dri("resetm", [P, TPC])
    ones_in = dri("ones_in", [P, 1])
    onesb_in = dri("onesb_in", [P, 1], BF16)
    onesrow_in = dri("onesrow_in", [1, P], BF16)
    identb_in = dri("identb_in", [P, P], BF16)
    identf_in = dri("identf_in", [P, P], F32)
    onesrowf_in = dri("onesrowf_in", [1, P], F32)
    permb_in = dri("permb_in", [P, P], F32)

    o_out = nc.dram_tensor("o_out", [D, TPC], F32, kind="ExternalOutput").ap()

    dout = {}

    def dbgout(name, shape, dt=F32):
        if name in dbg:
            dout[name] = nc.dram_tensor("dbg_" + name, shape, dt, kind="ExternalOutput").ap()
            return dout[name]
        return None

    d_w2 = nc.dram_tensor("d_w2", [NT, P, DI], BF16).ap()
    d_zs = nc.dram_tensor("d_zs", [NT, P, DI], BF16).ap()
    d_yznT = nc.dram_tensor("d_yznT", [KC * 2, NT, P, P], BF16).ap()
    d_s = nc.dram_tensor("d_s", [KC, P, TPC], BF16).ap()
    d_xn = nc.dram_tensor("d_xn", [P, KC * TPC], BF16).ap()
    d_acs_hi = nc.dram_tensor("d_acs_hi", [H, TPC], BF16).ap()
    d_acs_lo = nc.dram_tensor("d_acs_lo", [H, TPC], BF16).ap()
    d_gamT = nc.dram_tensor("d_gamT", [NCH, H], F32).ap()
    d_cc1i = nc.dram_tensor("d_cc1i", [1, 256], F32).ap()
    d_cc1o = nc.dram_tensor("d_cc1o", [4, 256], F32).ap()
    d_ccFi = nc.dram_tensor("d_ccFi", [P, DI], F32).ap()
    d_ccFo = nc.dram_tensor("d_ccFo", [4 * P, DI], F32).ap()
    d_ccMi = nc.dram_tensor("d_ccMi", [1, H], F32).ap()
    d_ccMo = nc.dram_tensor("d_ccMo", [4, H], F32).ap()
    d_coef = nc.dram_tensor("d_coef", [4, H], F32).ap()

    with ExitStack() as stack:
        sb = lambda n, s, d: stack.enter_context(nc.sbuf_tensor(n, s, d))
        pst = lambda n, s, d=F32: stack.enter_context(nc.psum_tensor(n, s, d))
        PRG = Prog(nc, stack)

        cst = sb("cst", [P, NCONST], F32)
        vsm_sb = sb("vsm_sb", [P, 8], F32)
        onesc = sb("onesc", [P, 1], F32)
        onesbc = sb("onesbc", [P, 1], BF16)
        onesrow = sb("onesrow", [1, P], BF16)
        identb = sb("identb", [P, P], BF16)
        identf = sb("identf", [P, P], F32)
        onesrowf = sb("onesrowf", [1, P], F32)
        permb = sb("permb", [P, P], F32)
        trim = sb("trim", [P, Q], F32)
        resetm = sb("resetm_sb", [P, TPC], F32)
        mth_sb = sb("mth_sb", [P, 8], F32)
        mab_sb = sb("mab_sb", [P, 8], F32)
        gam_sb = sb("gam_sb", [P, NCH], F32)
        acstot = sb("acstot", [P, 1], F32)

        ARENA_BYTES = 200 * 1024
        arena_t = sb("arena", [P, ARENA_BYTES // 4], F32)
        AR = Arena(arena_t, ARENA_BYTES)
        xn_sb = AR.alloc("xn_sb", [P, KC, TPC], BF16)
        sig1T = AR.alloc("sig1T", [P, NT, H], F32)
        eC2T = AR.alloc("eC2T", [P, NT, H], F32)
        eF2T = AR.alloc("eF2T", [P, NT, H], F32)
        ewT = AR.alloc("ewT", [P, NT, H], F32)
        eRelT = AR.alloc("eRelT", [P, NT, H], F32)
        acs2T = AR.alloc("acs2T", [P, NT, H], F32)

        ps = [pst(f"ps{i}", [P, 512]) for i in range(7)]
        ps_tp = pst("ps_tp", [P, 1024], BF16)

        block = stack.enter_context(nc.Block())

        ev_cst = PRG.dma_sy("const", lambda s: s.dma_start(out=cst[:, :], in_=consts))
        ev_vsm = PRG.dma_sy("const", lambda s: s.dma_start(out=vsm_sb[:, :], in_=vsm))
        ev_ones = PRG.dma_sy("const", lambda s: s.dma_start(out=onesc[:, :], in_=ones_in))
        ev_onesb = PRG.dma_sy("const", lambda s: s.dma_start(out=onesbc[:, :], in_=onesb_in))
        ev_onesr = PRG.dma_sy("const", lambda s: s.dma_start(out=onesrow[:, :], in_=onesrow_in))
        ev_id = PRG.dma_po("bconst", lambda g: g.dma_start(out=identb[:, :], in_=identb_in))
        ev_idf = PRG.dma_sy("const", lambda s: s.dma_start(out=identf[:, :], in_=identf_in))
        ev_orf = PRG.dma_sy("const", lambda s: s.dma_start(out=onesrowf[:, :], in_=onesrowf_in))
        ev_perm = PRG.dma_sy("const", lambda s: s.dma_start(out=permb[:, :], in_=permb_in))
        ev_tri = PRG.dma_sy("const", lambda s: s.dma_start(out=trim[:, :], in_=trimask))
        ev_rsm = PRG.dma_sy("const", lambda s: s.dma_start(out=resetm[:, :], in_=resetm_in))
        ev_mth = PRG.dma_sy("const", lambda s: s.dma_start(out=mth_sb[:, :], in_=mth))
        ev_mab = PRG.dma_sy("const", lambda s: s.dma_start(out=mab_sb[:, :], in_=mab))

        ev_cst = ev_vsm = ev_ones = ev_onesb = ev_onesr = ev_idf = ev_orf = \
            ev_perm = ev_tri = ev_mth = ev_mab = ("d_const", PRG.cnt["d_const"])
        for _e in ("te", "ve", "ac", "po"):
            PRG.wait(_e, ev_cst)
            PRG.wait(_e, ev_id)
        x_sb = AR.alloc("x_sb", [P, KC, TPC], F32)
        ev_x = PRG.dma_sy("xload", lambda s: s.dma_start(
            out=x_sb, in_=xT.rearrange("(kc p) t -> p kc t", p=P)))

        wB_sb = AR.alloc("wB_sb", [P, KC, 448], BF16)
        ev_wB = PRG.dma_po("wB", lambda g: g.dma_start(
            out=wB_sb, in_=wtB.rearrange("(kc p) n -> p kc n", p=P)))

        def _finish():
            PRG.wait("sy", ev_x)
            PRG.barrier()
            for kc in range(KC):
                PRG.dma_sy("fin", lambda s, kc=kc: s.dma_start(
                    out=o_out.rearrange("(kc p) t -> p kc t", p=P)[:, kc, :],
                    in_=x_sb[:, kc, :]))
            for _sn, _sv in sorted(PRG.cnt.items()):
                if _sn.startswith("d_") and _sv > 0:
                    PRG.q["sy"].append(["waitS", _sn, _sv, "wait"])
            PRG.finalize(block)

        if stop_after == "p0":
            _finish()
            return nc

        xsq = AR.alloc("xsq", [P, KC, TPC], BF16)
        PRG.wait("ac", ev_x)
        ev_sq = None
        for kc in range(KC):
            ev_sq = PRG.ac(lambda a, kc=kc: a.activation(
                xsq[:, kc, :], x_sb[:, kc, :], AF.Square, bias=cst[:, C_ZERO:C_ZERO + 1],
                scale=1.0), inc=1 if kc == KC - 1 else 0)
        if stop_after == "p1a":
            _finish()
            return nc
        PRG.waits("te", [ev_x, ev_ones, ev_onesb, ev_sq])
        for th in range(2):
            for kc in range(KC):
                PRG.te(lambda t, th=th, kc=kc: t.matmul(
                    ps[th][0:1, :], onesc[:, :], x_sb[:, kc, 512 * th:512 * (th + 1)],
                    start=(kc == 0), stop=(kc == KC - 1)))
        for th in range(2):
            for kc in range(KC):
                PRG.te(lambda t, th=th, kc=kc: t.matmul(
                    ps[2 + th][0:1, :], onesbc[:, :], xsq[:, kc, 512 * th:512 * (th + 1)],
                    start=(kc == 0), stop=(kc == KC - 1)))
        ev_stat = PRG.mark("te")
        if stop_after == "p1b":
            _finish()
            return nc

        rowm = AR.alloc("rowm", [1, TPC], F32)
        rowv = AR.alloc("rowv", [1, TPC], F32)
        rowr = AR.alloc("rowr", [1, TPC], F32)
        rtmp = AR.alloc("rtmp", [1, TPC], F32)
        rmr_hi = AR.alloc("rmr_hi", [1, TPC], BF16)
        rmr_lo = AR.alloc("rmr_lo", [1, TPC], BF16)
        rmm_hi = AR.alloc("rmm_hi", [1, TPC], BF16)
        rmm_lo = AR.alloc("rmm_lo", [1, TPC], BF16)
        PRG.wait("ve", ev_stat)
        for th in range(2):
            sl = slice(512 * th, 512 * (th + 1))
            PRG.ve(lambda v, sl=sl, th=th: v.tensor_scalar(
                out=rowm[:, sl], in0=ps[th][0:1, :], scalar1=1.0 / D, scalar2=None,
                op0=ALU.mult))
            PRG.ve(lambda v, sl=sl, th=th: v.tensor_scalar(
                out=rowv[:, sl], in0=ps[2 + th][0:1, :], scalar1=1.0 / D, scalar2=None,
                op0=ALU.mult))
        ev_statrd = PRG.mark("ve")
        PRG.ve(lambda v: v.scalar_tensor_tensor(
            out=rtmp[:, :], in0=rowm[:, :], scalar=1.0, in1=rowm[:, :],
            op0=ALU.mult, op1=ALU.mult))
        ev_v = PRG.ve(lambda v: v.tensor_tensor(
            rowv[:, :], rowv[:, :], rtmp[:, :], ALU.subtract), inc=1)
        PRG.waits("ac", [ev_v, ev_cst])
        ev_sqr = PRG.ac(lambda a: a.activation(
            rtmp[:, :], rowv[:, :], AF.Sqrt, bias=cst[0:1, C_EPSLN:C_EPSLN + 1],
            scale=1.0), inc=1)
        PRG.wait("ve", ev_sqr)
        PRG.ve(lambda v: v.reciprocal(rowr[:, :], rtmp[:, :]))
        PRG.ve(lambda v: v.tensor_copy(rmr_hi[:, :], rowr[:, :]))
        PRG.ve(lambda v: v.tensor_tensor(rtmp[:, :], rowr[:, :], rmr_hi[:, :], ALU.subtract))
        PRG.ve(lambda v: v.tensor_copy(rmr_lo[:, :], rtmp[:, :]))
        PRG.ve(lambda v: v.tensor_tensor(rtmp[:, :], rowm[:, :], rowr[:, :], ALU.mult))
        PRG.ve(lambda v: v.tensor_copy(rmm_hi[:, :], rtmp[:, :]))
        PRG.ve(lambda v: v.tensor_tensor(rowm[:, :], rtmp[:, :], rmm_hi[:, :], ALU.subtract))
        ev_rows = PRG.ve(lambda v: v.tensor_copy(rmm_lo[:, :], rowm[:, :]), inc=1)

        if stop_after == "p1c":
            _finish()
            return nc
        PRG.waits("te", [ev_rows, ev_onesr, ev_statrd])
        for th in range(2):
            sl = slice(512 * th, 512 * (th + 1))
            PRG.te(lambda t, sl=sl, th=th: t.matmul(
                ps[4 + th][:, :], onesrow[:, :], rmr_hi[:, sl], start=True, stop=False))
            PRG.te(lambda t, sl=sl, th=th: t.matmul(
                ps[4 + th][:, :], onesrow[:, :], rmr_lo[:, sl], start=False, stop=True))
            PRG.te(lambda t, sl=sl, th=th: t.matmul(
                ps[0 + th][:, :], onesrow[:, :], rmm_hi[:, sl], start=True, stop=False))
            PRG.te(lambda t, sl=sl, th=th: t.matmul(
                ps[0 + th][:, :], onesrow[:, :], rmm_lo[:, sl], start=False, stop=True))
        ev_bc = PRG.mark("te")
        if stop_after == "p1d":
            _finish()
            return nc

        xtmp = AR.alloc("xtmp", [P, 512], F32)
        PRG.wait("ve", ev_bc)
        ev_xn = None
        for kc in range(KC):
            for th in range(2):
                sl = slice(512 * th, 512 * (th + 1))
                PRG.ve(lambda v, kc=kc, sl=sl, th=th: v.tensor_tensor(
                    xtmp[:, :], x_sb[:, kc, sl], ps[4 + th][:, :], ALU.mult))
                ev_xn = PRG.ve(lambda v, kc=kc, sl=sl, th=th: v.tensor_tensor(
                    xn_sb[:, kc, sl], xtmp[:, :], ps[0 + th][:, :], ALU.subtract),
                    inc=1 if (kc == KC - 1 and th == 1) else 0)

        if dbgout("xn", [P, KC * TPC]):
            xd = AR.alloc("xd_dbg", [P, TPC], F32)
            for kc in range(KC):
                PRG.wait("ve", ev_xn)
                e = PRG.ve(lambda v, kc=kc: v.tensor_copy(xd, xn_sb[:, kc, :]), inc=1)
                PRG.wait("sy", e)
                e2 = PRG.dma_sy("dbgxn", lambda s, kc=kc: s.dma_start(
                    out=dout["xn"][:, TPC * kc:TPC * (kc + 1)], in_=xd))
                PRG.wait("ve", e2)
            AR.free("xd_dbg")

        PRG.barrier()
        PRG.dma_quiesce()
        for n in ("xsq", "rowm", "rowv", "rowr", "rtmp", "rmr_hi", "rmr_lo",
                  "rmm_hi", "rmm_lo", "xtmp", "x_sb"):
            AR.free(n)

        if stop_after == "p1":
            _finish()
            return nc

        Braw = AR.alloc("Braw", [P, TPC], F32)
        Bsq = AR.alloc("Bsq", [P, TPC], BF16)
        Craw = AR.alloc("Craw", [P, TPC], F32)
        Csq = AR.alloc("Csq", [P, TPC], BF16)
        spF = AR.alloc("spF", [P, TPC], F32)
        spG = AR.alloc("spG", [P, TPC], F32)
        thf = AR.alloc("thf", [P, TPC], F32)

        PRG.waits("te", [ev_wB, ev_xn])

        def passB_mm(cols, mwid, b0, b1, outbase=0):
            for th in range(2):
                sl = slice(512 * th, 512 * (th + 1))
                pb = ps[b0] if th == 0 else ps[b1]
                for kc in range(KC):
                    PRG.te(lambda t, cols=cols, mwid=mwid, kc=kc, sl=sl, pb=pb, ob=outbase:
                           t.matmul(pb[ob:ob + mwid, :], wB_sb[:, kc, cols:cols + mwid],
                                    xn_sb[:, kc, sl], start=(kc == 0), stop=(kc == KC - 1)))

        def evac(dst, b0, b1, func, rows=P, marks=True):
            evs = []
            for th in range(2):
                sl = slice(512 * th, 512 * (th + 1))
                pb = ps[b0] if th == 0 else ps[b1]
                evs.append(PRG.ac(lambda a, sl=sl, pb=pb, dst=dst, func=func, rows=rows:
                                  a.activation(dst[0:rows, sl], pb[0:rows, :], func,
                                               bias=cst[0:rows, C_ZERO:C_ZERO + 1], scale=1.0),
                                  inc=1 if marks else 0))
            return evs[-1]

        passB_mm(0, 128, 0, 1)
        evB = PRG.mark("te")
        PRG.wait("ac", evB)
        evBr = evac(Braw, 0, 1, AF.Identity)
        evBs = evac(Bsq, 0, 1, AF.Square)
        passB_mm(128, 128, 2, 3)
        evC = PRG.mark("te")
        PRG.wait("ac", evC)
        evCr = evac(Craw, 2, 3, AF.Identity)
        evCs = evac(Csq, 2, 3, AF.Square)
        PRG.waits("te", [evBr, evBs])
        passB_mm(256, 64, 0, 1)
        ev1 = PRG.mark("te")
        PRG.wait("ac", ev1)
        ev_dtr = evac(spF, 0, 1, AF.Identity, rows=64)
        PRG.waits("te", [evCr, evCs])
        passB_mm(320, 64, 2, 3)
        ev2 = PRG.mark("te")
        PRG.wait("ac", ev2)
        ev_lam = evac(spG, 2, 3, AF.Identity, rows=64)
        PRG.waits("te", [ev_dtr])
        passB_mm(384, 64, 4, 5)
        passB_mm(384, 64, 4, 5, outbase=64)
        ev3 = PRG.mark("te")
        PRG.wait("ac", ev3)
        ev_th = evac(thf, 4, 5, AF.Identity)

        if dbgout("passB", [P, 4 * TPC]):
            t_ = dout["passB"]
            PRG.waits("sy", [evBr, evCr, ev_dtr, ev_lam, ev_th])
            PRG.dma_sy("dbg", lambda s: s.dma_start(out=t_[:, 0:TPC], in_=Braw))
            PRG.dma_sy("dbg", lambda s: s.dma_start(out=t_[:, TPC:2 * TPC], in_=Craw))
            PRG.dma_sy("dbg", lambda s: s.dma_start(out=t_[0:64, 2 * TPC:3 * TPC], in_=spF[0:64, :]))
            PRG.dma_sy("dbg", lambda s: s.dma_start(out=t_[0:64, 3 * TPC:4 * TPC], in_=spG[0:64, :]))

        PRG.barrier()

        if stop_after == "p2":
            _finish()
            return nc

        AR.free("wB_sb")
        dt_t = AR.alloc("dt_t", [64, TPC], F32)
        lam_t = AR.alloc("lam_t", [64, TPC], F32)
        A_t = AR.alloc("A_t", [64, TPC], F32)
        An_t = AR.alloc("An_t", [64, TPC], F32)
        sc1 = AR.alloc("sc1", [64, TPC], F32)
        sc2 = AR.alloc("sc2", [64, TPC], F32)
        negeA = AR.alloc("negeA", [64, 1], F32)
        tsum = AR.alloc("tsum", [P, 1], F32)

        PRG.waits("ve", [ev_dtr, ev_vsm])
        PRG.ve(lambda v: v.tensor_scalar(out=spF[0:64, :], in0=spF[0:64, :],
                                         scalar1=vsm_sb[0:64, 2:3], scalar2=None, op0=ALU.add))
        ev = PRG.mark("ve")
        PRG.wait("ac", ev)
        PRG.ac(lambda a: a.activation(sc1[0:64, :], spF[0:64, :], AF.Abs,
                                      bias=cst[0:64, C_ZERO:C_ZERO + 1], scale=1.0))
        PRG.ac(lambda a: a.activation(sc2[0:64, :], sc1[0:64, :], AF.Exp,
                                      bias=cst[0:64, C_ZERO:C_ZERO + 1], scale=-1.0))
        ev = PRG.ac(lambda a: a.activation(sc1[0:64, :], sc2[0:64, :], AF.Ln,
                                           bias=cst[0:64, C_ONE:C_ONE + 1], scale=1.0), inc=1)
        PRG.wait("ve", ev)
        PRG.ve(lambda v: v.tensor_scalar(out=sc2[0:64, :], in0=spF[0:64, :],
                                         scalar1=0.0, scalar2=None, op0=ALU.max))
        ev_dt = PRG.ve(lambda v: v.tensor_tensor(dt_t[0:64, :], sc2[0:64, :], sc1[0:64, :],
                                                 ALU.add), inc=1)
        PRG.wait("ac", ev_lam)
        ev_lm = PRG.ac(lambda a: a.activation(lam_t[0:64, :], spG[0:64, :], AF.Sigmoid,
                                              bias=cst[0:64, C_ZERO:C_ZERO + 1], scale=1.0),
                       inc=1)
        PRG.ac(lambda a: a.activation(negeA[0:64, :], vsm_sb[0:64, 0:1], AF.Exp,
                                      bias=cst[0:64, C_ZERO:C_ZERO + 1], scale=1.0))
        ev = PRG.mark("ac")
        PRG.waits("ve", [ev, ev_dt])
        PRG.ve(lambda v: v.tensor_scalar(out=negeA[0:64, :], in0=negeA[0:64, :],
                                         scalar1=-1.0, scalar2=None, op0=ALU.mult))
        ev_A = PRG.ve(lambda v: v.tensor_scalar(out=A_t[0:64, :], in0=dt_t[0:64, :],
                                                scalar1=negeA[0:64, :], scalar2=None,
                                                op0=ALU.mult), inc=1)
        PRG.wait("ve", ev_th)
        ev_ts = PRG.ve(lambda v: v.tensor_reduce(tsum[:, :], thf[:, :],
                                                 mybir.AxisListType.X, ALU.add), inc=1)
        cc1v = d_cc1i.rearrange("a (b c) -> (a b) c", c=1)
        PRG.waits("po", [ev_A, ev_ts, ev_cst])
        e1 = PRG.dma_po("cc1s", lambda g: g.dma_start(out=cc1v[0:64], in_=A_t[0:64, 0:1],
                                              allow_slow_non_contiguous=True))
        e2 = PRG.dma_po("cc1s", lambda g: g.dma_start(out=cc1v[64:128], in_=tsum[0:64, :],
                                              allow_slow_non_contiguous=True))
        e3 = PRG.dma_po("cc1s", lambda g: g.dma_start(out=cc1v[128:192], in_=tsum[0:64, :],
                                              allow_slow_non_contiguous=True))
        e4 = PRG.dma_po("cc1s", lambda g: g.dma_start(out=cc1v[192:256],
                                              in_=cst[0:64, C_ZERO:C_ZERO + 1],
                                              allow_slow_non_contiguous=True))
        PRG.waits("po", [e1, e2, e3, e4])
        ev_cc1 = PRG.cc(lambda g: g.collective_compute(
            "AllGather", ALU.bypass, replica_groups=[[0, 1, 2, 3], [4, 5, 6, 7]],
            ins=[d_cc1i.opt()], outs=[d_cc1o.opt()]))

        wbufs = [AR.alloc(f"wbuf{i}", [P, KC, 512], BF16) for i in range(2)]
        zsts = [AR.alloc(f"zst{i}", [P, 512], BF16) for i in range(2)]
        w2sls = [AR.alloc(f"w2sl{i}", [P, 512], BF16) for i in range(2)]
        xdFg = AR.alloc("xdFg", [P, NT, 512], BF16)
        Ftr = AR.alloc("Ftr", [P, 512], F32)
        st_pA = {"wbuf": [None, None], "wuse": [None, None], "zst": [None, None],
                 "w2d": [None, None], "psA": [None, None], "Fev": None, "Fdma": None}

        def passA_group(g, is_z, deps):
            par = g % 2
            PRG.wait("po", st_pA["wuse"][par])
            st_pA["wbuf"][par] = PRG.dma_po(
                f"wb{par}", lambda gg, g=g: gg.dma_start(
                    out=wbufs[par],
                    in_=wtA[:, 512 * g:512 * (g + 1)].rearrange("(kc p) n -> p kc n", p=P)))
            for t in range(NT):
                bpar = t % 2
                PRG.waits("te", [st_pA["wbuf"][par], st_pA["psA"][bpar], ev_xn])
                for kc in range(KC):
                    PRG.te(lambda tt, kc=kc, t=t, par=par, bpar=bpar: tt.matmul(
                        ps[bpar][:, :], xn_sb[:, kc, 128 * t:128 * (t + 1)],
                        wbufs[par][:, kc, :], start=(kc == 0), stop=(kc == KC - 1)))
                ev_mm = PRG.mark("te")
                if t == NT - 1:
                    st_pA["wuse"][par] = ev_mm
                if is_z:
                    PRG.waits("ac", [ev_mm, st_pA["zst"][par]])
                    eva = PRG.ac(lambda a, bpar=bpar, par=par: a.activation(
                        zsts[par], ps[bpar][:, :], AF.Silu,
                        bias=cst[:, C_ZERO:C_ZERO + 1], scale=1.0), inc=1)
                    st_pA["psA"][bpar] = eva
                    PRG.wait("sy", eva)
                    st_pA["zst"][par] = PRG.dma_sy(f"zs{par}", lambda sy, t=t, g=g, par=par: sy.dma_start(
                        out=d_zs[t, :, 512 * (g - 8):512 * (g - 7)], in_=zsts[par]))
                else:
                    PRG.waits("ve", [ev_mm, st_pA["w2d"][par],
                                     st_pA.get("w2u", [None, None])[par],
                                     st_pA.get("xdu")] + deps)
                    evv = PRG.ve(lambda v, bpar=bpar, par=par, t=t, gp=g: v.tensor_tensor(
                        w2sls[par].rearrange("p (a b) -> p a b", a=8),
                        ps[bpar][:, :].rearrange("p (a b) -> p a b", a=8),
                        sig1T[:, t, 8 * gp:8 * gp + 8][:, :, None].broadcast_to([P, 8, 64]),
                        ALU.mult), inc=1)
                    PRG.wait("sy", evv)
                    st_pA["w2d"][par] = PRG.dma_sy(f"wd{par}", lambda sy, t=t, gp=g, par=par: sy.dma_start(
                        out=d_w2[t, :, 512 * gp:512 * (gp + 1)], in_=w2sls[par]))
                    PRG.wait("po", evv)
                    PRG.waits("po", [st_pA.get("xdu")])
                    evp = PRG.po(lambda g_, par=par, t=t, gp=g: g_.tensor_tensor(
                        xdFg[:, t, :].rearrange("p (a b) -> p a b", a=8),
                        w2sls[par].rearrange("p (a b) -> p a b", a=8),
                        eF2T[:, t, 8 * gp:8 * gp + 8][:, :, None].broadcast_to([P, 8, 64]),
                        ALU.mult), inc=1)
                    st_pA.setdefault("w2u", [None, None])[par] = evp
                    st_pA["psA"][bpar] = evv
                    if t == NT - 1:
                        st_pA["wuse"][par] = evp
            if not is_z:
                ev_xdF = PRG.mark("po")
                PRG.waits("te", [ev_xdF, ev_BnT, st_pA["Fev"], st_pA.get("Fmrg")])
                st_pA["xdu"] = None
                for c in range(NCH):
                    t, half = c // 2, c % 2
                    PRG.te(lambda tt, t=t, half=half, c=c: tt.matmul(
                        ps[2 + half][:, :], BnT_b[64 * half:64 * half + 64, t, :],
                        xdFg[64 * half:64 * half + 64, t, :],
                        start=(c < 2), stop=(c >= NCH - 2),
                        tile_position=(64 * half, 0)))
                evf = PRG.mark("te")
                st_pA["xdu"] = evf
                PRG.waits("ac", [evf, st_pA["Fdma"], st_pA.get("Fmrg")])
                eva = PRG.ac(lambda a: a.activation(
                    Ftr, ps[2][:, :], AF.Identity,
                    bias=cst[:, C_ZERO:C_ZERO + 1], scale=1.0), inc=1)
                st_pA["Fev"] = eva
                PRG.waits("ve", [eva])
                evm = PRG.ve(lambda v: v.tensor_tensor(Ftr, Ftr, ps[3][:, :], ALU.add),
                             inc=1)
                st_pA["Fmrg"] = evm
                PRG.wait("sy", evm)
                st_pA["Fdma"] = PRG.dma_sy("ftr", lambda sy, gp=g: sy.dma_start(
                    out=d_ccFi[:, 512 * gp:512 * (gp + 1)], in_=Ftr))

        passA_group(8, True, [])
        passA_group(9, True, [])

        g1sb = AR.alloc("g1sb", [64, 256], F32)
        g1aT = AR.alloc("g1aT", [P, 4], F32)
        g1bT = AR.alloc("g1bT", [P, 4], F32)
        mrA = AR.alloc("mrA", [P, 1], F32)
        mrB = AR.alloc("mrB", [P, 1], F32)
        prefF = AR.alloc("prefF", [P, 1], F32)
        PRG.wait("sy", ev_cc1)
        ev_g1 = PRG.dma_sy("g1", lambda sy: sy.dma_start(out=g1sb[0:4, :], in_=d_cc1o))
        PRG.waits("te", [ev_g1, ev_idf, ev_lam, ev_th])
        PRG.te(lambda t: t.transpose(ps[3][:, 0:4], g1sb[0:4, 0:128], identf[0:4, 0:4]))
        ev_t1 = PRG.mark("te")
        PRG.te(lambda t: t.transpose(ps[4][:, 0:4], g1sb[0:4, 128:256], identf[0:4, 0:4]))
        ev_t2 = PRG.mark("te")
        PRG.waits("ve", [ev_t1, ev_t2, ev_mth])
        PRG.ve(lambda v: v.tensor_copy(g1aT[:, :], ps[3][:, 0:4]))
        PRG.ve(lambda v: v.tensor_copy(g1bT[:, :], ps[4][:, 0:4]))
        ev_g1rd = PRG.mark("ve")
        PRG.ve(lambda v: v.tensor_tensor(g1aT[:, :], g1aT[:, :], mth_sb[:, 0:4], ALU.mult))
        PRG.ve(lambda v: v.tensor_tensor(g1bT[:, :], g1bT[:, :], mth_sb[:, 4:8], ALU.mult))
        PRG.ve(lambda v: v.tensor_reduce(mrA[:, :], g1aT[:, :], mybir.AxisListType.X, ALU.add))
        PRG.ve(lambda v: v.tensor_reduce(mrB[:, :], g1bT[:, :], mybir.AxisListType.X, ALU.add))
        PRG.ve(lambda v: v.tensor_copy(prefF[0:64, :], mrB[0:64, :]))
        ev_pref = PRG.ve(lambda v: v.tensor_copy(prefF[64:128, :], mrA[64:128, :]), inc=1)

        onesT = AR.alloc("onesT", [P, TPC], F32)
        angf = AR.alloc("angf", [P, TPC], F32)
        kif = AR.alloc("kif", [P, TPC], I32)
        csf = AR.alloc("csf", [P, TPC], F32)
        snf = AR.alloc("snf", [P, TPC], F32)
        scrA = AR.alloc("scrA", [P, TPC], F32)
        scrB = AR.alloc("scrB", [P, TPC], F32)
        PRG.ve(lambda v: v.tensor_scalar(out=onesT[:, :], in0=thf[:, :], scalar1=0.0,
                                         scalar2=1.0, op0=ALU.mult, op1=ALU.add))
        PRG.ve(lambda v: v.tensor_tensor_scan(angf[:, :], onesT[:, :], thf[:, :],
                                              prefF[:, :], ALU.mult, ALU.add))
        PRG.ve(lambda v: v.tensor_scalar(out=kif[:, :], in0=angf[:, :], scalar1=INV_2PI,
                                         scalar2=None, op0=ALU.mult))
        PRG.ve(lambda v: v.scalar_tensor_tensor(out=thf[:, :], in0=kif[:, :],
                                                scalar=-TWO_PI, in1=angf[:, :],
                                                op0=ALU.mult, op1=ALU.add))
        PRG.ve(lambda v: v.tensor_scalar(out=angf[:, :], in0=thf[:, :],
                                         scalar1=float(np.pi), scalar2=None, op0=ALU.is_gt))
        PRG.ve(lambda v: v.scalar_tensor_tensor(out=thf[:, :], in0=angf[:, :],
                                                scalar=-TWO_PI, in1=thf[:, :],
                                                op0=ALU.mult, op1=ALU.add))
        PRG.ve(lambda v: v.tensor_scalar(out=angf[:, :], in0=thf[:, :],
                                         scalar1=float(-np.pi), scalar2=None, op0=ALU.is_lt))
        PRG.ve(lambda v: v.scalar_tensor_tensor(out=thf[:, :], in0=angf[:, :],
                                                scalar=TWO_PI, in1=thf[:, :],
                                                op0=ALU.mult, op1=ALU.add))
        ev_angr = PRG.mark("ve")
        PRG.wait("ac", ev_angr)
        ev_sin = PRG.ac(lambda a: a.activation(snf[:, :], thf[:, :], AF.Sin,
                                               bias=cst[:, C_ZERO:C_ZERO + 1], scale=1.0),
                        inc=1)
        PRG.ac(lambda a: a.activation(angf[:, :], thf[:, :], AF.Abs,
                                       bias=cst[:, C_ZERO:C_ZERO + 1], scale=1.0))
        ev_cos = PRG.ac(lambda a: a.activation(csf[:, :], angf[:, :], AF.Sin,
                                               bias=cst[:, C_HPI:C_HPI + 1], scale=-1.0),
                        inc=1)

        passA_group(10, True, [])

        Bn_f = AR.alloc("Bn_f", [P, TPC], F32)
        Cn_f = AR.alloc("Cn_f", [P, TPC], F32)
        Bn_b = AR.alloc("Bn_b", [P, TPC], BF16)
        Cn_b = AR.alloc("Cn_b", [P, TPC], BF16)
        BnT_b = AR.alloc("BnT_b", [P, NT, NS], BF16)
        rrow = AR.alloc("rrow", [1, TPC], F32)
        rhj = AR.alloc("rhj", [1, TPC], BF16)
        rlj = AR.alloc("rlj", [1, TPC], BF16)

        def rms_rope(raw, sq, wci, dst_f, dst_b, evsq):
            PRG.waits("te", [evsq])
            for th in range(2):
                PRG.te(lambda t, th=th: t.matmul(
                    ps[3 + th][0:1, :], onesbc[:, :], sq[:, 512 * th:512 * (th + 1)],
                    start=True, stop=True))
            evq = PRG.mark("te")
            PRG.wait("ac", evq)
            for th in range(2):
                PRG.ac(lambda a, th=th: a.activation(
                    rrow[:, 512 * th:512 * (th + 1)], ps[3 + th][0:1, :], AF.Sqrt,
                    bias=cst[0:1, C_EPSRMS:C_EPSRMS + 1], scale=1.0 / NS))
            evq2 = PRG.mark("ac")
            PRG.wait("ve", evq2)
            PRG.ve(lambda v: v.reciprocal(rrow[:, :], rrow[:, :]))
            PRG.ve(lambda v: v.tensor_copy(rhj[:, :], rrow[:, :]))
            PRG.ve(lambda v: v.tensor_tensor(rrow[:, :], rrow[:, :], rhj[:, :], ALU.subtract))
            evq3 = PRG.ve(lambda v: v.tensor_copy(rlj[:, :], rrow[:, :]), inc=1)
            PRG.wait("te", evq3)
            for th in range(2):
                sl = slice(512 * th, 512 * (th + 1))
                PRG.te(lambda t, sl=sl, th=th: t.matmul(
                    ps[3 + th][:, :], onesrow[:, :], rhj[:, sl], start=True, stop=False))
                PRG.te(lambda t, sl=sl, th=th: t.matmul(
                    ps[3 + th][:, :], onesrow[:, :], rlj[:, sl], start=False, stop=True))
            evq4 = PRG.mark("te")
            PRG.wait("ve", evq4)
            for th in range(2):
                sl = slice(512 * th, 512 * (th + 1))
                PRG.ve(lambda v, sl=sl, th=th: v.scalar_tensor_tensor(
                    out=dst_f[:, sl], in0=raw[:, sl], scalar=vsm_sb[:, wci:wci + 1],
                    in1=ps[3 + th][:, :], op0=ALU.mult, op1=ALU.mult))
            ev_n = PRG.mark("ve")
            PRG.waits("te", [ev_n, ev_perm])
            for th in range(2):
                sl = slice(512 * th, 512 * (th + 1))
                PRG.te(lambda t, sl=sl, th=th: t.matmul(
                    ps[5 + th][:, :], permb[:, :], dst_f[:, sl], start=True, stop=True))
            ev_sw = PRG.mark("te")
            PRG.waits("ve", [ev_sw, ev_cos, ev_sin])
            for th in range(2):
                sl = slice(512 * th, 512 * (th + 1))
                PRG.ve(lambda v, sl=sl: v.tensor_tensor(scrA[:, sl], csf[:, sl],
                                                        dst_f[:, sl], ALU.mult))
                PRG.ve(lambda v, sl=sl, th=th: v.scalar_tensor_tensor(
                    out=scrB[:, sl], in0=snf[:, sl], scalar=cst[:, C_SIGN:C_SIGN + 1],
                    in1=ps[5 + th][:, :], op0=ALU.mult, op1=ALU.mult))
                PRG.ve(lambda v, sl=sl: v.tensor_tensor(dst_f[:, sl], scrA[:, sl],
                                                        scrB[:, sl], ALU.add))
            ev_r = PRG.ve(lambda v: v.tensor_copy(dst_b, dst_f), inc=1)
            return ev_r

        PRG.wait("te", ev_g1rd)
        ev_Bn = rms_rope(Braw, Bsq, 3, Bn_f, Bn_b, evBs)
        ev_Cn = rms_rope(Craw, Csq, 4, Cn_f, Cn_b, evCs)

        passA_group(11, True, [])

        PRG.waits("te", [ev_Bn, ev_id])
        for t8 in range(NT):
            PRG.te(lambda t, t8=t8: t.transpose(
                ps_tp[:, 0:P], Bn_b[:, 128 * t8:128 * (t8 + 1)], identb[:, :]))
            evq = PRG.mark("te")
            PRG.wait("ac", evq)
            eva = PRG.ac(lambda a, t8=t8: a.activation(
                BnT_b[:, t8, :], ps_tp[:, 0:P], AF.Identity,
                bias=cst[:, C_ZERO:C_ZERO + 1], scale=1.0), inc=1)
            PRG.wait("te", eva)
        ev_BnT = PRG.mark("ac")

        ddg = AR.alloc("ddg", [P, TPC], BF16)
        PRG.waits("ve", [ev_Bn, ev_Cn])
        PRG.ve(lambda v: v.tensor_tensor(scrA[:, :], Bn_f[:, :], Cn_f[:, :], ALU.mult))
        ev_dd = PRG.ve(lambda v: v.tensor_copy(ddg[:, :], scrA[:, :]), inc=1)
        PRG.wait("te", ev_dd)
        for th in range(2):
            PRG.te(lambda t, th=th: t.matmul(
                ps[3 + th][0:1, :], onesbc[:, :], ddg[:, 512 * th:512 * (th + 1)],
                start=True, stop=True))
        ev_dr = PRG.mark("te")
        PRG.wait("ve", ev_dr)
        for th in range(2):
            sl = slice(512 * th, 512 * (th + 1))
            PRG.ve(lambda v, sl=sl, th=th: v.tensor_copy(rrow[:, sl], ps[3 + th][0:1, :]))
        PRG.ve(lambda v: v.tensor_copy(rhj[:, :], rrow[:, :]))
        PRG.ve(lambda v: v.tensor_tensor(rrow[:, :], rrow[:, :], rhj[:, :], ALU.subtract))
        ev_dl = PRG.ve(lambda v: v.tensor_copy(rlj[:, :], rrow[:, :]), inc=1)
        PRG.wait("te", ev_dl)
        for th in range(2):
            sl = slice(512 * th, 512 * (th + 1))
            PRG.te(lambda t, sl=sl, th=th: t.matmul(
                ps[3 + th][0:64, :], onesrow[:, 0:64], rhj[:, sl], start=True, stop=False))
            PRG.te(lambda t, sl=sl, th=th: t.matmul(
                ps[3 + th][0:64, :], onesrow[:, 0:64], rlj[:, sl], start=False, stop=True))
        ev_db = PRG.mark("te")

        passA_group(12, True, [])

        early = ["kif", "angf", "Braw", "Bsq", "Craw", "Csq", "thf"]
        if "rope" not in dbg:
            early += ["csf", "snf"]
        early += ["scrA", "scrB"]
        for n in early:
            AR.free(n)
        Acs = AR.alloc("Acs", [64, TPC], F32)
        Acs2 = AR.alloc("Acs2", [64, TPC], F32)
        AcsSt = AR.alloc("AcsSt", [64, TPC], F32)
        eC2 = AR.alloc("eC2", [64, TPC], F32)
        eRel = AR.alloc("eRel", [64, TPC], F32)
        eF2 = AR.alloc("eF2", [64, TPC], F32)
        sig1 = AR.alloc("sig1", [64, TPC], F32)
        ew_t = AR.alloc("ew_t", [64, TPC], F32)
        e2_t = AR.alloc("e2_t", [64, TPC], F32)

        PRG.waits("ve", [ev_A, ev_pref, ev_rsm])
        PRG.ve(lambda v: v.tensor_tensor_scan(Acs[0:64, :], resetm[0:64, :], A_t[0:64, :],
                                              0.0, ALU.mult, ALU.add))
        PRG.ve(lambda v: v.tensor_copy(An_t[0:64, 0:TPC - 1], A_t[0:64, 1:TPC]))
        PRG.ve(lambda v: v.tensor_copy(An_t[0:64, TPC - 1:TPC], mrA[0:64, :]))
        PRG.ve(lambda v: v.tensor_tensor(Acs2[0:64, :], Acs[0:64, :], An_t[0:64, :], ALU.add))
        PRG.ve(lambda v: v.tensor_tensor_scan(AcsSt[0:64, :], onesT[0:64, :], A_t[0:64, :],
                                              0.0, ALU.mult, ALU.add))
        PRG.ve(lambda v: v.tensor_tensor(
            sc1[0:64, :].rearrange("p (a b) -> p a b", b=Q),
            Acs[0:64, Q - 1:TPC:Q][:, :, None].broadcast_to([64, NCH, Q]),
            Acs2[0:64, :].rearrange("p (a b) -> p a b", b=Q), ALU.subtract))
        ev5 = PRG.mark("ve")
        PRG.wait("ac", ev5)
        PRG.ac(lambda a: a.activation(eC2[0:64, :], sc1[0:64, :], AF.Exp,
                                      bias=cst[0:64, C_ZERO:C_ZERO + 1], scale=1.0))
        PRG.ac(lambda a: a.activation(eRel[0:64, :], Acs[0:64, :], AF.Exp,
                                      bias=cst[0:64, C_ZERO:C_ZERO + 1], scale=1.0))
        ev_acst = PRG.ac(lambda a: a.activation(acstot[0:64, :], AcsSt[0:64, TPC - 1:TPC],
                                                AF.Identity, bias=cst[0:64, C_ZERO:C_ZERO + 1],
                                                scale=1.0), inc=1)
        PRG.ve(lambda v: v.tensor_tensor(sc2[0:64, :], AcsSt[0:64, :], An_t[0:64, :],
                                         ALU.add))
        ev5b = PRG.mark("ve")
        PRG.wait("ac", ev5b)
        ev_eF = PRG.ac(lambda a: a.activation(eF2[0:64, :], sc2[0:64, :], AF.Exp,
                                              bias=acstot[0:64, 0:1], scale=-1.0), inc=1)
        PRG.ac(lambda a: a.activation(sc1[0:64, :], An_t[0:64, :], AF.Exp,
                                      bias=cst[0:64, C_ZERO:C_ZERO + 1], scale=1.0))
        ev6 = PRG.mark("ac")
        PRG.waits("ve", [ev6, ev_lm])
        PRG.ve(lambda v: v.tensor_tensor(sc2[0:64, :], lam_t[0:64, :], sc1[0:64, :], ALU.mult))
        PRG.ve(lambda v: v.scalar_tensor_tensor(out=sc2[0:64, :], in0=lam_t[0:64, :],
                                                scalar=-1.0, in1=sc2[0:64, :],
                                                op0=ALU.mult, op1=ALU.add))
        PRG.ve(lambda v: v.tensor_scalar(out=sc2[0:64, :], in0=sc2[0:64, :], scalar1=1.0,
                                         scalar2=None, op0=ALU.add))
        PRG.ve(lambda v: v.tensor_tensor(sig1[0:64, :], dt_t[0:64, :], sc2[0:64, :], ALU.mult))
        PRG.ve(lambda v: v.tensor_tensor(sc1[0:64, :], dt_t[0:64, :], lam_t[0:64, :], ALU.mult))
        PRG.wait("ve", ev_db)
        for th in range(2):
            sl = slice(512 * th, 512 * (th + 1))
            PRG.ve(lambda v, sl=sl, th=th: v.tensor_tensor(
                e2_t[0:64, sl], sc1[0:64, sl], ps[3 + th][0:64, :], ALU.mult))
        PRG.ve(lambda v: v.tensor_scalar(out=e2_t[0:64, :], in0=e2_t[0:64, :],
                                         scalar1=vsm_sb[0:64, 1:2], scalar2=None, op0=ALU.add))
        PRG.ve(lambda v: v.reciprocal(sc2[0:64, :], sig1[0:64, :]))
        PRG.ve(lambda v: v.tensor_tensor(ew_t[0:64, :], e2_t[0:64, :], sc2[0:64, :], ALU.mult))
        PRG.ve(lambda v: v.tensor_copy(gam_sb[0:64, :], Acs[0:64, Q - 1:TPC:Q]))
        ev7 = PRG.mark("ve")
        PRG.wait("ac", ev7)
        ev_gam = PRG.ac(lambda a: a.activation(gam_sb[0:64, :], gam_sb[0:64, :], AF.Exp,
                                               bias=cst[0:64, C_ZERO:C_ZERO + 1], scale=1.0),
                        inc=1)

        passA_group(13, True, [])

        def sigT(src, dst, evdep):
            PRG.waits("te", [evdep, ev_idf])
            for t8 in range(NT):
                PRG.te(lambda t, t8=t8, src=src: t.transpose(
                    ps[3 + (t8 % 2)][:, 0:64], src[0:64, 128 * t8:128 * (t8 + 1)],
                    identf[0:64, 0:64]))
                evq = PRG.mark("te")
                PRG.wait("ac", evq)
                eva = PRG.ac(lambda a, t8=t8, dst=dst: a.activation(
                    dst[:, t8, :], ps[3 + (t8 % 2)][:, 0:64], AF.Identity,
                    bias=cst[:, C_ZERO:C_ZERO + 1], scale=1.0), inc=1)
                PRG.wait("te", eva)
            return PRG.mark("ac")

        ev_sv = PRG.mark("ve")
        ev_sigT = sigT(sig1, sig1T, ev_sv)
        ev_eC2T = sigT(eC2, eC2T, ev_sv)
        ev_eF2T = sigT(eF2, eF2T, ev_eF)
        ev_ewT = sigT(ew_t, ewT, ev_sv)
        ev_eRelT = sigT(eRel, eRelT, ev_sv)
        ev_acs2T = sigT(Acs2, acs2T, ev_sv)

        achib = AR.alloc("achib", [64, TPC], BF16)
        aclob = AR.alloc("aclob", [64, TPC], BF16)
        PRG.ve(lambda v: v.tensor_copy(achib[0:64, :], Acs[0:64, :]))
        PRG.ve(lambda v: v.tensor_tensor(sc1[0:64, :], Acs[0:64, :], achib[0:64, :],
                                         ALU.subtract))
        ev8 = PRG.ve(lambda v: v.tensor_copy(aclob[0:64, :], sc1[0:64, :]), inc=1)
        PRG.wait("sy", ev8)
        ev_ahi = PRG.dma_sy("acsh", lambda sy: sy.dma_start(out=d_acs_hi, in_=achib[0:64, :]))
        ev_alo = PRG.dma_sy("acsl", lambda sy: sy.dma_start(out=d_acs_lo, in_=aclob[0:64, :]))
        gamT_sb = AR.alloc("gamT_sb", [64, H], F32)
        PRG.waits("te", [ev_gam])
        PRG.te(lambda t: t.transpose(ps[3][0:NCH, 0:64], gam_sb[0:64, 0:NCH], identf[0:64, 0:64]))
        ev9 = PRG.mark("te")
        PRG.wait("ac", ev9)
        ev10 = PRG.ac(lambda a: a.activation(gamT_sb[0:NCH, :], ps[3][0:NCH, 0:64],
                                             AF.Identity, bias=cst[0:NCH, C_ZERO:C_ZERO + 1],
                                             scale=1.0), inc=1)
        PRG.wait("sy", ev10)
        ev_gamd = PRG.dma_sy("gamd", lambda sy: sy.dma_start(out=d_gamT, in_=gamT_sb[0:NCH, :]))
        PRG.waits("te", [ev_acst])
        PRG.te(lambda t: t.transpose(ps[4][0:1, 0:64], acstot[0:64, 0:1], identf[0:64, 0:64]))
        ev11 = PRG.mark("te")
        PRG.wait("ac", ev11)
        PRG.wait("ac", ev_gamd)
        ev12 = PRG.ac(lambda a: a.activation(gamT_sb[0:1, :], ps[4][0:1, 0:64], AF.Identity,
                                             bias=cst[0:1, C_ZERO:C_ZERO + 1], scale=1.0),
                      inc=1)
        PRG.wait("sy", ev12)
        ev_Mi = PRG.dma_sy("mi", lambda sy: sy.dma_start(out=d_ccMi, in_=gamT_sb[0:1, :]))

        passA_group(14, True, [])
        passA_group(15, True, [])

        xs_deps = [ev_sigT, ev_eF2T]
        for g in range(8):
            passA_group(g, False, xs_deps)

        if "F" in dbg and dbgout("F", [P, DI]):
            PRG.wait("sy", st_pA["Fdma"])
            PRG.dma_sy("dbg", lambda sy: sy.dma_start(out=dout["F"], in_=d_ccFi))
        if "w2" in dbg and dbgout("w2", [NT * P, DI]):
            PRG.wait("sy", st_pA["w2d"][0])
            PRG.wait("sy", st_pA["w2d"][1])
            PRG.dma_sy("dbg", lambda sy: sy.dma_start(
                out=dout["w2"], in_=d_w2.rearrange("a p n -> (a p) n")))
        if "scal" in dbg and dbgout("scal", [64, 8 * TPC]):
            t_ = dout["scal"]
            PRG.waits("sy", [ev_eF, ev7, ev_gam])
            for i, src in enumerate((dt_t, lam_t, A_t, Acs, Acs2, sig1, ew_t, e2_t)):
                PRG.dma_sy("dbg", lambda sy, i=i, src=src: sy.dma_start(
                    out=t_[:, TPC * i:TPC * (i + 1)], in_=src[0:64, :]))
        if "rope" in dbg and dbgout("rope", [P, 4 * TPC]):
            t_ = dout["rope"]
            PRG.waits("sy", [ev_Bn, ev_Cn, ev_cos, ev_sin])
            PRG.dma_sy("dbg", lambda sy: sy.dma_start(out=t_[:, 0:TPC], in_=Bn_f))
            PRG.dma_sy("dbg", lambda sy: sy.dma_start(out=t_[:, TPC:2 * TPC], in_=Cn_f))
            PRG.dma_sy("dbg", lambda sy: sy.dma_start(out=t_[:, 2 * TPC:3 * TPC], in_=csf))
            PRG.dma_sy("dbg", lambda sy: sy.dma_start(out=t_[:, 3 * TPC:4 * TPC], in_=snf))

        nc._dbg_blocks = dict(AR.blocks)
        ev_xnst = PRG.dma_sy("xnst", lambda sy: sy.dma_start(
            out=d_xn, in_=xn_sb.rearrange("p a b -> p (a b)")))
        PRG.barrier()
        PRG.dma_quiesce()
        plist = ["spF", "spG", "xn_sb", "sig1T", "eF2T",
                 "dt_t", "lam_t", "A_t", "An_t", "sc1", "sc2", "negeA", "tsum",
                 "g1sb", "g1aT", "g1bT", "mrB", "prefF", "onesT",
                 "Bn_f", "Cn_f", "rrow", "rhj", "rlj", "ddg",
                 "Acs", "AcsSt", "eC2", "eRel", "eF2", "sig1", "ew_t", "e2_t",
                 "achib", "aclob", "gamT_sb", "Acs2",
                 "wbuf0", "wbuf1", "zst0", "zst1", "w2sl0", "w2sl1", "xdFg", "Ftr"]
        if "rope" in dbg:
            plist += ["csf", "snf"]
        for n in plist:
            AR.free(n)

        if stop_after == "p5":
            _finish()
            return nc

        PRG.waits("po", [st_pA["Fdma"], ev_Mi])
        ev_ccF = PRG.cc(lambda g: g.collective_compute(
            "AllGather", ALU.bypass, replica_groups=[[0, 1, 2, 3], [4, 5, 6, 7]],
            ins=[d_ccFi.opt()], outs=[d_ccFo.opt()]))
        ev_ccM = PRG.cc(lambda g: g.collective_compute(
            "AllGather", ALU.bypass, replica_groups=[[0, 1, 2, 3], [4, 5, 6, 7]],
            ins=[d_ccMi.opt()], outs=[d_ccMo.opt()]))

        R_sb = AR.alloc("R_sb", [P, DI], F32)
        Rb = [AR.alloc("Rb0", [P, DI], BF16), AR.alloc("Rb1", [P, DI], BF16)]
        Fg = AR.alloc("Fg", [P, DI], F32)
        g2 = AR.alloc("g2", [64, H], F32)
        mT = AR.alloc("mT", [64, 4], F32)
        Ssuf = AR.alloc("Ssuf", [64, 4], F32)
        coefT = AR.alloc("coefT", [64, 4], F32)
        coefR = AR.alloc("coefR", [4, H], F32)
        crow = AR.alloc("crow", [1, 4 * H], F32)

        PRG.wait("sy", ev_ccM)
        ev13 = PRG.dma_sy("g2l", lambda sy: sy.dma_start(out=g2[0:4, :], in_=d_ccMo))
        PRG.waits("ve", [ev13, ev_mab])
        ev14 = PRG.ve(lambda v: v.tensor_scalar(out=g2[0:4, :], in0=g2[0:4, :],
                                                scalar1=mab_sb[0:4, 4:5], scalar2=None,
                                                op0=ALU.mult), inc=1)
        PRG.wait("te", ev14)
        PRG.te(lambda t: t.transpose(ps[3][0:64, 0:4], g2[0:4, 0:64], identf[0:4, 0:4]))
        ev15 = PRG.mark("te")
        PRG.wait("ve", ev15)
        PRG.ve(lambda v: v.tensor_copy(mT[0:64, :], ps[3][0:64, 0:4]))
        PRG.ve(lambda v: v.tensor_scalar(out=Ssuf[0:64, 3:4], in0=mT[0:64, 3:4],
                                         scalar1=0.0, scalar2=None, op0=ALU.mult))
        PRG.ve(lambda v: v.tensor_copy(Ssuf[0:64, 2:3], mT[0:64, 3:4]))
        PRG.ve(lambda v: v.tensor_tensor(Ssuf[0:64, 1:2], mT[0:64, 2:3], mT[0:64, 3:4],
                                         ALU.add))
        PRG.ve(lambda v: v.tensor_tensor(Ssuf[0:64, 0:1], Ssuf[0:64, 1:2], mT[0:64, 1:2],
                                         ALU.add))
        ev16 = PRG.mark("ve")
        PRG.wait("ac", ev16)
        ev17 = PRG.ac(lambda a: a.activation(coefT[0:64, :], Ssuf[0:64, :], AF.Exp,
                                             bias=cst[0:64, C_ZERO:C_ZERO + 1], scale=1.0),
                      inc=1)
        PRG.wait("ve", ev17)
        ev18 = PRG.ve(lambda v: v.tensor_tensor(coefT[0:64, :], coefT[0:64, :],
                                                mab_sb[0:64, 0:4], ALU.mult), inc=1)
        PRG.wait("te", ev18)
        PRG.te(lambda t: t.transpose(ps[3][0:4, 0:64], coefT[0:64, 0:4], identf[0:64, 0:64]))
        ev19 = PRG.mark("te")
        PRG.wait("ac", ev19)
        ev20 = PRG.ac(lambda a: a.activation(coefR[0:4, :], ps[3][0:4, 0:64], AF.Identity,
                                             bias=cst[0:4, C_ZERO:C_ZERO + 1], scale=1.0),
                      inc=1)
        PRG.wait("po", ev20)
        ev22 = PRG.dma_po("coefw", lambda g: g.dma_start(out=d_coef, in_=coefR[0:4, :]))
        PRG.wait("sy", ev22)
        ev23 = PRG.dma_sy("crowl", lambda sy: sy.dma_start(out=crow[0:1, :],
                                                  in_=d_coef.rearrange("a b -> (a b)")))
        PRG.wait("sy", ev_ccF)
        ev_R = None
        for j in range(4):
            PRG.wait("sy", ev_R)
            evL = PRG.dma_sy("fgl", lambda sy, j=j: sy.dma_start(
                out=Fg, in_=d_ccFo[P * j:P * (j + 1), :]))
            PRG.waits("te", [ev23, evL, ev_R])
            PRG.te(lambda t, j=j: t.matmul(ps[3][:, 0:64], onesrowf[:, :],
                                           crow[0:1, 64 * j:64 * (j + 1)],
                                           start=True, stop=True))
            ev24 = PRG.mark("te")
            PRG.waits("ve", [ev24, evL])
            if j == 0:
                ev_R = PRG.ve(lambda v: v.tensor_tensor(
                    R_sb.rearrange("p (a b) -> p a b", a=H),
                    Fg.rearrange("p (a b) -> p a b", a=H),
                    ps[3][:, 0:64][:, :, None].broadcast_to([P, H, HD]), ALU.mult), inc=1)
            else:
                PRG.ve(lambda v: v.tensor_tensor(
                    Fg.rearrange("p (a b) -> p a b", a=H),
                    Fg.rearrange("p (a b) -> p a b", a=H),
                    ps[3][:, 0:64][:, :, None].broadcast_to([P, H, HD]), ALU.mult))
                ev_R = PRG.ve(lambda v: v.tensor_tensor(R_sb, R_sb, Fg, ALU.add), inc=1)
        ev_Rb = PRG.ve(lambda v: v.tensor_copy(Rb[0], R_sb), inc=1)

        if "R" in dbg and dbgout("R", [P, DI]):
            PRG.wait("sy", ev_Rb)
            xdump = AR.alloc("xdump", [P, DI], F32)
            e = PRG.ve(lambda v: v.tensor_copy(xdump, R_sb), inc=1)
            PRG.wait("sy", e)
            e2_ = PRG.dma_sy("dbgR", lambda sy: sy.dma_start(out=dout["R"], in_=xdump))
            PRG.wait("ve", e2_)
            AR.free("xdump")

        PRG.barrier()
        PRG.dma_quiesce()
        AR.free("Fg")
        AR.free("g2"); AR.free("mT"); AR.free("Ssuf"); AR.free("coefT")
        AR.free("coefR"); AR.free("crow"); AR.free("mrA")

        if stop_after == "p7":
            _finish()
            return nc

        onw_sb = AR.alloc("onw_sb", [P, DI], BF16)
        ev_onw = PRG.dma_po("onw", lambda g: g.dma_start(out=onw_sb, in_=onwbc))
        w2p = [AR.alloc(f"w2p{i}", [P, DI], BF16) for i in range(2)]
        zsp0 = AR.alloc("zsp0", [P, DI], BF16)
        zsp = [zsp0, zsp0]
        xdp0 = AR.alloc("xdp0", [P, DI], BF16)
        xdp = [xdp0, xdp0]
        achi_c = [AR.alloc(f"achi{i}", [1, DI], BF16) for i in range(2)]
        aclo_c = [AR.alloc(f"aclo{i}", [1, DI], BF16) for i in range(2)]
        gmr = [AR.alloc(f"gmr{i}", [1, 2 * H], F32) for i in range(2)]
        dmb = [AR.alloc(f"dmb{i}", [P, 512], F32) for i in range(2)]
        emb = [AR.alloc(f"emb{i}", [P, 512], BF16) for i in range(2)]
        mtall = [AR.alloc(f"mtall{i}", [P, 8, 512], BF16) for i in range(2)]
        exh2all = AR.alloc("exh2all", [P, 8, 512], BF16)
        scm = AR.alloc("scm", [P, Q], BF16)
        gP_sb = AR.alloc("gP_sb", [P, 2, H], F32)
        t1b = AR.alloc("t1b", [P, 512], F32)
        ybf = AR.alloc("ybf", [P, 512], F32)
        e2b = AR.alloc("e2b", [P, 512], F32)
        yzp = AR.alloc("yzp", [P, DI], BF16)
        sqs = AR.alloc("sqs", [P, 512], F32)
        sscol = AR.alloc("sscol", [P, 8], F32)
        rstdc = AR.alloc("rstdc", [P, 1], F32)
        yznb = AR.alloc("yznb", [P, DI], BF16)
        yevb = AR.alloc("yevb", [P, 512], BF16)

        EV = {
            "ld_w2p": [None, None], "ld_zsp": [None, None],
            "ld_ah": [None, None], "ld_al": [None, None], "ld_gmr": [None, None],
            "use_w2p": [None, None], "use_zsp": [None, None],
            "use_ah": [None, None], "use_gmr": [None, None],
            "Rb0cast": ev_Rb,
            "yzn_rd": None, "tp": None, "yzdma": None,
            "bcA_rd": [None, None],
            "s0_rd": [None, None], "s1_rd": None,
            "epi_rd": [None, None],
            "mtall": [None, None], "exh2": None, "xdp_ev": [None, None],
            "scm_ev": None, "gP_ev": None, "pair_ve": None,
        }

        PRG.waits("sy", [st_pA["w2d"][0], st_pA["w2d"][1],
                         st_pA["zst"][0], st_pA["zst"][1], ev_ahi, ev_alo, ev_gamd])

        def p8_load(p8):
            if p8 >= NT:
                return
            par = p8 % 2
            PRG.wait("sy", EV["use_w2p"][par])
            EV["ld_w2p"][par] = PRG.dma_sy(f"w2p{par}", lambda sy, p8=p8, par=par:
                                           sy.dma_start(out=w2p[par], in_=d_w2[p8]))
            PRG.wait("sy", EV["use_ah"][par])
            for ci, c in enumerate((2 * p8, 2 * p8 + 1)):
                e_a = PRG.dma_sy(f"ah{ci}", lambda sy, c=c, ci=ci: sy.dma_start(
                    out=achi_c[ci].rearrange("a (h t) -> a h t", t=Q),
                    in_=d_acs_hi[:, Q * c:Q * (c + 1)].rearrange("h t -> () h t"),
                    allow_slow_non_contiguous=True))
                e_b = PRG.dma_sy(f"al{ci}", lambda sy, c=c, ci=ci: sy.dma_start(
                    out=aclo_c[ci].rearrange("a (h t) -> a h t", t=Q),
                    in_=d_acs_lo[:, Q * c:Q * (c + 1)].rearrange("h t -> () h t"),
                    allow_slow_non_contiguous=True))
                EV["ld_ah"][ci] = e_a
                EV["ld_al"][ci] = e_b
            PRG.wait("sy", EV["use_gmr"][par])
            EV["ld_gmr"][par] = PRG.dma_sy(f"gmr{par}", lambda sy, p8=p8, par=par:
                                           sy.dma_start(
                out=gmr[par].rearrange("a (b c) -> a b c", b=2),
                in_=d_gamT[2 * p8:2 * p8 + 2, :].rearrange("b c -> () b c")))

        p8_load(0)
        for p8 in range(NT):
            par = p8 % 2
            c0, c1 = 2 * p8, 2 * p8 + 1
            ahevs = [EV["ld_ah"][0], EV["ld_ah"][1], EV["ld_al"][0], EV["ld_al"][1]]
            PRG.wait("sy", EV["use_zsp"][par])
            EV["ld_zsp"][par] = PRG.dma_sy("zsp0", lambda sy, p8=p8, par=par:
                                           sy.dma_start(out=zsp[par], in_=d_zs[p8]))

            PRG.waits("ve", [EV["ld_w2p"][par], EV["s1_rd"], ev_eC2T])
            EV["xdp_ev"][par] = PRG.ve(lambda v, p8=p8, par=par: v.tensor_tensor(
                xdp[par].rearrange("p (a b) -> p a b", a=H),
                w2p[par].rearrange("p (a b) -> p a b", a=H),
                eC2T[:, p8, :][:, :, None].broadcast_to([P, H, HD]), ALU.mult), inc=1)
            PRG.waits("te", [ev_Bn, ev_Cn, EV["scm_ev"], EV["s1_rd"]])
            for ci, c in enumerate((c0, c1)):
                PRG.te(lambda t, c=c, ci=ci: t.matmul(
                    ps[6][64 * ci:64 * ci + 64, 0:64], Bn_b[:, Q * c:Q * (c + 1)],
                    Cn_b[:, Q * c:Q * (c + 1)], start=True, stop=True))
            ev_sc = PRG.mark("te")
            PRG.waits("ve", [ev_sc, ev_tri])
            ev_scm = PRG.ve(lambda v: v.tensor_tensor(scm, ps[6][:, 0:64], trim[:, :],
                                                      ALU.mult), inc=1)
            EV["scm_ev"] = ev_scm
            PRG.waits("te", [EV["ld_gmr"][par], EV["gP_ev"], ev_scm])
            for ci in range(2):
                PRG.te(lambda t, ci=ci, par=par: t.matmul(
                    ps[6][:, 64 + 64 * ci:128 + 64 * ci][:, 0:64] if False else
                    ps[6][:, 128 + 64 * ci:192 + 64 * ci],
                    onesrowf[:, :], gmr[par][0:1, 64 * ci:64 * (ci + 1)],
                    start=True, stop=True))
            ev_gp = PRG.mark("te")
            PRG.wait("ac", ev_gp)
            ev_gpe = PRG.ac(lambda a: a.activation(
                gP_sb.rearrange("p a b -> p (a b)"), ps[6][:, 128:256], AF.Identity,
                bias=cst[:, C_ZERO:C_ZERO + 1], scale=1.0), inc=1)
            EV["use_gmr"][par] = ev_gpe
            EV["gP_ev"] = ev_gpe

            pend_mtil = []

            def flush_mtil():
                while pend_mtil:
                    hg0, hp0, ev_e0 = pend_mtil.pop(0)
                    PRG.waits("ve", [ev_e0, ev_scm, EV["mtall"][par]])
                    PRG.ve(lambda v, hg=hg0, par=par, hp2=hp0: v.tensor_tensor(
                        mtall[par][:, hg, :].rearrange("p (a b) -> p a b", a=8),
                        emb[hp2].rearrange("p (a b) -> p a b", a=8),
                        scm[:, None, :].broadcast_to([P, 8, Q]), ALU.mult))

            for hg in range(8):
                hp2 = hg % 2
                bcb = ps[4 + hp2]
                PRG.waits("te", ahevs + [EV["bcA_rd"][hp2], EV["s0_rd"][hp2]])
                for ci in range(2):
                    PRG.te(lambda t, ci=ci, hg=hg, bcb=bcb: t.matmul(
                        bcb[64 * ci:64 * ci + 64, :], onesrow[0:1, 0:64],
                        achi_c[ci][0:1, 512 * hg:512 * (hg + 1)], start=True, stop=False))
                    PRG.te(lambda t, ci=ci, hg=hg, bcb=bcb: t.matmul(
                        bcb[64 * ci:64 * ci + 64, :], onesrow[0:1, 0:64],
                        aclo_c[ci][0:1, 512 * hg:512 * (hg + 1)], start=False, stop=True))
                ev_bca = PRG.mark("te")
                PRG.waits("ve", [ev_bca, ev_acs2T])
                PRG.ve(lambda v, p8=p8, hg=hg, hp2=hp2, bcb=bcb:
                       v.scalar_tensor_tensor(
                    out=dmb[hp2].rearrange("p (a b) -> p a b", a=8),
                    in0=acs2T[:, p8, 8 * hg:8 * hg + 8]
                        [:, :, None].broadcast_to([P, 8, Q]),
                    scalar=-1.0,
                    in1=bcb[:, :].rearrange("p (a b) -> p a b", a=8),
                    op0=ALU.mult, op1=ALU.add))
                ev_dm0 = PRG.ve(lambda v, hp2=hp2: v.tensor_scalar(
                    out=dmb[hp2], in0=dmb[hp2], scalar1=0.0, scalar2=None,
                    op0=ALU.min), inc=1)
                EV["bcA_rd"][hp2] = ev_dm0
                PRG.wait("ac", ev_dm0)
                ev_e = PRG.ac(lambda a, hp2=hp2: a.activation(
                    emb[hp2], dmb[hp2], AF.Exp,
                    bias=cst[:, C_ZERO:C_ZERO + 1], scale=1.0), inc=1)
                if len(pend_mtil) >= 1:
                    flush_mtil()
                pend_mtil.append((hg, hp2, ev_e))
            flush_mtil()
            ev_mtall = PRG.mark("ve")
            PRG.waits("po", [EV["ld_w2p"][par], ev_ewT, EV["exh2"]])
            for hg in range(8):
                PRG.po(lambda g, hg=hg, p8=p8, par=par: g.tensor_tensor(
                    exh2all[:, hg, :].rearrange("p (a b) -> p a b", a=8),
                    w2p[par][:, 512 * hg:512 * (hg + 1)].rearrange("p (a b) -> p a b", a=8),
                    ewT[:, p8, 8 * hg:8 * hg + 8][:, :, None].broadcast_to([P, 8, HD]),
                    ALU.mult))
            ev_exh2 = PRG.mark("po")
            EV["use_ah"] = [ev_mtall, ev_mtall]

            p8_load(p8 + 1)

            rb_ready_prev = EV["Rb0cast"]

            def ydyo0s0(hg, p8=p8, par=par, c0=c0):
                hp2 = hg % 2
                ydb = ps[0 + hp2]
                yob = ps[2 + hp2]
                PRG.waits("te", [ev_mtall, EV["ld_w2p"][par], EV["epi_rd"][hp2],
                                 rb_ready_prev, EV["xdp_ev"][par], ev_BnT,
                                 EV["s0_rd"][hp2], ev_gpe])
                for hp in range(8):
                    for ci in range(2):
                        PRG.te(lambda t, hp=hp, ci=ci, hg=hg, ydb=ydb, par=par: t.matmul(
                            ydb[64 * ci:64 * ci + 64, 64 * hp:64 * hp + 64],
                            mtall[par][64 * ci:64 * ci + 64, hg, 64 * hp:64 * hp + 64],
                            w2p[par][64 * ci:64 * ci + 64,
                                     512 * hg + 64 * hp:512 * hg + 64 * hp + 64],
                            start=True, stop=True, tile_position=(64 * ci, 64 * ci)))
                for hp in range(8):
                    PRG.te(lambda t, hp=hp, hg=hg, c=c0, yob=yob: t.matmul(
                        yob[0:64, 64 * hp:64 * hp + 64], Cn_b[:, Q * c:Q * (c + 1)],
                        Rb[0][:, 512 * hg + 64 * hp:512 * hg + 64 * hp + 64],
                        start=True, stop=True, tile_position=(0, 0)))
                PRG.te(lambda t, p8=p8, hg=hg, par=par, hp2=hp2: t.matmul(
                    ps[4 + hp2][:, :], BnT_b[0:64, p8, :],
                    xdp[par][0:64, 512 * hg:512 * (hg + 1)],
                    start=True, stop=True, tile_position=(0, 0)))
                return PRG.mark("te")

            def upd0(hg, ev_te):
                hp2 = hg % 2
                hcol = slice(512 * hg, 512 * (hg + 1))
                PRG.waits("ve", [ev_te, ev_gpe])
                PRG.ve(lambda v, hg=hg, hcol=hcol: v.tensor_tensor(
                    t1b.rearrange("p (a b) -> p a b", a=8),
                    R_sb[:, hcol].rearrange("p (a b) -> p a b", a=8),
                    gP_sb[:, 0, 8 * hg:8 * hg + 8][:, :, None].broadcast_to([P, 8, HD]),
                    ALU.mult))
                ev = PRG.ve(lambda v, hcol=hcol, hp2=hp2: v.tensor_tensor(
                    R_sb[:, hcol], t1b, ps[4 + hp2][:, :], ALU.add), inc=1)
                EV["s0_rd"][hp2] = ev
                PRG.wait("ac", ev)
                evc = PRG.ac(lambda a, hcol=hcol: a.activation(
                    Rb[1][:, hcol], R_sb[:, hcol], AF.Identity,
                    bias=cst[:, C_ZERO:C_ZERO + 1], scale=1.0), inc=1)
                return evc

            def yo1s1(hg, ev_cast, p8=p8, par=par, c1=c1):
                hp2 = hg % 2
                yob = ps[2 + hp2]
                PRG.waits("te", [ev_cast, EV["s1_rd"]])
                for hp in range(8):
                    PRG.te(lambda t, hp=hp, hg=hg, c=c1, yob=yob: t.matmul(
                        yob[64:128, 64 * hp:64 * hp + 64], Cn_b[:, Q * c:Q * (c + 1)],
                        Rb[1][:, 512 * hg + 64 * hp:512 * hg + 64 * hp + 64],
                        start=True, stop=True, tile_position=(0, 64)))
                PRG.te(lambda t, p8=p8, hg=hg, par=par: t.matmul(
                    ps[6][:, :], BnT_b[64:128, p8, :],
                    xdp[par][64:128, 512 * hg:512 * (hg + 1)],
                    start=True, stop=True, tile_position=(64, 0)))
                return PRG.mark("te")

            def upd1(hg, ev_te):
                hcol = slice(512 * hg, 512 * (hg + 1))
                PRG.waits("ve", [ev_te])
                PRG.ve(lambda v, hg=hg, hcol=hcol: v.tensor_tensor(
                    t1b.rearrange("p (a b) -> p a b", a=8),
                    R_sb[:, hcol].rearrange("p (a b) -> p a b", a=8),
                    gP_sb[:, 1, 8 * hg:8 * hg + 8][:, :, None].broadcast_to([P, 8, HD]),
                    ALU.mult))
                ev = PRG.ve(lambda v, hcol=hcol: v.tensor_tensor(
                    R_sb[:, hcol], t1b, ps[6][:, :], ALU.add), inc=1)
                EV["s1_rd"] = ev
                PRG.wait("ac", ev)
                evc = PRG.ac(lambda a, hcol=hcol: a.activation(
                    Rb[0][:, hcol], R_sb[:, hcol], AF.Identity,
                    bias=cst[:, C_ZERO:C_ZERO + 1], scale=1.0), inc=1)
                return evc

            def epi(hg, ev_yo1, p8=p8, par=par):
                hp2 = hg % 2
                ydb = ps[0 + hp2]
                yob = ps[2 + hp2]
                hcol = slice(512 * hg, 512 * (hg + 1))
                PRG.waits("ve", [ev_yo1, ev_eRelT])
                PRG.ve(lambda v, p8=p8, hg=hg, yob=yob: v.tensor_tensor(
                    t1b.rearrange("p (a b) -> p a b", a=8),
                    yob[:, :].rearrange("p (a b) -> p a b", a=8),
                    eRelT[:, p8, 8 * hg:8 * hg + 8][:, :, None].broadcast_to([P, 8, HD]),
                    ALU.mult))
                ev1 = PRG.ve(lambda v, ydb=ydb: v.scalar_tensor_tensor(
                    out=ybf, in0=ydb[:, :], scalar=1.0, in1=t1b,
                    op0=ALU.mult, op1=ALU.add), inc=1)
                EV["epi_rd"][hp2] = ev1
                PRG.waits("po", [ev1, ev_exh2, EV["ld_zsp"][par], EV["yzp_rd"]
                                 if "yzp_rd" in EV else None])
                PRG.po(lambda g, hg=hg: g.tensor_tensor(e2b, ybf, exh2all[:, hg, :],
                                                        ALU.add))
                evz = PRG.po(lambda g, hcol=hcol, par=par: g.tensor_tensor(
                    yzp[:, hcol], e2b, zsp[par][:, hcol], ALU.mult), inc=1)
                PRG.waits("ve", [evz])
                PRG.ve(lambda v, hcol=hcol, hg=hg: v.scalar_tensor_tensor(
                    out=sqs, in0=yzp[:, hcol], scalar=1.0, in1=yzp[:, hcol],
                    op0=ALU.mult, op1=ALU.mult, accum_out=sscol[:, hg:hg + 1]))
                return evz

            ev_te = [None] * 9
            ev_c0l = [None] * 8
            ev_te[0] = ydyo0s0(0)
            ev_c0l[0] = upd0(0, ev_te[0])
            ev_te[1] = ydyo0s0(1)
            ev_c0l[1] = upd0(1, ev_te[1])
            for hg in range(8):
                ev_y1 = yo1s1(hg, ev_c0l[hg])
                ev_c1 = upd1(hg, ev_y1)
                epi(hg, ev_y1)
                if hg < 6:
                    ev_te[hg + 2] = ydyo0s0(hg + 2)
                    ev_c0l[hg + 2] = upd0(hg + 2, ev_te[hg + 2])
            EV["Rb0cast"] = ev_c1
            EV["use_w2p"][par] = ev_exh2
            _pom = PRG.mark("po")
            EV["use_zsp"] = [_pom, _pom]
            EV["mtall"][par] = PRG.mark("te")
            EV["exh2"] = _pom

            ev_ss = PRG.ve(lambda v: v.tensor_reduce(rstdc, sscol, mybir.AxisListType.X,
                                                     ALU.add), inc=1)
            PRG.wait("ac", ev_ss)
            ev_sq3 = PRG.ac(lambda a: a.activation(rstdc, rstdc, AF.Sqrt,
                                                   bias=cst[:, C_EPSRMS:C_EPSRMS + 1],
                                                   scale=1.0 / DI), inc=1)
            PRG.waits("ve", [ev_sq3, ev_onw, EV["yzn_rd"]])
            PRG.ve(lambda v: v.reciprocal(rstdc, rstdc))
            ev_yzn = PRG.ve(lambda v: v.scalar_tensor_tensor(
                out=yznb, in0=yzp, scalar=rstdc[:, 0:1], in1=onw_sb,
                op0=ALU.mult, op1=ALU.mult), inc=1)
            EV["yzp_rd"] = ev_yzn
            for bq in range(8):
                PRG.waits("te", [ev_yzn, EV["tp"], ev_id])
                for k4 in range(4):
                    dc = 4 * bq + k4
                    PRG.te(lambda t, dc=dc, k4=k4: t.transpose(
                        ps_tp[:, 128 * k4:128 * (k4 + 1)],
                        yznb[:, 128 * dc:128 * (dc + 1)], identb[:, :]))
                ev_tp = PRG.mark("te")
                PRG.waits("ac", [ev_tp, EV["yzdma"]])
                ev_te2 = PRG.ac(lambda a: a.activation(
                    yevb, ps_tp[:, 0:512],
                    AF.Identity, bias=cst[:, C_ZERO:C_ZERO + 1], scale=1.0), inc=1)
                EV["tp"] = ev_te2
                PRG.wait("sy", ev_te2)
                EV["yzdma"] = PRG.dma_sy("yzw", lambda sy, bq=bq, p8=p8: sy.dma_start(
                    out=d_yznT[4 * bq:4 * bq + 4, p8].rearrange("a di t -> di a t"),
                    in_=yevb.rearrange("p (a t) -> p a t", a=4)))
            EV["yzn_rd"] = ev_tp

        ev_ld = {"yzdma": EV["yzdma"]}

        if "y" in dbg and dbgout("y", [NT * P, DI]):
            PRG.wait("sy", ev_ld["yzdma"])
            PRG.dma_sy("dbg", lambda sy: sy.dma_start(
                out=dout["y"].rearrange("(pr t) (dc di) -> dc pr di t", pr=NT, dc=KC * 2),
                in_=d_yznT[:, :, :, :]))

        PRG.barrier()
        PRG.dma_quiesce()
        for n in ("onw_sb", "w2p0", "w2p1", "zsp0", "xdp0", "achi0", "achi1",
                  "aclo0", "aclo1", "gmr0", "gmr1", "dmb0", "dmb1", "emb0", "emb1",
                  "mtall0", "mtall1", "exh2all", "scm", "gP_sb", "t1b", "ybf", "e2b",
                  "yzp", "sqs", "sscol", "rstdc", "yznb", "yevb", "R_sb", "Rb0", "Rb1",
                  "Bn_b", "Cn_b", "BnT_b",
                  "eC2T", "ewT", "eRelT", "acs2T"):
            AR.free(n)

        if stop_after == "p8":
            _finish()
            return nc

        yzn_all = AR.alloc("yzn_all", [P, KC * 2, TPC], BF16)
        wo_sl = [AR.alloc(f"wo_sl{i}", [P, KC * 2, P], BF16) for i in range(2)]
        s_ev = AR.alloc("s_ev", [P, TPC], BF16)
        ev_yl = None
        PRG.wait("sy", ev_ld["yzdma"])
        for dc in range(KC * 2):
            ev_yl = PRG.dma_sy("yl", lambda sy, dc=dc: sy.dma_start(
                out=yzn_all[:, dc, :].rearrange("p (a t) -> p a t", a=NT),
                in_=d_yznT[dc].rearrange("pr di t -> di pr t")))
        ev_wo = [None, None]
        ev_wou = [None, None]
        ev_sdma = None
        ev_sev = None
        for dm in range(KC):
            par = dm % 2
            PRG.wait("po", ev_wou[par])
            ev_wo[par] = PRG.dma_po(f"wo{par}", lambda g, dm=dm, par=par: g.dma_start(
                out=wo_sl[par],
                in_=woutT[:, P * dm:P * (dm + 1)].rearrange("(kc p) n -> p kc n", p=P)))
            PRG.waits("te", [ev_wo[par], ev_yl, ev_sev])
            for kc in range(KC * 2):
                for th in range(2):
                    PRG.te(lambda t, kc=kc, th=th, par=par: t.matmul(
                        ps[th][:, :], wo_sl[par][:, kc, :],
                        yzn_all[:, kc, 512 * th:512 * (th + 1)],
                        start=(kc == 0), stop=(kc == KC * 2 - 1)))
            ev_mm = PRG.mark("te")
            if dm >= KC - 2:
                ev_wou[par] = ev_mm
            else:
                ev_wou[par] = ev_mm
            PRG.waits("ac", [ev_mm, ev_sdma])
            ev_sev = PRG.ac(lambda a: a.activation(
                s_ev[:, 0:512], ps[0][:, :], AF.Identity,
                bias=cst[:, C_ZERO:C_ZERO + 1], scale=1.0))
            ev_sev = PRG.ac(lambda a: a.activation(
                s_ev[:, 512:1024], ps[1][:, :], AF.Identity,
                bias=cst[:, C_ZERO:C_ZERO + 1], scale=1.0), inc=1)
            PRG.wait("sy", ev_sev)
            ev_sdma = PRG.dma_sy("sd", lambda sy, dm=dm: sy.dma_start(out=d_s[dm], in_=s_ev))

        PRG.barrier()
        PRG.dma_quiesce()
        for n in ("yzn_all", "wo_sl0", "wo_sl1", "s_ev"):
            AR.free(n)

        if stop_after == "p9":
            _finish()
            return nc

        hidact = AR.alloc("hidact", [P, 64, TPC], BF16)
        xn2_sb = AR.alloc("xn2_sb", [P, KC, TPC], BF16)
        PRG.wait("sy", ev_xnst)
        ev_xnld = PRG.dma_sy("xnld", lambda sy: sy.dma_start(
            out=xn2_sb.rearrange("p a b -> p (a b)"), in_=d_xn))
        wg_sl = [AR.alloc(f"wg_sl{i}", [P, KC, P], BF16) for i in range(2)]
        wu_sl = [AR.alloc(f"wu_sl{i}", [P, KC, P], BF16) for i in range(2)]
        sgb = AR.alloc("sgb", [P, TPC], F32)
        ev_g = {"wg": [None, None], "wu": [None, None], "wgu": [None, None],
                "wuu": [None, None], "hv": None}
        for hc in range(64):
            par = hc % 2
            PRG.wait("po", ev_g["wgu"][par])
            ev_g["wg"][par] = PRG.dma_po(f"wg{par}", lambda g, hc=hc, par=par: g.dma_start(
                out=wg_sl[par],
                in_=wgT[:, P * hc:P * (hc + 1)].rearrange("(kc p) n -> p kc n", p=P)))
            PRG.wait("po", ev_g["wuu"][par])
            ev_g["wu"][par] = PRG.dma_po(f"wu{par}", lambda g, hc=hc, par=par: g.dma_start(
                out=wu_sl[par],
                in_=wuT[:, P * hc:P * (hc + 1)].rearrange("(kc p) n -> p kc n", p=P)))
            PRG.waits("te", [ev_g["wg"][par], ev_g["hv"], ev_g.get("sg"), ev_xnld])
            for kc in range(KC):
                for th in range(2):
                    PRG.te(lambda t, kc=kc, th=th, par=par: t.matmul(
                        ps[0 + th][:, :], wg_sl[par][:, kc, :],
                        xn2_sb[:, kc, 512 * th:512 * (th + 1)],
                        start=(kc == 0), stop=(kc == KC - 1)))
            ev_gmm = PRG.mark("te")
            ev_g["wgu"][par] = ev_gmm
            PRG.wait("te", ev_g["wu"][par])
            for kc in range(KC):
                for th in range(2):
                    PRG.te(lambda t, kc=kc, th=th, par=par: t.matmul(
                        ps[2 + th][:, :], wu_sl[par][:, kc, :],
                        xn2_sb[:, kc, 512 * th:512 * (th + 1)],
                        start=(kc == 0), stop=(kc == KC - 1)))
            ev_umm = PRG.mark("te")
            ev_g["wuu"][par] = ev_umm
            PRG.wait("ac", ev_gmm)
            ev_sg = PRG.ac(lambda a: a.activation(
                sgb[:, 0:512], ps[0][:, :], AF.Silu, bias=cst[:, C_ZERO:C_ZERO + 1],
                scale=1.0))
            ev_sg = PRG.ac(lambda a: a.activation(
                sgb[:, 512:1024], ps[1][:, :], AF.Silu, bias=cst[:, C_ZERO:C_ZERO + 1],
                scale=1.0), inc=1)
            ev_g["sg"] = ev_sg
            PRG.waits("ve", [ev_sg, ev_umm])
            for th in range(2):
                ev_hv = PRG.ve(lambda v, hc=hc, th=th: v.tensor_tensor(
                    hidact[:, hc, 512 * th:512 * (th + 1)], sgb[:, 512 * th:512 * (th + 1)],
                    ps[2 + th][:, :], ALU.mult), inc=1)
            ev_g["hv"] = ev_hv

        PRG.barrier()
        PRG.dma_quiesce()
        AR.free("wg_sl0"); AR.free("wg_sl1"); AR.free("wu_sl0"); AR.free("wu_sl1")
        AR.free("sgb")
        AR.free("xn2_sb")

        wd_sl = [AR.alloc(f"wd_sl{i}", [P, KC, 512], BF16) for i in range(2)]
        gchunk = AR.alloc("gchunk", [P, TPC], BF16)
        xchunk = AR.alloc("xchunk", [P, TPC], F32)
        schunk = AR.alloc("schunk", [P, TPC], BF16)
        obuf = AR.alloc("obuf", [P, TPC], F32)
        groups = [(0, 3), (3, 3), (6, 3), (9, 3), (12, 2), (14, 2)]
        ev_d = {"wd": [None, None], "wdu": [None, None], "ev_ev": None, "odma": None,
                "xl": None, "sl": ev_sdma}
        for gi, (dm0, ndm) in enumerate(groups):
            for kq in range(4):
                par = kq % 2
                PRG.wait("po", ev_d["wdu"][par])
                ev_d["wd"][par] = PRG.dma_po(f"wdn{par}", lambda g, dm0=dm0, ndm=ndm, kq=kq, par=par: g.dma_start(
                    out=wd_sl[par][:, :, 0:128 * ndm],
                    in_=wdT[2048 * kq:2048 * (kq + 1), P * dm0:P * dm0 + P * ndm]
                        .rearrange("(kc p) n -> p kc n", p=P)))
                PRG.waits("te", [ev_d["wd"][par], ev_d["ev_ev"] if kq == 0 else None,
                                 ev_g["hv"]])
                for kc in range(KC):
                    for dmc in range(ndm):
                        for th in range(2):
                            PRG.te(lambda t, kc=kc, kq=kq, dmc=dmc, th=th, par=par: t.matmul(
                                ps[2 * dmc + th][:, :], wd_sl[par][:, kc, 128 * dmc:128 * dmc + 128],
                                hidact[:, 16 * kq + kc, 512 * th:512 * (th + 1)],
                                start=(kq == 0 and kc == 0), stop=(kq == 3 and kc == KC - 1)))
                ev_dmm = PRG.mark("te")
                ev_d["wdu"][par] = ev_dmm
            for dmc in range(ndm):
                dm = dm0 + dmc
                PRG.waits("ac", [ev_dmm, ev_d["ev_ev"], ev_d.get("gcr")])
                ev_gc = PRG.ac(lambda a, dmc=dmc: a.activation(
                    gchunk[:, 0:512], ps[2 * dmc][:, :], AF.Silu,
                    bias=cst[:, C_ZERO:C_ZERO + 1], scale=1.0))
                ev_gc = PRG.ac(lambda a, dmc=dmc: a.activation(
                    gchunk[:, 512:1024], ps[2 * dmc + 1][:, :], AF.Silu,
                    bias=cst[:, C_ZERO:C_ZERO + 1], scale=1.0), inc=1)
                ev_d["ev_ev"] = ev_gc
                PRG.waits("sy", [ev_d["xl"]])
                ev_x2 = PRG.dma_sy("xc", lambda sy, dm=dm: sy.dma_start(
                    out=xchunk, in_=xT.rearrange("(kc p) t -> p kc t", p=P)[:, dm, :]))
                PRG.waits("sy", [ev_d["sl"]])
                ev_s2 = PRG.dma_sy("sc", lambda sy, dm=dm: sy.dma_start(
                    out=schunk, in_=d_s[dm]))
                PRG.waits("ve", [ev_gc, ev_x2, ev_s2, ev_d["odma"]])
                PRG.ve(lambda v: v.tensor_tensor(obuf, gchunk, schunk, ALU.mult))
                ev_ob = PRG.ve(lambda v: v.tensor_tensor(obuf, obuf, xchunk, ALU.add), inc=1)
                ev_d["xl"] = ev_ob
                ev_d["sl"] = ev_ob
                ev_d["gcr"] = ev_ob
                PRG.wait("sy", ev_ob)
                ev_d["odma"] = PRG.dma_sy("outd", lambda sy, dm=dm: sy.dma_start(
                    out=o_out.rearrange("(kc p) t -> p kc t", p=P)[:, dm, :], in_=obuf))

        for _sn, _sv in sorted(PRG.cnt.items()):
            if _sn.startswith("d_") and _sv > 0:
                PRG.q["sy"].append(["waitS", _sn, _sv, "wait"])
        PRG.finalize(block)
        return nc



def _host_prep(inputs):
    W_in = np.asarray(inputs["L0_attn_W_in"], np.float32)
    u = np.asarray(inputs["L0_attn_u"], np.float32)
    A_log = np.asarray(inputs["L0_attn_A_log"], np.float32)
    Dh = np.asarray(inputs["L0_attn_D"], np.float32)
    dtb = np.asarray(inputs["L0_attn_dt_bias"], np.float32)
    Bnw = np.asarray(inputs["L0_attn_B_norm_w"], np.float32)
    Cnw = np.asarray(inputs["L0_attn_C_norm_w"], np.float32)
    onw = np.asarray(inputs["L0_attn_out_norm_w"], np.float32)
    W_out = np.asarray(inputs["L0_attn_W_out"], np.float32)
    Wg = np.asarray(inputs["L0_ffn_w_gate"], np.float32)
    Wu = np.asarray(inputs["L0_ffn_w_up"], np.float32)
    Wd = np.asarray(inputs["L0_ffn_w_down"], np.float32)

    perm = np.concatenate([np.arange(0, NS, 2), np.arange(1, NS, 2)])
    wtA = np.ascontiguousarray(np.concatenate([W_in[DI:2 * DI], W_in[0:DI]], 0).T)
    rows = np.concatenate([
        W_in[2 * DI + perm], W_in[2 * DI + NS + perm],
        W_in[2 * DI + 2 * NS:2 * DI + 2 * NS + H],
        W_in[2 * DI + 2 * NS + H:2 * DI + 2 * NS + 2 * H],
        W_in[2 * DI + 2 * NS + 2 * H:],
    ], 0)
    wtB = np.ascontiguousarray(rows.T)
    woutT = np.ascontiguousarray(W_out.T)
    wgT = np.ascontiguousarray(Wg.T)
    wuT = np.ascontiguousarray(Wu.T)
    wdT = np.ascontiguousarray(Wd.T)

    vsm = np.zeros((P, 8), np.float32)
    vsm[0:H, 0] = A_log
    vsm[0:H, 1] = Dh
    vsm[0:H, 2] = dtb
    vsm[:, 3] = Bnw[perm]
    vsm[:, 4] = Cnw[perm]

    consts = np.zeros((P, NCONST), np.float32)
    consts[:, C_ONE] = 1.0
    consts[:, C_HPI] = HALF_PI
    consts[:, C_EPSLN] = EPS
    consts[:, C_EPSRMS] = EPS
    consts[0:H, C_SIGN] = -1.0
    consts[H:, C_SIGN] = 1.0

    onwbc = np.broadcast_to(onw[None, :], (P, DI)).copy()

    trimask = np.zeros((P, Q), np.float32)
    for si in range(Q):
        trimask[si, si + 1:] = 1.0
        trimask[64 + si, si + 1:] = 1.0

    resetm = np.ones((P, TPC), np.float32)
    resetm[:, 0::Q] = 0.0
    ones_col = np.ones((P, 1), np.float32)
    onesb_col = np.ones((P, 1), ml_dtypes.bfloat16)
    onesb_row = np.ones((1, P), ml_dtypes.bfloat16)
    identb = np.eye(P, dtype=ml_dtypes.bfloat16)
    identf = np.eye(P, dtype=np.float32)
    onesf_row = np.ones((1, P), np.float32)
    permm = np.zeros((P, P), np.float32)
    for i in range(64):
        permm[64 + i, i] = 1.0
        permm[i, 64 + i] = 1.0

    shared = dict(wtA=wtA, wtB=wtB, woutT=woutT, wgT=wgT, wuT=wuT, wdT=wdT,
                  vsm=vsm, consts=consts, onwbc=onwbc, trimask=trimask, resetm=resetm,
                  ones_in=ones_col, onesb_in=onesb_col, onesrow_in=onesb_row,
                  identb_in=identb, identf_in=identf, onesrowf_in=onesf_row,
                  permb_in=permm)

    in_maps = []
    for q in range(8):
        b, m = divmod(q, 4)
        im = dict(shared)
        im["xT"] = np.ascontiguousarray(u[b, TPC * m:TPC * (m + 1), :].T)
        mthq = np.zeros((P, 8), np.float32)
        if m < 3:
            mthq[0:H, m + 1] = 1.0
        mthq[H:, 0:m] = 1.0
        mthq[0:H, 4:4 + m] = 1.0
        im["mth"] = mthq
        mabq = np.zeros((P, 8), np.float32)
        for j in range(4):
            if j < m:
                mabq[:, j] = 1.0
        for kk in range(4):
            if kk <= m - 1:
                mabq[kk, 4] = 1.0
        im["mab"] = mabq
        in_maps.append(im)
    return in_maps


def kernel(**inputs):
    in_maps = _host_prep(inputs)
    nc = build_program()
    res = run_bass_kernel_spmd(nc, in_maps, core_ids=list(range(8)))
    out = np.zeros((2, 4096, D), np.float32)
    for q in range(8):
        b, m = divmod(q, 4)
        out[b, TPC * m:TPC * (m + 1), :] = np.asarray(res.results[q]["o_out"]).T
    return out
